# version 16
# baseline (speedup 1.0000x reference)
from contextlib import ExitStack

import numpy as np
import concourse.bass as bass
import concourse.mybir as mybir
from concourse.bass_utils import run_bass_kernel_spmd

F32 = mybir.dt.float32
BF16 = mybir.dt.bfloat16
AF = mybir.ActivationFunctionType
ALU = mybir.AluOpType
AX = mybir.AxisListType.X

NCORES = 8
S = 8192
D = 2048
NH = 16
DH = 128
HID = 8192
T = S // NCORES
HPC = NH // NCORES
S4 = float(np.float32(1.0 / np.sqrt(np.sqrt(float(DH)))))
LN_EPS = 1e-5
RMS_EPS = float(np.finfo(np.float32).eps)

ENGS = ("sync", "tensor", "vector", "scalar", "gpsimd")


class Sem:
    __slots__ = ("h", "n", "name")

    def __init__(self, h, name):
        self.h = h
        self.n = 0
        self.name = name


class Sched:
    def __init__(self, nc):
        self.nc = nc
        self.streams = {e: [] for e in ENGS}

    def sem(self, name):
        return Sem(self.nc.alloc_semaphore(name=name), name)

    def op(self, eng, build, waits=(), inc=None, inc_n=1):
        wl = []
        for w in waits:
            if isinstance(w, tuple):
                s, v = w
            else:
                s, v = w, w.n
            if v > 0:
                wl.append((s.h, v))
        il = None
        if inc is not None:
            inc.n += inc_n
            il = (inc.h, inc_n)

        def thunk(e, build=build, wl=wl, il=il):
            for h, v in wl:
                e.wait_ge(h, v)
            r = build(e)
            if il is not None:
                r.then_inc(il[0], il[1])

        self.streams[eng].append(thunk)

    def wait(self, eng, waits):
        wl = []
        for w in waits:
            if isinstance(w, tuple):
                s, v = w
            else:
                s, v = w, w.n
            if v > 0:
                wl.append((s.h, v))

        def thunk(e, wl=wl):
            for h, v in wl:
                e.wait_ge(h, v)

        self.streams[eng].append(thunk)


DBG = False


def build():
    nc = bass.Bass("TRN2", target_bir_lowering=False)

    x_in = nc.dram_tensor("x", [T, D], F32, kind="ExternalInput").ap()
    wqT = nc.dram_tensor("wqT", [D, HPC * DH], F32, kind="ExternalInput").ap()
    wkT = nc.dram_tensor("wkT", [D, HPC * DH], F32, kind="ExternalInput").ap()
    wvT = nc.dram_tensor("wvT", [D, HPC * DH], F32, kind="ExternalInput").ap()
    woT = nc.dram_tensor("woT", [HPC * DH, D], F32, kind="ExternalInput").ap()
    b6 = nc.dram_tensor("b6", [6, DH], F32, kind="ExternalInput").ap()
    bo_in = nc.dram_tensor("bo", [1, D], F32, kind="ExternalInput").ap()
    wgu_in = nc.dram_tensor("wgu", [2 * HID // NCORES, D], F32, kind="ExternalInput").ap()
    wpr_in = nc.dram_tensor("wpr", [D // NCORES, HID], F32, kind="ExternalInput").ap()
    ident_in = nc.dram_tensor("identin", [128, 128], F32, kind="ExternalInput").ap()
    triu_in = nc.dram_tensor("triuin", [128, 128], F32, kind="ExternalInput").ap()
    out_t = nc.dram_tensor("out", [T, D], F32, kind="ExternalOutput").ap()
    if DBG:
        d_xnag = nc.dram_tensor("d_xnag", [NCORES * D, T], BF16, kind="ExternalOutput").ap()
        d_qt = [nc.dram_tensor(f"d_qt{h}", [128, S], BF16, kind="ExternalOutput").ap() for h in range(2)]
        d_kt = [nc.dram_tensor(f"d_kt{h}", [128, S], BF16, kind="ExternalOutput").ap() for h in range(2)]
        d_vt = [nc.dram_tensor(f"d_vt{h}", [128, S], BF16, kind="ExternalOutput").ap() for h in range(2)]
        d_num = [nc.dram_tensor(f"d_num{h}", [128, S], BF16, kind="ExternalOutput").ap() for h in range(2)]
        d_negb = nc.dram_tensor("d_negb", [128, 2], F32, kind="ExternalOutput").ap()
        d_catt = nc.dram_tensor("d_catt", [S, D], BF16, kind="ExternalOutput").ap()
        d_rs = nc.dram_tensor("d_rs", [T, D], BF16, kind="ExternalOutput").ap()
        d_wgu = nc.dram_tensor("d_wgu", [NCORES * D, 2 * HID // NCORES], BF16, kind="ExternalOutput").ap()
        d_wpr = nc.dram_tensor("d_wpr", [NCORES * HID, D // NCORES], BF16, kind="ExternalOutput").ap()
        d_ht = [nc.dram_tensor(f"d_ht{i}", [128, 64 * 512], BF16, kind="ExternalOutput").ap() for i in range(2)]
        d_rstd = nc.dram_tensor("d_rstd", [128, 8], F32, kind="ExternalOutput").ap()
        m_tap = None

    c_xn = nc.dram_tensor("c_xn", [D, T], BF16, kind="Internal")
    xn_ag = nc.dram_tensor("xn_ag", [NCORES * D, T], BF16, kind="Internal", addr_space="Shared")
    c_wgu = nc.dram_tensor("c_wgu", [D, 2 * HID // NCORES], BF16, kind="Internal")
    wgu_ag = nc.dram_tensor("wgu_ag", [NCORES * D, 2 * HID // NCORES], BF16, kind="Internal", addr_space="Shared")
    c_wpr = nc.dram_tensor("c_wpr", [HID, D // NCORES], BF16, kind="Internal")
    wpr_ag = nc.dram_tensor("wpr_ag", [NCORES * HID, D // NCORES], BF16, kind="Internal", addr_space="Shared")
    c_att = nc.dram_tensor("c_att", [S, D], BF16, kind="Internal")
    att_rs = nc.dram_tensor("att_rs", [T, D], BF16, kind="Internal")
    c_bar = nc.dram_tensor("c_bar", [1, 64], F32, kind="Internal")
    c_bar2 = nc.dram_tensor("c_bar2", [1, 64], F32, kind="Internal")

    RG = [list(range(NCORES))]
    sch = Sched(nc)

    m_x = [sch.sem(f"m_x{i}") for i in range(2)]
    m_b = sch.sem("m_b")
    m_id = sch.sem("m_id")
    m_w = sch.sem("m_w")
    m_wo = sch.sem("m_wo")
    m_q = [sch.sem(f"m_q{i}") for i in range(2)]
    m_xga = [sch.sem(f"m_xga{i}") for i in range(12)]
    m_wgu = [sch.sem(f"m_wgu{i}") for i in range(16)]
    m_wp = [sch.sem(f"m_wp{i}") for i in range(2)]
    m_fx = [sch.sem(f"m_fx{i}") for i in range(4)]
    m_cxn = sch.sem("m_cxn")
    m_cb = sch.sem("m_cb")
    m_cw = [sch.sem(f"m_cw{i}") for i in range(2)]
    m_ca = [sch.sem(f"m_ca{i}") for i in range(2)]
    m_out4 = [sch.sem(f"m_out{i}") for i in range(4)]
    c_sem = sch.sem("c_sem")
    m_tap = sch.sem("m_tap")

    v_a = sch.sem("v_a")
    s_a = sch.sem("s_a")
    t_a = sch.sem("t_a")
    v_ae = sch.sem("v_ae")

    v_b = sch.sem("v_b")
    g_b = sch.sem("g_b")
    t_b = sch.sem("t_b")
    v_be = sch.sem("v_be")

    t_c = sch.sem("t_c")
    s_c = sch.sem("s_c")
    t_cn = sch.sem("t_cn")
    v_c = sch.sem("v_c")

    s_d = sch.sem("s_d")
    t_ds = sch.sem("t_ds")
    t_da = sch.sem("t_da")
    v_dm = sch.sem("v_dm")
    s_dn = sch.sem("s_dn")
    v_dl = sch.sem("v_dl")
    t_dl = sch.sem("t_dl")
    v_dn = sch.sem("v_dn")
    t_dr = sch.sem("t_dr")

    t_e = sch.sem("t_e")
    v_e = sch.sem("v_e")
    s_e = sch.sem("s_e")

    t_f = sch.sem("t_f")
    s_f = sch.sem("s_f")
    v_f = sch.sem("v_f")
    t_ft = sch.sem("t_ft")
    t_p = sch.sem("t_p")
    v_p = sch.sem("v_p")

    big = ExitStack()
    ident = big.enter_context(nc.sbuf_tensor("ident", [128, 128], BF16))
    triu = big.enter_context(nc.sbuf_tensor("triu", [128, 128], BF16))
    onesk = big.enter_context(nc.sbuf_tensor("onesk", [128, 1], BF16))
    onesr_f = big.enter_context(nc.sbuf_tensor("onesr_f", [1, 128], F32))
    onesr_b = big.enter_context(nc.sbuf_tensor("onesr_b", [1, 128], BF16))
    bvec = big.enter_context(nc.sbuf_tensor("bvec", [128, 8], F32))
    bo8bf = big.enter_context(nc.sbuf_tensor("bo8bf", [1, D], BF16))
    negBv = big.enter_context(nc.sbuf_tensor("negBv", [128, 2], F32))
    nrm = big.enter_context(nc.sbuf_tensor("nrm", [1, 8], F32))
    nrm2 = big.enter_context(nc.sbuf_tensor("nrm2", [1, 8], F32))
    rrow = big.enter_context(nc.sbuf_tensor("rrow", [1, 512], F32))
    rstd8 = big.enter_context(nc.sbuf_tensor("rstd8", [128, 8], F32))
    rmsp = big.enter_context(nc.sbuf_tensor("rmsp", [128, 64], F32))
    lnst = big.enter_context(nc.sbuf_tensor("lnst", [128, 8], F32))
    xnT = big.enter_context(nc.sbuf_tensor("xnT", [128, 16 * 1024], BF16))

    stCDE = ExitStack()
    wq_sb = stCDE.enter_context(nc.sbuf_tensor("wq_sb", [128, 2048], BF16))
    wk_sb = stCDE.enter_context(nc.sbuf_tensor("wk_sb", [128, 2048], BF16))
    wv_sb = stCDE.enter_context(nc.sbuf_tensor("wv_sb", [128, 2048], BF16))
    QTn = [stCDE.enter_context(nc.sbuf_tensor(f"QTn{h}", [128, S], BF16)) for h in range(HPC)]
    KT = stCDE.enter_context(nc.sbuf_tensor("KT", [128, S], BF16))
    Vtk = stCDE.enter_context(nc.sbuf_tensor("Vtk", [128, S], BF16))
    xgab = [stCDE.enter_context(nc.sbuf_tensor(f"xgab{i}", [128, 8192], BF16)) for i in range(3)]

    sch.op("gpsimd", lambda e: e.dma_start(out=ident[:], in_=ident_in), inc=m_id, inc_n=16)
    sch.op("gpsimd", lambda e: e.dma_start(out=triu[:], in_=triu_in), inc=m_id, inc_n=16)
    IDENT_W = 32
    sch.op("gpsimd", lambda e: e.memset(onesk[:], 1.0), inc=g_b)
    sch.op("gpsimd", lambda e: e.memset(onesr_f[:], 1.0), inc=g_b)
    sch.op("gpsimd", lambda e: e.memset(onesr_b[:], 1.0), inc=g_b)
    sch.op("gpsimd", lambda e: e.memset(nrm2[:], 0.0), inc=g_b)
    sch.op("gpsimd", lambda e: e.memset(rrow[:, 0:64], 0.0), inc=g_b)
    ONES_G = g_b.n
    sch.op("gpsimd", lambda e: e.dma_start(out=c_bar.ap(), in_=rrow[:, 0:64]), waits=[(g_b, ONES_G)], inc=m_cb, inc_n=16)


    stA = ExitStack()
    xld = [stA.enter_context(nc.sbuf_tensor(f"xld{i}", [128, D], F32)) for i in range(2)]
    xnbuf = stA.enter_context(nc.sbuf_tensor("xnbuf", [128, D], BF16))
    psA = ExitStack()
    ps_tra = psA.enter_context(nc.psum_tensor("ps_tra", [128, 1024], BF16))
    ps_bias = psA.enter_context(nc.psum_tensor("ps_bias", [128, 8], F32))

    b6row = xld[0]
    borow = xld[1]
    sch.op("sync", lambda e: e.dma_start(out=b6row[0:1, 0:768].bitcast(F32) if False else b6row[0:1, 0:768], in_=b6.rearrange("a b -> (a b)")[None, :]), inc=m_b, inc_n=16)
    sch.op("sync", lambda e: e.dma_start(out=borow[0:1, :], in_=bo_in), inc=m_b, inc_n=16)
    sch.op("vector", lambda e: e.tensor_scalar_mul(out=bo8bf[:], in0=borow[0:1, :], scalar1=0.125),
           waits=[(m_b, 32)], inc=v_a)
    for i in range(6):
        sch.op("tensor",
               lambda e, i=i: e.matmul(ps_bias[:, i:i + 1], b6row[0:1, i * 128:(i + 1) * 128],
                                       onesr_f[:, 0:1], start=True, stop=True),
               waits=[(m_b, 32), (g_b, ONES_G)], inc=t_a)
    sch.op("vector", lambda e: e.tensor_scalar(out=bvec[:, 0:4], in0=ps_bias[:, 0:4],
                                               scalar1=S4, scalar2=None, op0=ALU.mult),
           waits=[(t_a, 6)], inc=v_a)
    sch.op("vector", lambda e: e.tensor_copy(out=bvec[:, 4:6], in_=ps_bias[:, 4:6]),
           waits=[(t_a, 6)], inc=v_a)
    BVEC_V = v_a.n

    NT = T // 128
    xn_done = [0, 0]
    for ti in range(NT):
        buf = xld[ti % 2]
        mxs = m_x[ti % 2]
        bgate = []
        if ti == 0:
            bgate = [(t_a, 6)]
        elif ti == 1:
            bgate = [(v_a, 1)]
        sch.op("sync", lambda e, ti=ti, buf=buf: e.dma_start(out=buf[:], in_=x_in[ti * 128:(ti + 1) * 128, :]),
               waits=[(v_a, xn_done[ti % 2])] + bgate, inc=mxs, inc_n=16)
        xw = (mxs, mxs.n)
        sch.op("vector", lambda e, buf=buf: e.reduce_sum(out=lnst[:, 0:1], in_=buf[:], axis=AX),
               waits=[xw, (s_a, s_a.n)], inc=v_a)
        sch.op("scalar", lambda e, buf=buf: e.activation(out=xnbuf[:], in_=buf[:], func=AF.Square,
                                                         accum_out=lnst[:, 1:2]),
               waits=[xw, (v_a, v_a.n), (t_a, t_a.n)], inc=s_a)
        sch.op("vector", lambda e: e.tensor_scalar_mul(out=lnst[:, 2:3], in0=lnst[:, 0:1], scalar1=1.0 / D),
               waits=[(v_a, v_a.n)], inc=v_a)
        sch.op("vector", lambda e: e.tensor_scalar(out=lnst[:, 3:4], in0=lnst[:, 2:3],
                                                   scalar1=lnst[:, 2:3], scalar2=-LN_EPS,
                                                   op0=ALU.mult, op1=ALU.add),
               waits=[(v_a, v_a.n)], inc=v_a)
        sch.op("vector", lambda e: e.scalar_tensor_tensor(out=lnst[:, 4:5], in0=lnst[:, 1:2],
                                                          scalar=1.0 / D, in1=lnst[:, 3:4],
                                                          op0=ALU.mult, op1=ALU.subtract),
               waits=[(v_a, v_a.n), (s_a, s_a.n)], inc=v_a)
        sch.op("scalar", lambda e: e.activation(out=lnst[:, 5:6], in_=lnst[:, 4:5], func=AF.Sqrt),
               waits=[(v_a, v_a.n)], inc=s_a)
        sch.op("vector", lambda e: e.reciprocal(out=lnst[:, 6:7], in_=lnst[:, 5:6]),
               waits=[(s_a, s_a.n)], inc=v_a)
        sch.op("vector", lambda e: e.scalar_tensor_tensor(out=lnst[:, 7:8], in0=lnst[:, 2:3],
                                                          scalar=-1.0, in1=lnst[:, 6:7],
                                                          op0=ALU.mult, op1=ALU.mult),
               waits=[(v_a, v_a.n)], inc=v_a)
        sch.op("vector", lambda e, buf=buf: e.tensor_scalar(out=xnbuf[:], in0=buf[:],
                                                            scalar1=lnst[:, 6:7], scalar2=lnst[:, 7:8],
                                                            op0=ALU.mult, op1=ALU.add),
               waits=[(v_a, v_a.n), (t_a, t_a.n), (s_a, s_a.n)], inc=v_a)
        xn_done[ti % 2] = v_a.n
        xnw = (v_a, v_a.n)
        for bq in range(2):
            for s8 in range(8):
                dc = bq * 8 + s8
                sch.op("tensor", lambda e, dc=dc, s8=s8: e.transpose(
                    ps_tra[:, s8 * 128:(s8 + 1) * 128], xnbuf[:, dc * 128:(dc + 1) * 128], ident[:]),
                    waits=[xnw, (m_id, IDENT_W), (v_ae, v_ae.n)], inc=t_a)
            dst = bass.AP(xnT, ti * 128 + bq * 8 * 1024, [[16 * 1024, 128], [1024, 8], [1, 128]])
            srcv = ps_tra.ap().rearrange("p (a n) -> p a n", n=128)
            sch.op("vector", lambda e, dst=dst, srcv=srcv: e.tensor_copy(out=dst, in_=srcv),
                   waits=[(t_a, t_a.n)], inc=v_ae)
    for dc in range(16):
        sch.op("sync", lambda e, dc=dc: e.dma_start(out=c_xn.ap()[dc * 128:(dc + 1) * 128, :],
                                                    in_=xnT[:, dc * 1024:(dc + 1) * 1024]),
               waits=[(v_ae, v_ae.n)], inc=m_cxn, inc_n=16)
    sch.op("gpsimd", lambda e: e.collective_compute("AllGather", ALU.bypass, replica_groups=RG,
                                                    ins=[c_xn.ap().opt()], outs=[xn_ag.ap().opt()]),
           waits=[(m_cxn, m_cxn.n)], inc=c_sem)
    sch.op("gpsimd", lambda e: e.collective_compute("AllReduce", ALU.add, replica_groups=RG,
                                                    ins=[c_bar.ap().opt()], outs=[c_bar2.ap().opt()]),
           waits=[(m_cb, 16)], inc=c_sem)
    C_XN = c_sem.n

    stB = ExitStack()
    wbf = [stB.enter_context(nc.sbuf_tensor(f"wbf{i}", [128, 512], BF16)) for i in range(2)]
    tqb = stB.enter_context(nc.sbuf_tensor("tqb", [128, 512], BF16))
    cbq = stB.enter_context(nc.sbuf_tensor("cbq", [128, 512], BF16))
    dbq = stB.enter_context(nc.sbuf_tensor("dbq", [128, 512], BF16))
    qbq = stB.enter_context(nc.sbuf_tensor("qbq", [128, 512], BF16))
    whb = stB.enter_context(nc.sbuf_tensor("whb", [128, 512], BF16))
    web = stB.enter_context(nc.sbuf_tensor("web", [128, 512], BF16))
    wfb = [stB.enter_context(nc.sbuf_tensor(f"wfb{i}", [128, 512], BF16)) for i in range(6)]
    gsum = stB.enter_context(nc.sbuf_tensor("gsum", [128, 8], F32))
    gscl = stB.enter_context(nc.sbuf_tensor("gscl", [128, 8], BF16))
    ginv = stB.enter_context(nc.sbuf_tensor("ginv", [128, 8], F32))
    wTs = [stB.enter_context(nc.sbuf_tensor(f"wTs{i}", [128, 512], BF16)) for i in range(2)]
    psB = ExitStack()
    ps_trb = psB.enter_context(nc.psum_tensor("ps_trb", [128, 1024], BF16))

    qunits = []
    for ti in range(16):
        for sc in range(4):
            qunits.append(("gu", ti, sc, wgu_in[ti * 128:(ti + 1) * 128, sc * 512:(sc + 1) * 512]))
    for ri in range(2):
        for sc in range(16):
            qunits.append(("pr", ri, sc, wpr_in[ri * 128:(ri + 1) * 128, sc * 512:(sc + 1) * 512]))

    WEFF = {}
    TRB = {}
    STG = {}
    qstate = {"ui": 0, "trb": ps_trb[:]}

    def emit_qunit():
        ui = qstate["ui"]
        if ui >= len(qunits):
            return False
        qstate["ui"] = ui + 1
        kind, i0, i1, srcap = qunits[ui]
        trb = qstate["trb"]
        wb = wbf[ui % 2]
        wf = wfb[ui % 6]
        war = [(v_b, WEFF.get(ui - 2, 0))]
        mqs = m_q[ui % 2]
        sch.op("gpsimd", lambda e, wb=wb, srcap=srcap: e.dma_start(out=wb[:], in_=srcap),
               waits=war, inc=mqs, inc_n=16)
        lw = (mqs, mqs.n)
        v3 = wb.ap().rearrange("p (g k) -> p g k", k=64)
        sch.op("vector", lambda e, v3=v3: e.reduce_sum(out=gsum[:], in_=v3, axis=AX, apply_absolute_value=True),
               waits=[lw, (v_b, v_b.n)], inc=v_b)
        sch.op("vector", lambda e: e.tensor_scalar(out=gscl[:], in0=gsum[:], scalar1=1.0 / 64,
                                                   scalar2=1e-8, op0=ALU.mult, op1=ALU.max),
               waits=[(v_b, v_b.n)], inc=v_b)
        sch.op("vector", lambda e: e.reciprocal(out=ginv[:], in_=gscl[:]), waits=[(v_b, v_b.n)], inc=v_b)
        ginv_b = bass.AP(ginv, 0, [[8, 128], [1, 8], [0, 64]])
        gscl_b = bass.AP(gscl, 0, [[8, 128], [1, 8], [0, 64]])
        tq3 = tqb.ap().rearrange("p (g k) -> p g k", k=64)
        sch.op("vector", lambda e, v3=v3, tq3=tq3, ginv_b=ginv_b: e.tensor_tensor(
            out=tq3, in0=v3, in1=ginv_b, op=ALU.mult),
            waits=[(v_b, v_b.n)], inc=v_b)
        sch.op("vector", lambda e: e.tensor_single_scalar(out=cbq[:], in_=tqb[:], scalar=0.5, op=ALU.is_gt),
               waits=[(v_b, v_b.n)], inc=v_b)
        sch.op("vector", lambda e: e.tensor_single_scalar(out=dbq[:], in_=tqb[:], scalar=-0.5, op=ALU.is_lt),
               waits=[(v_b, v_b.n)], inc=v_b)
        sch.op("vector", lambda e: e.tensor_sub(out=qbq[:], in0=cbq[:], in1=dbq[:]),
               waits=[(v_b, v_b.n)], inc=v_b)
        qb3 = qbq.ap().rearrange("p (g k) -> p g k", k=64)
        wh3 = whb.ap().rearrange("p (g k) -> p g k", k=64)
        sch.op("vector", lambda e, qb3=qb3, wh3=wh3, gscl_b=gscl_b: e.tensor_tensor(
            out=wh3, in0=qb3, in1=gscl_b, op=ALU.mult),
            waits=[(v_b, v_b.n)], inc=v_b)
        sch.op("vector", lambda e, wb=wb: e.tensor_sub(out=web[:], in0=whb[:], in1=wb[:]),
               waits=[(v_b, v_b.n)], inc=v_b)
        sch.op("vector", lambda e, wb=wb, wf=wf: e.tensor_add(out=wf[:], in0=wb[:], in1=web[:]),
               waits=[(v_b, v_b.n), (t_b, TRB.get(ui - 6, 0))], inc=v_b)
        WEFF[ui] = v_b.n
        ww = (v_b, v_b.n)

        if True:
            trb = qstate["trb"]
            st = wTs[ui % 2]
            for dc in range(4):
                sch.op("tensor", lambda e, wf=wf, dc=dc, trb=trb: e.transpose(
                    trb[:, dc * 128:(dc + 1) * 128], wf[:, dc * 128:(dc + 1) * 128], ident[:]),
                    waits=[ww, (m_id, IDENT_W), (v_be, v_be.n), (v_c, v_c.n)], inc=t_b)
            sch.op("vector", lambda e, st=st, trb=trb: e.tensor_copy(out=st[:], in_=trb[:, 0:512]),
                   waits=[(t_b, t_b.n), (m_cw[ui % 2], STG.get(ui - 2, 0))], inc=v_be)
            TRB[ui] = t_b.n
            ew = (v_be, v_be.n)
            for dc in range(4):
                if kind == "gu":
                    dst = c_wgu.ap()[(i1 * 4 + dc) * 128:(i1 * 4 + dc + 1) * 128, i0 * 128:(i0 + 1) * 128]
                else:
                    dst = c_wpr.ap()[(i1 * 4 + dc) * 128:(i1 * 4 + dc + 1) * 128, i0 * 128:(i0 + 1) * 128]
                sch.op("gpsimd", lambda e, st=st, dc=dc, dst=dst: e.dma_start(
                    out=dst, in_=st[:, dc * 128:(dc + 1) * 128]),
                    waits=[ew], inc=m_cw[ui % 2], inc_n=16)
            STG[ui] = m_cw[ui % 2].n
        return True

    def pump(n):
        for _ in range(n):
            if not emit_qunit():
                return

    for _ in range(4):
        pump(6)
    QUANT_COLLECTIVES = []

    def emit_quant_collectives():
        assert qstate["ui"] == len(qunits), qstate["ui"]
        sch.op("gpsimd", lambda e: e.collective_compute("AllGather", ALU.bypass, replica_groups=RG,
                                                        ins=[c_wgu.ap().opt()], outs=[wgu_ag.ap().opt()]),
               waits=[(m_cw[0], m_cw[0].n), (m_cw[1], m_cw[1].n)], inc=c_sem)
        QUANT_COLLECTIVES.append(c_sem.n)
        sch.op("gpsimd", lambda e: e.collective_compute("AllGather", ALU.bypass, replica_groups=RG,
                                                        ins=[c_wpr.ap().opt()], outs=[wpr_ag.ap().opt()]),
               waits=[(m_cw[0], m_cw[0].n), (m_cw[1], m_cw[1].n)], inc=c_sem)
        sch.op("gpsimd", lambda e: e.collective_compute("AllReduce", ALU.add, replica_groups=RG,
                                                        ins=[c_bar.ap().opt()], outs=[c_bar2.ap().opt()]),
               inc=c_sem)
        QUANT_COLLECTIVES.append(c_sem.n)
        QUANT_COLLECTIVES[0] = c_sem.n
        if DBG:
            sch.op("sync", lambda e: e.dma_start(out=d_wgu, in_=wgu_ag.ap()),
                   waits=[(c_sem, QUANT_COLLECTIVES[0])], inc=m_tap, inc_n=16)
            sch.op("sync", lambda e: e.dma_start(out=d_wpr, in_=wpr_ag.ap()),
                   waits=[(c_sem, QUANT_COLLECTIVES[1])], inc=m_tap, inc_n=16)
    if DBG:
        sch.op("sync", lambda e: e.dma_start(out=d_xnag, in_=xn_ag.ap()), waits=[(c_sem, C_XN)], inc=m_tap, inc_n=16)
    psB.close()
    psA.close()
    stB.close()
    stA.close()
    stCsc = ExitStack()
    sqs = [stCsc.enter_context(nc.sbuf_tensor(f"sqs{i}", [128, 512], BF16)) for i in range(2)]
    vfs = [stCsc.enter_context(nc.sbuf_tensor(f"vfs{i}", [128, 512], BF16)) for i in range(2)]
    numbf = stCsc.enter_context(nc.sbuf_tensor("numbf", [128, 512], BF16))
    PTb = [stCsc.enter_context(nc.sbuf_tensor(f"PTb{i}", [128, 512], BF16)) for i in range(4)]
    PTs = stCsc.enter_context(nc.sbuf_tensor("PTs", [128, 512], BF16))

    xga_free = {}
    lfree = {}
    numfree = {}

    def phase_C(h, prev_d_done):
        stC = ExitStack()
        ps_q = [stC.enter_context(nc.psum_tensor(f"psq{h}{i}", [128, 512], F32)) for i in range(2)]
        ps_k = [stC.enter_context(nc.psum_tensor(f"psk{h}{i}", [128, 512], F32)) for i in range(2)]
        ps_v = [stC.enter_context(nc.psum_tensor(f"psv{h}{i}", [128, 512], F32)) for i in range(2)]
        ps_n = stC.enter_context(nc.psum_tensor(f"psn{h}", [1, 512], F32))
        ps_tv = stC.enter_context(nc.psum_tensor(f"pstv{h}", [128, 1024], BF16))
        qstate["trb"] = ps_tv[:]

        for (wsb, wsrc) in ((wq_sb, wqT), (wk_sb, wkT), (wv_sb, wvT)):
            for q in range(4):
                srcap = wsrc[q * 512:(q + 1) * 512, h * 128:(h + 1) * 128].rearrange("(a p) n -> p a n", p=128)
                dstap = wsb.ap()[:, q * 512:(q + 1) * 512].rearrange("p (a n) -> p a n", n=128)
                sch.op("gpsimd", lambda e, dstap=dstap, srcap=srcap: e.dma_start(out=dstap, in_=srcap),
                       waits=([] if h == 0 else [(t_c, t_c.n)]), inc=m_w, inc_n=16)
        W_ATTN = m_w.n
        sch.op("vector", lambda e: e.memset(nrm[:, 0:2], -3e38), waits=[(v_c, v_c.n)], inc=v_c)
        ev_prev = (0, 0, 0)
        vtr = {}
        for pr in range(8):
            tA = 2 * pr
            lwq = {}
            for q in range(4):
                for k in range(2):
                    tt = tA + k
                    buf = tt % 3
                    rank, tl = tt // 2, (tt % 2) * 512
                    slot = buf * 4 + q
                    waits = [(t_c, xga_free.get((buf, q), 0))]
                    if h == 0 and pr == 0 and q == 0:
                        waits.append((c_sem, C_XN))
                    mgs = m_xga[slot]
                    srcap = xn_ag.ap()[rank * D + q * 512: rank * D + (q + 1) * 512,
                                       tl:tl + 512].rearrange("(a p) n -> p a n", p=128)
                    dstap = xgab[buf][:, q * 2048:(q + 1) * 2048].rearrange("p (a n) -> p a n", n=512)
                    sch.op("sync", lambda e, dstap=dstap, srcap=srcap: e.dma_start(out=dstap, in_=srcap),
                           waits=waits, inc=mgs, inc_n=16)
                    lwq[(k, q)] = (mgs, mgs.n)
            for dc in range(16):
                for mi, (wsb, ps) in enumerate(((wq_sb, ps_q), (wk_sb, ps_k), (wv_sb, ps_v))):
                    lhsT = wsb[:, dc * 128:(dc + 1) * 128]
                    for k in range(2):
                        buf = (tA + k) % 3
                        waits = [lwq[(k, dc // 4)]]
                        if dc == 0:
                            waits.append((m_w, W_ATTN))
                            waits.append((s_c, ev_prev[mi]))
                            if pr == 0:
                                waits += prev_d_done
                        sch.op("tensor", lambda e, b=ps[k], lhsT=lhsT, buf=buf, dc=dc: e.matmul(
                            b[:], lhsT, xgab[buf][:, dc * 512:(dc + 1) * 512], start=(dc == 0), stop=(dc == 15)),
                            waits=waits, inc=(t_c if (mi == 2 and k == 1) else None))
                for k in range(2):
                    xga_free[((tA + k) % 3, dc // 4)] = t_c.n
            mmw = (t_c, t_c.n)
            for k in range(2):
                tt = tA + k
                col = tt * 512
                sch.op("scalar", lambda e, h=h, b=ps_q[k], col=col: e.activation(
                    out=QTn[h][:, col:col + 512], in_=b[:], func=AF.Identity,
                    bias=bvec[:, 0 + h:1 + h], scale=S4),
                    waits=[mmw, (v_a, BVEC_V)], inc=s_c)
            evq = s_c.n
            for k in range(2):
                tt = tA + k
                col = tt * 512
                sch.op("scalar", lambda e, h=h, b=ps_k[k], col=col: e.activation(
                    out=KT[:, col:col + 512], in_=b[:], func=AF.Identity,
                    bias=bvec[:, 2 + h:3 + h], scale=S4),
                    waits=[mmw], inc=s_c)
            evk = s_c.n
            vfree = []
            for k in range(2):
                tt = tA + k
                sch.op("scalar", lambda e, h=h, b=ps_v[k], k=k: e.activation(
                    out=vfs[k][:], in_=b[:], func=AF.Identity, bias=bvec[:, 4 + h:5 + h], scale=1.0),
                    waits=[mmw, (t_cn, vtr.get(k, 0))], inc=s_c)
                vfree.append(s_c.n)
            evv = s_c.n
            ev_prev = (evq, evk, evv)
            for k in range(2):
                tt = tA + k
                col = tt * 512
                sch.op("scalar", lambda e, h=h, col=col: e.activation(
                    out=sqs[0][:], in_=QTn[h][:, col:col + 512], func=AF.Square),
                    waits=[(t_cn, t_cn.n), (s_c, s_c.n)], inc=s_c)
                sch.op("tensor", lambda e: e.matmul(ps_n[:], onesk[:], sqs[0][:], start=True, stop=True),
                       waits=[(s_c, s_c.n), (v_c, v_c.n), (g_b, ONES_G)], inc=t_cn)
                sch.op("vector", lambda e: e.reduce_max(out=nrm[:, 2:3], in_=ps_n[:], axis=AX),
                       waits=[(t_cn, t_cn.n), (v_c, v_c.n)], inc=v_c)
                sch.op("vector", lambda e: e.tensor_max(out=nrm[:, 0:1], in0=nrm[:, 0:1], in1=nrm[:, 2:3]),
                       waits=[(v_c, v_c.n)], inc=v_c)
                sch.op("scalar", lambda e, col=col: e.activation(
                    out=sqs[1][:], in_=KT[:, col:col + 512], func=AF.Square),
                    waits=[(t_cn, t_cn.n), (s_c, s_c.n)], inc=s_c)
                sch.op("tensor", lambda e: e.matmul(ps_n[:], onesk[:], sqs[1][:], start=True, stop=True),
                       waits=[(s_c, s_c.n), (v_c, v_c.n)], inc=t_cn)
                sch.op("vector", lambda e: e.reduce_max(out=nrm[:, 3:4], in_=ps_n[:], axis=AX),
                       waits=[(t_cn, t_cn.n), (v_c, v_c.n)], inc=v_c)
                sch.op("vector", lambda e: e.tensor_max(out=nrm[:, 1:2], in0=nrm[:, 1:2], in1=nrm[:, 3:4]),
                       waits=[(v_c, v_c.n)], inc=v_c)
                for sub in range(4):
                    sch.op("tensor", lambda e, k=k, sub=sub: e.transpose(
                        ps_tv[:, (k * 4 + sub) * 128:(k * 4 + sub + 1) * 128],
                        vfs[k][:, sub * 128:(sub + 1) * 128], ident[:]),
                        waits=[(s_c, vfree[k]), (v_c, v_c.n), (v_be, v_be.n)], inc=t_cn)
                vtr[k] = t_cn.n
            sch.op("vector", lambda e, pr=pr: e.tensor_copy(out=Vtk[:, pr * 1024:(pr + 1) * 1024], in_=ps_tv[:]),
                   waits=[(t_cn, t_cn.n)], inc=v_c)
            pump(6 if h == 0 else 3)
        sch.op("vector", lambda e: e.tensor_mul(out=nrm[:, 4:5], in0=nrm[:, 0:1], in1=nrm[:, 1:2]),
               waits=[(v_c, v_c.n)], inc=v_c)
        sch.op("scalar", lambda e: e.activation(out=nrm2[:, 0:1], in_=nrm[:, 4:5], func=AF.Sqrt),
               waits=[(v_c, v_c.n), (s_c, s_c.n)], inc=s_c)
        sch.op("vector", lambda e: e.tensor_scalar_mul(out=nrm2[:, 1:2], in0=nrm2[:, 0:1], scalar1=-1.0),
               waits=[(s_c, s_c.n)], inc=v_c)
        sch.op("tensor", lambda e, b=ps_q[0]: e.matmul(b[:, 0:1], onesr_f[:], nrm2[:, 1:2], start=True, stop=True),
               waits=[(v_c, v_c.n), (s_c, s_c.n)], inc=t_cn)
        sch.op("vector", lambda e, h=h: e.tensor_copy(out=negBv[:, h:h + 1], in_=ps_q[0][:, 0:1]),
               waits=[(t_cn, t_cn.n)], inc=v_c)
        stC.close()
        if DBG:
            sch.op("sync", lambda e, h=h: e.dma_start(out=d_qt[h], in_=QTn[h][:]),
                   waits=[(s_c, s_c.n), (v_c, v_c.n)], inc=m_tap, inc_n=16)
            sch.op("sync", lambda e, h=h: e.dma_start(out=d_kt[h], in_=KT[:]),
                   waits=[(s_c, s_c.n)], inc=m_tap, inc_n=16)
            sch.op("sync", lambda e, h=h: e.dma_start(out=d_vt[h], in_=Vtk[:]),
                   waits=[(v_c, v_c.n)], inc=m_tap, inc_n=16)
            sch.op("sync", lambda e, h=h: e.dma_start(out=d_negb[:, h:h + 1], in_=negBv[:, h:h + 1], allow_slow_non_contiguous=True),
                   waits=[(v_c, v_c.n)], inc=m_tap, inc_n=16)
            return (m_tap, m_tap.n)
        return (v_c, v_c.n)

    def phase_D(h, bready):
        stD = ExitStack()
        ps_s = [stD.enter_context(nc.psum_tensor(f"pss{h}{i}", [128, 512], F32)) for i in range(2)]
        ps_nm = [stD.enter_context(nc.psum_tensor(f"psnm{h}{i}", [128, 512], F32)) for i in range(2)]
        ps_l = [stD.enter_context(nc.psum_tensor(f"psl{h}{i}", [1, 512], F32)) for i in range(2)]
        ps_r = stD.enter_context(nc.psum_tensor(f"psr{h}", [128, 512], F32))

        exp_done = {}
        av_done = {}
        add_done = {}
        gi = 0
        chunks = [(qt, kc) for qt in range(16) for kc in range(4 * (qt + 1))]
        ginfo = {}

        def emit_score(gi):
            qt, kc = chunks[gi]
            nkc = 4 * (qt + 1)
            diag = kc >= 4 * qt
            off = 128 * (kc - 4 * qt) if diag else 0
            n = 512 - off
            waits = [(s_d, exp_done.get(gi - 2, 0))]
            if gi == 0:
                waits.append(bready)
            sch.op("tensor", lambda e, qt=qt, kc=kc, off=off, n=n, b=ps_s[gi % 2]: e.matmul(
                b[:, off:off + n], KT[:, kc * 128:(kc + 1) * 128],
                QTn[h][:, qt * 512 + off: qt * 512 + off + n], start=True, stop=True),
                waits=waits, inc=t_ds)
            sw = t_ds.n
            pt = gi % 4
            sch.op("scalar", lambda e, b=ps_s[gi % 2], pt=pt, off=off, n=n: e.activation(
                out=PTb[pt][:, off:off + n], in_=b[:, off:off + n], func=AF.Exp,
                bias=negBv[:, h:h + 1], scale=1.0),
                waits=[(t_ds, sw), (t_da, av_done.get(gi - 4, 0)), (v_dl, add_done.get(gi - 4, 0))], inc=s_d)
            exp_done[gi] = s_d.n
            if diag:
                sch.op("vector", lambda e, pt=pt, off=off: e.tensor_mul(
                    out=PTb[pt][:, off:off + 128], in0=PTb[pt][:, off:off + 128], in1=triu[:]),
                    waits=[(s_d, s_d.n), (m_id, IDENT_W), (v_dl, v_dl.n)], inc=v_dm)
                ptw = (v_dm, v_dm.n)
            else:
                ptw = (s_d, s_d.n)
            ginfo[gi] = (qt, kc, off, n, pt, ptw, sw)

        def emit_consume(gi):
            qt, kc, off, n, pt, ptw, sw = ginfo[gi]
            nkc = 4 * (qt + 1)
            sb = qt % 2
            diag = kc >= 4 * qt
            first = (kc == 0)
            if diag:
                lmw = [ptw]
                if first:
                    lmw.append((v_dn, lfree.get((h, qt - 2), 0)))
                sch.op("tensor", lambda e, pt=pt, off=off, n=n, b=ps_l[sb], first=first, last=(kc == nkc - 1): e.matmul(
                    b[:, off:off + n], onesk[:], PTb[pt][:, off:off + n], start=first, stop=last),
                    waits=lmw)
            else:
                sub = kc % 4
                if sub == 0:
                    sch.op("vector", lambda e, pt=pt: e.tensor_copy(out=PTs[:], in_=PTb[pt][:]),
                           waits=[ptw, (t_dl, t_dl.n), (v_dl, v_dl.n)], inc=v_dl)
                else:
                    sch.op("vector", lambda e, pt=pt: e.tensor_add(out=PTs[:], in0=PTs[:], in1=PTb[pt][:]),
                           waits=[ptw, (v_dl, v_dl.n)], inc=v_dl)
                add_done[gi] = v_dl.n
                if sub == 3:
                    lmw = [(v_dl, v_dl.n)]
                    if kc == 3:
                        lmw.append((v_dn, lfree.get((h, qt - 2), 0)))
                    sch.op("tensor", lambda e, b=ps_l[sb], first=(kc == 3): e.matmul(
                        b[:], onesk[:], PTs[:], start=first, stop=False),
                        waits=lmw, inc=t_dl)
            avw = [ptw]
            if first:
                avw.append((s_dn, numfree.get((h, qt - 2), 0)))
            sch.op("tensor", lambda e, pt=pt, off=off, n=n, kc=kc, b=ps_nm[sb], first=first, last=(kc == nkc - 1): e.matmul(
                b[:, off:off + n], Vtk[:, kc * 128:(kc + 1) * 128], PTb[pt][:, off:off + n],
                start=first, stop=last),
                waits=avw, inc=t_da)
            av_done[gi] = t_da.n
            if kc == nkc - 1:
                emit_epilogue(qt)

        def emit_epilogue(qt):
            sb = qt % 2
            mmw = (t_da, t_da.n)
            qt_last_s = ginfo[gi_last_s[qt]][6]
            sch.op("vector", lambda e, b=ps_l[sb]: e.reciprocal(out=rrow[:], in_=b[:]),
                   waits=[mmw, (t_dl, t_dl.n), (t_dr, t_dr.n), (v_dn, v_dn.n)], inc=v_dn)
            lfree[(h, qt)] = v_dn.n
            sch.op("tensor", lambda e: e.matmul(ps_r[:], onesr_f[:], rrow[:], start=True, stop=True),
                   waits=[(v_dn, v_dn.n)], inc=t_dr)
            sch.op("scalar", lambda e, b=ps_nm[sb]: e.activation(out=numbf[:], in_=b[:], func=AF.Identity),
                   waits=[mmw, (v_dn, v_dn.n)], inc=s_dn)
            numfree[(h, qt)] = s_dn.n
            sch.op("vector", lambda e, qt=qt: e.tensor_tensor(
                out=QTn[h][:, qt * 512:(qt + 1) * 512], in0=numbf[:], in1=ps_r[:], op=ALU.mult),
                waits=[(s_dn, s_dn.n), (t_ds, qt_last_s), (t_dr, t_dr.n)], inc=v_dn)

        gi_last_s = {}
        NG = len(chunks)
        for gi in range(NG):
            emit_score(gi)
            qt, kc = chunks[gi]
            if kc == 4 * (qt + 1) - 1:
                gi_last_s[qt] = gi
            if gi >= 1:
                emit_consume(gi - 1)
        emit_consume(NG - 1)
        stD.close()

    prev = []
    for h in range(HPC):
        br = phase_C(h, prev)
        if h == HPC - 1:
            pump(len(qunits))
            emit_quant_collectives()
        phase_D(h, br)
        prev = [(v_dn, v_dn.n), (s_dn, s_dn.n), (t_dr, t_dr.n)]
        if DBG:
            sch.op("sync", lambda e, h=h: e.dma_start(out=d_num[h], in_=QTn[h][:]),
                   waits=[(v_dn, v_dn.n)], inc=m_tap, inc_n=16)
            prev = prev + [(m_tap, m_tap.n)]

    C_WGU, C_WPR = QUANT_COLLECTIVES
    B_CO = (m_cw[0].n, m_cw[1].n)
    stB.close()

    stE = ExitStack()
    wo_sb = stE.enter_context(nc.sbuf_tensor("wo_sb", [128, HPC * D], BF16))
    att_st = [stE.enter_context(nc.sbuf_tensor(f"attst{i}", [128, D], BF16)) for i in range(2)]
    psE = ExitStack()
    ps_o = [psE.enter_context(nc.psum_tensor(f"pso{i}", [128, 512], F32)) for i in range(8)]

    for hc in range(HPC):
        sch.op("gpsimd", lambda e, hc=hc: e.dma_start(out=wo_sb[:, hc * D:(hc + 1) * D],
                                                      in_=woT[hc * 128:(hc + 1) * 128, :]),
               inc=m_wo, inc_n=16)
    W_WO = m_wo.n
    efree = {}
    efree2 = {}
    edma = {}
    for qi in range(S // 128):
        bs = (qi % 2) * 4
        stg = att_st[qi % 2]
        for j in range(4):
            waits = [(v_a, 2), (m_wo, W_WO), (g_b, ONES_G),
                     (v_e, efree.get(qi - 2, 0)), (s_e, efree2.get(qi - 2, 0))]
            if qi == 0 and j == 0:
                waits += prev
            sch.op("tensor", lambda e, j=j, b=ps_o[bs + j]: e.matmul(
                b[:], onesr_b[:], bo8bf[:, j * 512:(j + 1) * 512], start=True, stop=False),
                waits=waits)
        for hh in range(HPC):
            for j in range(4):
                sch.op("tensor", lambda e, hh=hh, qi=qi, j=j, b=ps_o[bs + j]: e.matmul(
                    b[:], QTn[hh][:, qi * 128:(qi + 1) * 128],
                    wo_sb[:, hh * D + j * 512: hh * D + (j + 1) * 512],
                    start=False, stop=(hh == HPC - 1)),
                    inc=(t_e if (hh == HPC - 1 and j == 3) else None))
        mmw = (t_e, t_e.n)
        for j in range(4):
            if j < 2:
                sch.op("vector", lambda e, stg=stg, j=j, b=ps_o[bs + j]: e.tensor_copy(
                    out=stg[:, j * 512:(j + 1) * 512], in_=b[:]),
                    waits=[mmw, (m_cw[0], B_CO[0]), (m_cw[1], B_CO[1]), (m_ca[qi % 2], edma.get(qi - 2, 0))], inc=v_e)
            else:
                sch.op("scalar", lambda e, stg=stg, j=j, b=ps_o[bs + j]: e.activation(
                    out=stg[:, j * 512:(j + 1) * 512], in_=b[:], func=AF.Identity),
                    waits=[mmw, (m_cw[0], B_CO[0]), (m_cw[1], B_CO[1]), (m_ca[qi % 2], edma.get(qi - 2, 0))], inc=s_e)
        efree[qi] = v_e.n
        efree2[qi] = s_e.n
        mca = m_ca[qi % 2]
        sch.op("sync", lambda e, qi=qi, stg=stg: e.dma_start(
            out=c_att.ap()[qi * 128:(qi + 1) * 128, :], in_=stg[:]),
            waits=[(v_e, v_e.n), (s_e, s_e.n)], inc=mca, inc_n=16)
        edma[qi] = mca.n
    sch.op("gpsimd", lambda e: e.collective_compute("ReduceScatter", ALU.add, replica_groups=RG,
                                                    ins=[c_att.ap().opt()], outs=[att_rs.ap().opt()]),
           waits=[(m_ca[0], m_ca[0].n), (m_ca[1], m_ca[1].n)], inc=c_sem)
    sch.op("gpsimd", lambda e: e.collective_compute("AllReduce", ALU.add, replica_groups=RG,
                                                    ins=[c_bar.ap().opt()], outs=[c_bar2.ap().opt()]),
           inc=c_sem)
    C_RS = c_sem.n
    E_V, E_S, E_CO0, E_CO1 = v_e.n, s_e.n, m_ca[0].n, m_ca[1].n
    if DBG:
        sch.op("sync", lambda e: e.dma_start(out=d_catt, in_=c_att.ap()), waits=[(m_ca[0], m_ca[0].n), (m_ca[1], m_ca[1].n)], inc=m_tap, inc_n=16)
        sch.op("sync", lambda e: e.dma_start(out=d_rs, in_=att_rs.ap()), waits=[(c_sem, C_RS)], inc=m_tap, inc_n=16)
    psE.close()
    stE.close()
    stCsc.close()
    stCDE.close()

    stF = ExitStack()
    hT = stF.enter_context(nc.sbuf_tensor("hT", [128, 64 * 512], BF16))
    wgt = [stF.enter_context(nc.sbuf_tensor(f"wgt{i}", [128, 2048], BF16)) for i in range(4)]
    wut = [stF.enter_context(nc.sbuf_tensor(f"wut{i}", [128, 2048], BF16)) for i in range(4)]
    sgs = [stF.enter_context(nc.sbuf_tensor(f"sgs{i}", [128, 512], F32)) for i in range(2)]
    hti = [stF.enter_context(nc.sbuf_tensor(f"hti{i}", [128, 512], BF16)) for i in range(4)]
    hsq = stF.enter_context(nc.sbuf_tensor("hsq", [128, 512], BF16))
    wpt = [stF.enter_context(nc.sbuf_tensor(f"wpt{i}", [128, 2048], BF16)) for i in range(2)]
    fxb = [stF.enter_context(nc.sbuf_tensor(f"fxb{i}", [128, 1024], F32)) for i in range(2)]
    fab = [stF.enter_context(nc.sbuf_tensor(f"fab{i}", [128, 1024], BF16)) for i in range(2)]
    fob = [stF.enter_context(nc.sbuf_tensor(f"fob{i}", [128, 1024], F32)) for i in range(2)]

    sch.wait("vector", [(m_ca[0], E_CO0), (m_ca[1], E_CO1)])
    sch.wait("scalar", [(m_ca[0], E_CO0), (m_ca[1], E_CO1)])
    sch.wait("tensor", [(v_e, E_V), (s_e, E_S)])

    wslot = {}
    vffree = {}
    trev = {}
    rmsred = 0
    ppD = {}
    outD = {}
    for slab in range(2):
        stG = ExitStack()
        ps_g = [stG.enter_context(nc.psum_tensor(f"psg{slab}{i}", [128, 512], F32)) for i in range(4)]
        ps_u = [stG.enter_context(nc.psum_tensor(f"psu{slab}{i}", [128, 512], F32)) for i in range(4)]
        for g in range(16):
            lwq = {}
            for q in range(4):
                qs = q
                rg_, cg = (g * 512) // 2048, (g * 512) % 2048
                ru, cu = ((g + 16) * 512) // 2048, ((g + 16) * 512) % 2048
                waits = [(t_f, wslot.get(qs, 0))]
                if slab == 0 and g == 0 and q == 0:
                    waits.append((c_sem, C_WGU))
                mg1, mg2 = m_wgu[qs], m_wgu[4 + qs]
                srcg = wgu_ag.ap()[rg_ * D + q * 512: rg_ * D + (q + 1) * 512,
                                   cg:cg + 512].rearrange("(a p) n -> p a n", p=128)
                srcu = wgu_ag.ap()[ru * D + q * 512: ru * D + (q + 1) * 512,
                                   cu:cu + 512].rearrange("(a p) n -> p a n", p=128)
                dstg = wgt[qs].ap().rearrange("p (a n) -> p a n", n=512)
                dstu = wut[qs].ap().rearrange("p (a n) -> p a n", n=512)
                sch.op("sync", lambda e, dstg=dstg, srcg=srcg: e.dma_start(out=dstg, in_=srcg),
                       waits=waits, inc=mg1, inc_n=16)
                sch.op("sync", lambda e, dstu=dstu, srcu=srcu: e.dma_start(out=dstu, in_=srcu),
                       waits=[(t_f, wslot.get(qs, 0))], inc=mg2, inc_n=16)
                lwq[q] = [(mg1, mg1.n), (mg2, mg2.n)]
            for dc in range(16):
                qs = dc // 4
                co = (dc % 4) * 512
                for tc in range(4):
                    lhsT = xnT[:, dc * 1024 + (slab * 4 + tc) * 128: dc * 1024 + (slab * 4 + tc) * 128 + 128]
                    waits = list(lwq[dc // 4])
                    if dc == 0:
                        waits.append((v_f, trev.get((slab, g - 1), 0) if g > 0 else (trev.get((slab - 1, 15), 0))))
                        waits.append((v_ae, v_ae.n))
                    sch.op("tensor", lambda e, b=ps_g[tc], lhsT=lhsT, qs=qs, co=co, dc=dc: e.matmul(
                        b[:], lhsT, wgt[qs][:, co:co + 512], start=(dc == 0), stop=(dc == 15)), waits=waits)
                    sch.op("tensor", lambda e, b=ps_u[tc], lhsT=lhsT, qs=qs, co=co, dc=dc: e.matmul(
                        b[:], lhsT, wut[qs][:, co:co + 512], start=(dc == 0), stop=(dc == 15)),
                        inc=(t_f if tc == 3 else None))
                if dc % 4 == 3:
                    wslot[qs] = t_f.n
            mmw = (t_f, t_f.n)
            for tc in range(4):
                sg = sgs[tc % 2]
                ht = hti[tc % 4]
                sch.op("scalar", lambda e, sg=sg, b=ps_g[tc]: e.activation(out=sg[:], in_=b[:], func=AF.Silu),
                       waits=[mmw, (v_f, v_f.n)], inc=s_f)
                sch.op("vector", lambda e, sg=sg, ht=ht, b=ps_u[tc]: e.tensor_tensor(
                    out=ht[:], in0=sg[:], in1=b[:], op=ALU.mult),
                    waits=[(s_f, s_f.n), (t_ft, t_ft.n)], inc=v_f)
                vffree[(slab, g, tc)] = v_f.n
                htw = (v_f, v_f.n)
                sch.op("scalar", lambda e, ht=ht, g=g, tc=tc: e.activation(
                    out=hsq[:], in_=ht[:], func=AF.Square, accum_out=rmsp[:, tc * 16 + g:tc * 16 + g + 1]),
                    waits=[htw, (v_f, rmsred)], inc=s_f)
                for sub in range(4):
                    pst = ps_u[tc].bitcast(BF16)[:, sub * 128:(sub + 1) * 128]
                    sch.op("tensor", lambda e, ht=ht, sub=sub, pst=pst: e.transpose(
                        pst, ht[:, sub * 128:(sub + 1) * 128], ident[:]),
                        waits=[htw, (s_f, s_f.n), (v_f, v_f.n)], inc=t_ft)
                dst = bass.AP(hT, (g * 4) * 512 + tc * 128, [[64 * 512, 128], [512, 4], [1, 128]])
                srcv = ps_u[tc].bitcast(BF16)[:, 0:512].rearrange("p (a n) -> p a n", n=128)
                sch.op("vector", lambda e, dst=dst, srcv=srcv: e.tensor_copy(out=dst, in_=srcv),
                       waits=[(t_ft, t_ft.n)], inc=v_f)
            trev[(slab, g)] = v_f.n
        for tc in range(4):
            sch.op("vector", lambda e, tc=tc: e.reduce_sum(out=lnst[:, 0:1], in_=rmsp[:, tc * 16:(tc + 1) * 16], axis=AX),
                   waits=[(s_f, s_f.n), (v_f, v_f.n)], inc=v_f)
            sch.op("vector", lambda e: e.tensor_scalar(out=lnst[:, 2:3], in0=lnst[:, 0:1],
                                                       scalar1=1.0 / HID, scalar2=RMS_EPS,
                                                       op0=ALU.mult, op1=ALU.add),
                   waits=[(v_f, v_f.n)], inc=v_f)
            sch.op("scalar", lambda e: e.activation(out=lnst[:, 1:2], in_=lnst[:, 2:3], func=AF.Sqrt),
                   waits=[(v_f, v_f.n), (s_f, s_f.n)], inc=s_f)
            sch.op("vector", lambda e, slab=slab, tc=tc: e.reciprocal(
                out=rstd8[:, slab * 4 + tc:slab * 4 + tc + 1], in_=lnst[:, 1:2]),
                waits=[(s_f, s_f.n)], inc=v_f)
        rmsred = v_f.n
        if DBG:
            sch.op("sync", lambda e, slab=slab: e.dma_start(out=d_ht[slab], in_=hT[:]),
                   waits=[(v_f, v_f.n)], inc=m_tap, inc_n=16)
            sch.op("sync", lambda e: e.dma_start(out=d_rstd, in_=rstd8[:]),
                   waits=[(v_f, v_f.n)], inc=m_tap, inc_n=16)
        stG.close()

        stP = ExitStack()
        ps_p = [stP.enter_context(nc.psum_tensor(f"psp{slab}{i}", [128, 512], F32)) for i in range(8)]
        for tp in range(2):
            tcs = (2 * tp, 2 * tp + 1)
            for hc in range(64):
                slot = hc % 2
                waits = [(t_p, ppD.get(slot, 0))]
                if slab == 0 and tp == 0 and hc < 2:
                    waits.append((c_sem, C_WPR))
                mps = m_wp[slot]
                srcap = wpr_ag.ap().rearrange("(r q) j -> q r j", r=NCORES)[hc * 128:(hc + 1) * 128, :, :]
                dstap = wpt[slot].ap().rearrange("p (r j) -> p r j", j=256)
                sch.op("sync", lambda e, dstap=dstap, srcap=srcap: e.dma_start(out=dstap, in_=srcap),
                       waits=waits, inc=mps, inc_n=16)
                lw = (mps, mps.n)
                for k, tc in enumerate(tcs):
                    lhsT = hT[:, hc * 512 + tc * 128: hc * 512 + (tc + 1) * 128]
                    for j in range(4):
                        waits2 = [lw]
                        if hc == 0 and j == 0:
                            waits2.append((v_f, v_f.n))
                            waits2.append((v_p, v_p.n))
                        sch.op("tensor", lambda e, b=ps_p[k * 4 + j], lhsT=lhsT, s=slot, j=j, hc=hc: e.matmul(
                            b[:], lhsT, wpt[s][:, j * 512:(j + 1) * 512], start=(hc == 0), stop=(hc == 63)),
                            waits=waits2, inc=(t_p if (k == 1 and j == 3) else None))
                ppD[slot] = t_p.n
            mmw = (t_p, t_p.n)
            for k, tc in enumerate(tcs):
                stc = slab * 4 + tc
                row = slab * 512 + tc * 128
                for half in range(2):
                    mfa, mfx_ = m_fx[k], m_fx[2 + k]
                    mo = m_out4[k * 2 + half]
                    sch.op("sync", lambda e, k=k, row=row, half=half: e.dma_start(
                        out=fab[k][:], in_=att_rs.ap()[row:row + 128, half * 1024:(half + 1) * 1024]),
                        waits=[(c_sem, C_RS), (v_p, v_p.n), (mo, outD.get((k, half), 0))],
                        inc=mfa, inc_n=16)
                    sch.op("sync", lambda e, k=k, row=row, half=half: e.dma_start(
                        out=fxb[k][:], in_=x_in[row:row + 128, half * 1024:(half + 1) * 1024]),
                        inc=mfx_, inc_n=16)
                    fxw = (mfa, mfa.n)
                    fxw2 = (mfx_, mfx_.n)
                    for jj in range(2):
                        j = half * 2 + jj
                        sch.op("vector", lambda e, k=k, j=j, jj=jj, stc=stc, b=ps_p[k * 4 + j]: e.scalar_tensor_tensor(
                            out=fob[k][:, jj * 512:(jj + 1) * 512], in0=b[:], scalar=rstd8[:, stc:stc + 1],
                            in1=fab[k][:, jj * 512:(jj + 1) * 512], op0=ALU.mult, op1=ALU.add),
                            waits=[mmw, fxw, (v_f, v_f.n)], inc=v_p)
                        sch.op("vector", lambda e, k=k, jj=jj: e.tensor_add(
                            out=fob[k][:, jj * 512:(jj + 1) * 512], in0=fob[k][:, jj * 512:(jj + 1) * 512],
                            in1=fxb[k][:, jj * 512:(jj + 1) * 512]),
                            waits=[(v_p, v_p.n), fxw2], inc=v_p)
                    sch.op("sync", lambda e, k=k, row=row, half=half: e.dma_start(
                        out=out_t[row:row + 128, half * 1024:(half + 1) * 1024], in_=fob[k][:]),
                        waits=[(v_p, v_p.n)], inc=mo, inc_n=16)
                    outD[(k, half)] = mo.n
        stP.close()
    stF.close()

    sch.wait("sync", [(m, m.n) for m in m_out4])
    if DBG:
        sch.wait("sync", [(m_tap, m_tap.n)])
    sch.wait("gpsimd", [(c_sem, c_sem.n)])

    with nc.Block() as blk:
        blk.sync(lambda e: [f(e) for f in sch.streams["sync"]])
        blk.tensor(lambda e: [f(e) for f in sch.streams["tensor"]])
        blk.vector(lambda e: [f(e) for f in sch.streams["vector"]])
        blk.scalar(lambda e: [f(e) for f in sch.streams["scalar"]])
        blk.gpsimd(lambda e: [f(e) for f in sch.streams["gpsimd"]])
    big.close()
    return nc


def kernel(L0_attn_X, L0_attn_Wq, L0_attn_bq, L0_attn_Wk, L0_attn_bk, L0_attn_Wv,
           L0_attn_bv, L0_attn_Wo, L0_attn_bo, L0_ffn_w_gate_up, L0_ffn_w_proj):
    X = np.ascontiguousarray(np.asarray(L0_attn_X, dtype=np.float32).reshape(S, D))
    WqT = np.ascontiguousarray(np.asarray(L0_attn_Wq, np.float32).T)
    WkT = np.ascontiguousarray(np.asarray(L0_attn_Wk, np.float32).T)
    WvT = np.ascontiguousarray(np.asarray(L0_attn_Wv, np.float32).T)
    WoT = np.ascontiguousarray(np.asarray(L0_attn_Wo, np.float32).T)
    bq = np.asarray(L0_attn_bq, np.float32)
    bk = np.asarray(L0_attn_bk, np.float32)
    bv = np.asarray(L0_attn_bv, np.float32)
    bo = np.asarray(L0_attn_bo, np.float32).reshape(1, D)
    wgu = np.asarray(L0_ffn_w_gate_up, np.float32)
    wpr = np.asarray(L0_ffn_w_proj, np.float32)
    ident = np.eye(128, dtype=np.float32)
    triu = np.triu(np.ones((128, 128), dtype=np.float32))

    in_maps = []
    for c in range(NCORES):
        j0 = c * HPC * DH
        b6_c = np.stack([bq[j0:j0 + DH], bq[j0 + DH:j0 + 2 * DH],
                         bk[j0:j0 + DH], bk[j0 + DH:j0 + 2 * DH],
                         bv[j0:j0 + DH], bv[j0 + DH:j0 + 2 * DH]], axis=0)
        in_maps.append({
            "x": np.ascontiguousarray(X[c * T:(c + 1) * T]),
            "wqT": np.ascontiguousarray(WqT[:, j0:j0 + HPC * DH]),
            "wkT": np.ascontiguousarray(WkT[:, j0:j0 + HPC * DH]),
            "wvT": np.ascontiguousarray(WvT[:, j0:j0 + HPC * DH]),
            "woT": np.ascontiguousarray(WoT[j0:j0 + HPC * DH, :]),
            "b6": np.ascontiguousarray(b6_c),
            "bo": bo,
            "wgu": np.ascontiguousarray(wgu[c * (2 * HID // NCORES):(c + 1) * (2 * HID // NCORES)]),
            "wpr": np.ascontiguousarray(wpr[c * (D // NCORES):(c + 1) * (D // NCORES)]),
            "identin": ident,
            "triuin": triu,
        })

    nc = build()
    res = run_bass_kernel_spmd(nc, in_maps, core_ids=list(range(NCORES)))
    out = np.concatenate([res.results[c]["out"] for c in range(NCORES)], axis=0)
    return out.reshape(1, S, D)


# version 17
# speedup vs baseline: 1.0159x; 1.0159x over previous
from contextlib import ExitStack

import numpy as np
import concourse.bass as bass
import concourse.mybir as mybir
from concourse.bass_utils import run_bass_kernel_spmd

F32 = mybir.dt.float32
BF16 = mybir.dt.bfloat16
AF = mybir.ActivationFunctionType
ALU = mybir.AluOpType
AX = mybir.AxisListType.X

NCORES = 8
S = 8192
D = 2048
NH = 16
DH = 128
HID = 8192
T = S // NCORES
HPC = NH // NCORES
S4 = float(np.float32(1.0 / np.sqrt(np.sqrt(float(DH)))))
LN_EPS = 1e-5
RMS_EPS = float(np.finfo(np.float32).eps)

ENGS = ("sync", "tensor", "vector", "scalar", "gpsimd")


class Sem:
    __slots__ = ("h", "n", "name")

    def __init__(self, h, name):
        self.h = h
        self.n = 0
        self.name = name


class Sched:
    def __init__(self, nc):
        self.nc = nc
        self.streams = {e: [] for e in ENGS}

    def sem(self, name):
        return Sem(self.nc.alloc_semaphore(name=name), name)

    def op(self, eng, build, waits=(), inc=None, inc_n=1):
        wl = []
        for w in waits:
            if isinstance(w, tuple):
                s, v = w
            else:
                s, v = w, w.n
            if v > 0:
                wl.append((s.h, v))
        il = None
        if inc is not None:
            inc.n += inc_n
            il = (inc.h, inc_n)

        def thunk(e, build=build, wl=wl, il=il):
            for h, v in wl:
                e.wait_ge(h, v)
            r = build(e)
            if il is not None:
                r.then_inc(il[0], il[1])

        self.streams[eng].append(thunk)

    def wait(self, eng, waits):
        wl = []
        for w in waits:
            if isinstance(w, tuple):
                s, v = w
            else:
                s, v = w, w.n
            if v > 0:
                wl.append((s.h, v))

        def thunk(e, wl=wl):
            for h, v in wl:
                e.wait_ge(h, v)

        self.streams[eng].append(thunk)


DBG = False


def build():
    nc = bass.Bass("TRN2", target_bir_lowering=False)

    x_in = nc.dram_tensor("x", [T, D], F32, kind="ExternalInput").ap()
    wqT = nc.dram_tensor("wqT", [D, HPC * DH], F32, kind="ExternalInput").ap()
    wkT = nc.dram_tensor("wkT", [D, HPC * DH], F32, kind="ExternalInput").ap()
    wvT = nc.dram_tensor("wvT", [D, HPC * DH], F32, kind="ExternalInput").ap()
    woT = nc.dram_tensor("woT", [HPC * DH, D], F32, kind="ExternalInput").ap()
    b6 = nc.dram_tensor("b6", [6, DH], F32, kind="ExternalInput").ap()
    bo_in = nc.dram_tensor("bo", [1, D], F32, kind="ExternalInput").ap()
    wgu_in = nc.dram_tensor("wgu", [2 * HID // NCORES, D], F32, kind="ExternalInput").ap()
    wpr_in = nc.dram_tensor("wpr", [D // NCORES, HID], F32, kind="ExternalInput").ap()
    ident_in = nc.dram_tensor("identin", [128, 128], F32, kind="ExternalInput").ap()
    triu_in = nc.dram_tensor("triuin", [128, 128], F32, kind="ExternalInput").ap()
    out_t = nc.dram_tensor("out", [T, D], F32, kind="ExternalOutput").ap()
    if DBG:
        d_xnag = nc.dram_tensor("d_xnag", [NCORES * D, T], BF16, kind="ExternalOutput").ap()
        d_qt = [nc.dram_tensor(f"d_qt{h}", [128, S], BF16, kind="ExternalOutput").ap() for h in range(2)]
        d_kt = [nc.dram_tensor(f"d_kt{h}", [128, S], BF16, kind="ExternalOutput").ap() for h in range(2)]
        d_vt = [nc.dram_tensor(f"d_vt{h}", [128, S], BF16, kind="ExternalOutput").ap() for h in range(2)]
        d_num = [nc.dram_tensor(f"d_num{h}", [128, S], BF16, kind="ExternalOutput").ap() for h in range(2)]
        d_negb = nc.dram_tensor("d_negb", [128, 2], F32, kind="ExternalOutput").ap()
        d_catt = nc.dram_tensor("d_catt", [S, D], BF16, kind="ExternalOutput").ap()
        d_rs = nc.dram_tensor("d_rs", [T, D], BF16, kind="ExternalOutput").ap()
        d_wgu = nc.dram_tensor("d_wgu", [NCORES * D, 2 * HID // NCORES], BF16, kind="ExternalOutput").ap()
        d_wpr = nc.dram_tensor("d_wpr", [NCORES * HID, D // NCORES], BF16, kind="ExternalOutput").ap()
        d_ht = [nc.dram_tensor(f"d_ht{i}", [128, 64 * 512], BF16, kind="ExternalOutput").ap() for i in range(2)]
        d_rstd = nc.dram_tensor("d_rstd", [128, 8], F32, kind="ExternalOutput").ap()
        m_tap = None

    c_xn = nc.dram_tensor("c_xn", [D, T], BF16, kind="Internal")
    xn_ag = nc.dram_tensor("xn_ag", [NCORES * D, T], BF16, kind="Internal", addr_space="Shared")
    c_wgu = nc.dram_tensor("c_wgu", [D, 2 * HID // NCORES], BF16, kind="Internal")
    wgu_ag = nc.dram_tensor("wgu_ag", [NCORES * D, 2 * HID // NCORES], BF16, kind="Internal", addr_space="Shared")
    c_wpr = nc.dram_tensor("c_wpr", [HID, D // NCORES], BF16, kind="Internal")
    wpr_ag = nc.dram_tensor("wpr_ag", [NCORES * HID, D // NCORES], BF16, kind="Internal", addr_space="Shared")
    c_att = nc.dram_tensor("c_att", [S, D], BF16, kind="Internal")
    att_rs = nc.dram_tensor("att_rs", [T, D], BF16, kind="Internal")
    c_bar = nc.dram_tensor("c_bar", [1, 64], F32, kind="Internal")
    c_bar2 = nc.dram_tensor("c_bar2", [1, 64], F32, kind="Internal")

    RG = [list(range(NCORES))]
    sch = Sched(nc)

    m_x = [sch.sem(f"m_x{i}") for i in range(2)]
    m_b = sch.sem("m_b")
    m_id = sch.sem("m_id")
    m_w = sch.sem("m_w")
    m_wo = sch.sem("m_wo")
    m_q = [sch.sem(f"m_q{i}") for i in range(2)]
    m_xga = [sch.sem(f"m_xga{i}") for i in range(12)]
    m_wgu = [sch.sem(f"m_wgu{i}") for i in range(16)]
    m_wp = [sch.sem(f"m_wp{i}") for i in range(2)]
    m_fx = [sch.sem(f"m_fx{i}") for i in range(4)]
    m_cxn = sch.sem("m_cxn")
    m_cb = sch.sem("m_cb")
    m_cw = [sch.sem(f"m_cw{i}") for i in range(2)]
    m_ca = [sch.sem(f"m_ca{i}") for i in range(2)]
    m_out4 = [sch.sem(f"m_out{i}") for i in range(4)]
    c_sem = sch.sem("c_sem")
    m_tap = sch.sem("m_tap")

    v_a = sch.sem("v_a")
    s_a = sch.sem("s_a")
    t_a = sch.sem("t_a")
    v_ae = sch.sem("v_ae")

    v_b = sch.sem("v_b")
    g_b = sch.sem("g_b")
    t_b = sch.sem("t_b")
    v_be = sch.sem("v_be")

    t_c = sch.sem("t_c")
    s_c = sch.sem("s_c")
    t_cn = sch.sem("t_cn")
    v_c = sch.sem("v_c")

    s_d = sch.sem("s_d")
    t_ds = sch.sem("t_ds")
    t_da = sch.sem("t_da")
    v_dm = sch.sem("v_dm")
    s_dn = sch.sem("s_dn")
    v_dl = sch.sem("v_dl")
    t_dl = sch.sem("t_dl")
    v_dn = sch.sem("v_dn")
    t_dr = sch.sem("t_dr")

    t_e = sch.sem("t_e")
    v_e = sch.sem("v_e")
    s_e = sch.sem("s_e")

    t_f = sch.sem("t_f")
    s_f = sch.sem("s_f")
    v_f = sch.sem("v_f")
    t_ft = sch.sem("t_ft")
    t_p = sch.sem("t_p")
    v_p = sch.sem("v_p")

    big = ExitStack()
    ident = big.enter_context(nc.sbuf_tensor("ident", [128, 128], BF16))
    triu = big.enter_context(nc.sbuf_tensor("triu", [128, 128], BF16))
    onesk = big.enter_context(nc.sbuf_tensor("onesk", [128, 1], BF16))
    onesr_f = big.enter_context(nc.sbuf_tensor("onesr_f", [1, 128], F32))
    onesr_b = big.enter_context(nc.sbuf_tensor("onesr_b", [1, 128], BF16))
    bvec = big.enter_context(nc.sbuf_tensor("bvec", [128, 8], F32))
    bo8bf = big.enter_context(nc.sbuf_tensor("bo8bf", [1, D], BF16))
    negBv = big.enter_context(nc.sbuf_tensor("negBv", [128, 2], F32))
    nrm = big.enter_context(nc.sbuf_tensor("nrm", [1, 8], F32))
    nrm2 = big.enter_context(nc.sbuf_tensor("nrm2", [1, 8], F32))
    rrow = big.enter_context(nc.sbuf_tensor("rrow", [1, 512], F32))
    rstd8 = big.enter_context(nc.sbuf_tensor("rstd8", [128, 8], F32))
    rmsp = big.enter_context(nc.sbuf_tensor("rmsp", [128, 64], F32))
    lnst = big.enter_context(nc.sbuf_tensor("lnst", [128, 8], F32))
    xnT = big.enter_context(nc.sbuf_tensor("xnT", [128, 16 * 1024], BF16))

    stCDE = ExitStack()
    wq_sb = stCDE.enter_context(nc.sbuf_tensor("wq_sb", [128, 2048], BF16))
    wk_sb = stCDE.enter_context(nc.sbuf_tensor("wk_sb", [128, 2048], BF16))
    wv_sb = stCDE.enter_context(nc.sbuf_tensor("wv_sb", [128, 2048], BF16))
    QTn = [stCDE.enter_context(nc.sbuf_tensor(f"QTn{h}", [128, S], BF16)) for h in range(HPC)]
    KT = stCDE.enter_context(nc.sbuf_tensor("KT", [128, S], BF16))
    Vtk = stCDE.enter_context(nc.sbuf_tensor("Vtk", [128, S], BF16))
    xgab = [stCDE.enter_context(nc.sbuf_tensor(f"xgab{i}", [128, 8192], BF16)) for i in range(3)]

    sch.op("gpsimd", lambda e: e.dma_start(out=ident[:], in_=ident_in), inc=m_id, inc_n=16)
    sch.op("gpsimd", lambda e: e.dma_start(out=triu[:], in_=triu_in), inc=m_id, inc_n=16)
    IDENT_W = 32
    sch.op("gpsimd", lambda e: e.memset(onesk[:], 1.0), inc=g_b)
    sch.op("gpsimd", lambda e: e.memset(onesr_f[:], 1.0), inc=g_b)
    sch.op("gpsimd", lambda e: e.memset(onesr_b[:], 1.0), inc=g_b)
    sch.op("gpsimd", lambda e: e.memset(nrm2[:], 0.0), inc=g_b)
    sch.op("gpsimd", lambda e: e.memset(rrow[:, 0:64], 0.0), inc=g_b)
    ONES_G = g_b.n
    sch.op("gpsimd", lambda e: e.dma_start(out=c_bar.ap(), in_=rrow[:, 0:64]), waits=[(g_b, ONES_G)], inc=m_cb, inc_n=16)


    stA = ExitStack()
    xld = [stA.enter_context(nc.sbuf_tensor(f"xld{i}", [128, D], F32)) for i in range(2)]
    xnbuf = stA.enter_context(nc.sbuf_tensor("xnbuf", [128, D], BF16))
    psA = ExitStack()
    ps_tra = psA.enter_context(nc.psum_tensor("ps_tra", [128, 1024], BF16))
    ps_bias = psA.enter_context(nc.psum_tensor("ps_bias", [128, 8], F32))

    b6row = xld[0]
    borow = xld[1]
    sch.op("sync", lambda e: e.dma_start(out=b6row[0:1, 0:768].bitcast(F32) if False else b6row[0:1, 0:768], in_=b6.rearrange("a b -> (a b)")[None, :]), inc=m_b, inc_n=16)
    sch.op("sync", lambda e: e.dma_start(out=borow[0:1, :], in_=bo_in), inc=m_b, inc_n=16)
    sch.op("vector", lambda e: e.tensor_scalar_mul(out=bo8bf[:], in0=borow[0:1, :], scalar1=0.125),
           waits=[(m_b, 32)], inc=v_a)
    for i in range(6):
        sch.op("tensor",
               lambda e, i=i: e.matmul(ps_bias[:, i:i + 1], b6row[0:1, i * 128:(i + 1) * 128],
                                       onesr_f[:, 0:1], start=True, stop=True),
               waits=[(m_b, 32), (g_b, ONES_G)], inc=t_a)
    sch.op("vector", lambda e: e.tensor_scalar(out=bvec[:, 0:4], in0=ps_bias[:, 0:4],
                                               scalar1=S4, scalar2=None, op0=ALU.mult),
           waits=[(t_a, 6)], inc=v_a)
    sch.op("vector", lambda e: e.tensor_copy(out=bvec[:, 4:6], in_=ps_bias[:, 4:6]),
           waits=[(t_a, 6)], inc=v_a)
    BVEC_V = v_a.n

    NT = T // 128
    xn_done = [0, 0]
    for ti in range(NT):
        buf = xld[ti % 2]
        mxs = m_x[ti % 2]
        bgate = []
        if ti == 0:
            bgate = [(t_a, 6)]
        elif ti == 1:
            bgate = [(v_a, 1)]
        sch.op("sync", lambda e, ti=ti, buf=buf: e.dma_start(out=buf[:], in_=x_in[ti * 128:(ti + 1) * 128, :]),
               waits=[(v_a, xn_done[ti % 2])] + bgate, inc=mxs, inc_n=16)
        xw = (mxs, mxs.n)
        sch.op("vector", lambda e, buf=buf: e.reduce_sum(out=lnst[:, 0:1], in_=buf[:], axis=AX),
               waits=[xw, (s_a, s_a.n)], inc=v_a)
        sch.op("scalar", lambda e, buf=buf: e.activation(out=xnbuf[:], in_=buf[:], func=AF.Square,
                                                         accum_out=lnst[:, 1:2]),
               waits=[xw, (v_a, v_a.n), (t_a, t_a.n)], inc=s_a)
        sch.op("vector", lambda e: e.tensor_scalar_mul(out=lnst[:, 2:3], in0=lnst[:, 0:1], scalar1=1.0 / D),
               waits=[(v_a, v_a.n)], inc=v_a)
        sch.op("vector", lambda e: e.tensor_scalar(out=lnst[:, 3:4], in0=lnst[:, 2:3],
                                                   scalar1=lnst[:, 2:3], scalar2=-LN_EPS,
                                                   op0=ALU.mult, op1=ALU.add),
               waits=[(v_a, v_a.n)], inc=v_a)
        sch.op("vector", lambda e: e.scalar_tensor_tensor(out=lnst[:, 4:5], in0=lnst[:, 1:2],
                                                          scalar=1.0 / D, in1=lnst[:, 3:4],
                                                          op0=ALU.mult, op1=ALU.subtract),
               waits=[(v_a, v_a.n), (s_a, s_a.n)], inc=v_a)
        sch.op("scalar", lambda e: e.activation(out=lnst[:, 5:6], in_=lnst[:, 4:5], func=AF.Sqrt),
               waits=[(v_a, v_a.n)], inc=s_a)
        sch.op("vector", lambda e: e.reciprocal(out=lnst[:, 6:7], in_=lnst[:, 5:6]),
               waits=[(s_a, s_a.n)], inc=v_a)
        sch.op("vector", lambda e: e.scalar_tensor_tensor(out=lnst[:, 7:8], in0=lnst[:, 2:3],
                                                          scalar=-1.0, in1=lnst[:, 6:7],
                                                          op0=ALU.mult, op1=ALU.mult),
               waits=[(v_a, v_a.n)], inc=v_a)
        sch.op("vector", lambda e, buf=buf: e.tensor_scalar(out=xnbuf[:], in0=buf[:],
                                                            scalar1=lnst[:, 6:7], scalar2=lnst[:, 7:8],
                                                            op0=ALU.mult, op1=ALU.add),
               waits=[(v_a, v_a.n), (t_a, t_a.n), (s_a, s_a.n)], inc=v_a)
        xn_done[ti % 2] = v_a.n
        xnw = (v_a, v_a.n)
        for bq in range(2):
            for s8 in range(8):
                dc = bq * 8 + s8
                sch.op("tensor", lambda e, dc=dc, s8=s8: e.transpose(
                    ps_tra[:, s8 * 128:(s8 + 1) * 128], xnbuf[:, dc * 128:(dc + 1) * 128], ident[:]),
                    waits=[xnw, (m_id, IDENT_W), (v_ae, v_ae.n)], inc=t_a)
            dst = bass.AP(xnT, ti * 128 + bq * 8 * 1024, [[16 * 1024, 128], [1024, 8], [1, 128]])
            srcv = ps_tra.ap().rearrange("p (a n) -> p a n", n=128)
            sch.op("vector", lambda e, dst=dst, srcv=srcv: e.tensor_copy(out=dst, in_=srcv),
                   waits=[(t_a, t_a.n)], inc=v_ae)
    for dc in range(16):
        sch.op("sync", lambda e, dc=dc: e.dma_start(out=c_xn.ap()[dc * 128:(dc + 1) * 128, :],
                                                    in_=xnT[:, dc * 1024:(dc + 1) * 1024]),
               waits=[(v_ae, v_ae.n)], inc=m_cxn, inc_n=16)
    sch.op("gpsimd", lambda e: e.collective_compute("AllGather", ALU.bypass, replica_groups=RG,
                                                    ins=[c_xn.ap().opt()], outs=[xn_ag.ap().opt()]),
           waits=[(m_cxn, m_cxn.n)], inc=c_sem)
    sch.op("gpsimd", lambda e: e.collective_compute("AllReduce", ALU.add, replica_groups=RG,
                                                    ins=[c_bar.ap().opt()], outs=[c_bar2.ap().opt()]),
           waits=[(m_cb, 16)], inc=c_sem)
    C_XN = c_sem.n

    stB = ExitStack()
    wbf = [stB.enter_context(nc.sbuf_tensor(f"wbf{i}", [128, 512], BF16)) for i in range(2)]
    tqb = stB.enter_context(nc.sbuf_tensor("tqb", [128, 512], BF16))
    cbq = stB.enter_context(nc.sbuf_tensor("cbq", [128, 512], BF16))
    dbq = stB.enter_context(nc.sbuf_tensor("dbq", [128, 512], BF16))
    qbq = stB.enter_context(nc.sbuf_tensor("qbq", [128, 512], BF16))
    whb = stB.enter_context(nc.sbuf_tensor("whb", [128, 512], BF16))
    web = stB.enter_context(nc.sbuf_tensor("web", [128, 512], BF16))
    wfb = [stB.enter_context(nc.sbuf_tensor(f"wfb{i}", [128, 512], BF16)) for i in range(6)]
    gsum = stB.enter_context(nc.sbuf_tensor("gsum", [128, 8], F32))
    gscl = stB.enter_context(nc.sbuf_tensor("gscl", [128, 8], BF16))
    ginv = stB.enter_context(nc.sbuf_tensor("ginv", [128, 8], F32))
    wTs = [stB.enter_context(nc.sbuf_tensor(f"wTs{i}", [128, 512], BF16)) for i in range(2)]
    psB = ExitStack()
    ps_trb = psB.enter_context(nc.psum_tensor("ps_trb", [128, 1024], BF16))

    qunits = []
    for ti in range(16):
        for sc in range(4):
            qunits.append(("gu", ti, sc, wgu_in[ti * 128:(ti + 1) * 128, sc * 512:(sc + 1) * 512]))
    for ri in range(2):
        for sc in range(16):
            qunits.append(("pr", ri, sc, wpr_in[ri * 128:(ri + 1) * 128, sc * 512:(sc + 1) * 512]))

    WEFF = {}
    TRB = {}
    STG = {}
    qstate = {"ui": 0, "trb": ps_trb[:]}

    def emit_qunit():
        ui = qstate["ui"]
        if ui >= len(qunits):
            return False
        qstate["ui"] = ui + 1
        kind, i0, i1, srcap = qunits[ui]
        trb = qstate["trb"]
        wb = wbf[ui % 2]
        wf = wfb[ui % 6]
        war = [(v_b, WEFF.get(ui - 2, 0))]
        mqs = m_q[ui % 2]
        sch.op("gpsimd", lambda e, wb=wb, srcap=srcap: e.dma_start(out=wb[:], in_=srcap),
               waits=war, inc=mqs, inc_n=16)
        lw = (mqs, mqs.n)
        v3 = wb.ap().rearrange("p (g k) -> p g k", k=64)
        sch.op("vector", lambda e, v3=v3: e.reduce_sum(out=gsum[:], in_=v3, axis=AX, apply_absolute_value=True),
               waits=[lw, (v_b, v_b.n)], inc=v_b)
        sch.op("vector", lambda e: e.tensor_scalar(out=gscl[:], in0=gsum[:], scalar1=1.0 / 64,
                                                   scalar2=1e-8, op0=ALU.mult, op1=ALU.max),
               waits=[(v_b, v_b.n)], inc=v_b)
        sch.op("vector", lambda e: e.reciprocal(out=ginv[:], in_=gscl[:]), waits=[(v_b, v_b.n)], inc=v_b)
        ginv_b = bass.AP(ginv, 0, [[8, 128], [1, 8], [0, 64]])
        gscl_b = bass.AP(gscl, 0, [[8, 128], [1, 8], [0, 64]])
        tq3 = tqb.ap().rearrange("p (g k) -> p g k", k=64)
        sch.op("vector", lambda e, v3=v3, tq3=tq3, ginv_b=ginv_b: e.tensor_tensor(
            out=tq3, in0=v3, in1=ginv_b, op=ALU.mult),
            waits=[(v_b, v_b.n)], inc=v_b)
        sch.op("vector", lambda e: e.tensor_single_scalar(out=cbq[:], in_=tqb[:], scalar=0.5, op=ALU.is_gt),
               waits=[(v_b, v_b.n)], inc=v_b)
        sch.op("vector", lambda e: e.tensor_single_scalar(out=dbq[:], in_=tqb[:], scalar=-0.5, op=ALU.is_lt),
               waits=[(v_b, v_b.n)], inc=v_b)
        sch.op("vector", lambda e: e.tensor_sub(out=qbq[:], in0=cbq[:], in1=dbq[:]),
               waits=[(v_b, v_b.n)], inc=v_b)
        qb3 = qbq.ap().rearrange("p (g k) -> p g k", k=64)
        wh3 = whb.ap().rearrange("p (g k) -> p g k", k=64)
        sch.op("vector", lambda e, qb3=qb3, wh3=wh3, gscl_b=gscl_b: e.tensor_tensor(
            out=wh3, in0=qb3, in1=gscl_b, op=ALU.mult),
            waits=[(v_b, v_b.n)], inc=v_b)
        sch.op("vector", lambda e, wb=wb: e.tensor_sub(out=web[:], in0=whb[:], in1=wb[:]),
               waits=[(v_b, v_b.n)], inc=v_b)
        sch.op("vector", lambda e, wb=wb, wf=wf: e.tensor_add(out=wf[:], in0=wb[:], in1=web[:]),
               waits=[(v_b, v_b.n), (t_b, TRB.get(ui - 6, 0))], inc=v_b)
        WEFF[ui] = v_b.n
        ww = (v_b, v_b.n)

        if True:
            trb = qstate["trb"]
            st = wTs[ui % 2]
            for dc in range(4):
                sch.op("tensor", lambda e, wf=wf, dc=dc, trb=trb: e.transpose(
                    trb[:, dc * 128:(dc + 1) * 128], wf[:, dc * 128:(dc + 1) * 128], ident[:]),
                    waits=[ww, (m_id, IDENT_W), (v_be, v_be.n), (v_c, v_c.n)], inc=t_b)
            sch.op("vector", lambda e, st=st, trb=trb: e.tensor_copy(out=st[:], in_=trb[:, 0:512]),
                   waits=[(t_b, t_b.n), (m_cw[ui % 2], STG.get(ui - 2, 0))], inc=v_be)
            TRB[ui] = t_b.n
            ew = (v_be, v_be.n)
            for dc in range(4):
                if kind == "gu":
                    dst = c_wgu.ap()[(i1 * 4 + dc) * 128:(i1 * 4 + dc + 1) * 128, i0 * 128:(i0 + 1) * 128]
                else:
                    dst = c_wpr.ap()[(i1 * 4 + dc) * 128:(i1 * 4 + dc + 1) * 128, i0 * 128:(i0 + 1) * 128]
                sch.op("gpsimd", lambda e, st=st, dc=dc, dst=dst: e.dma_start(
                    out=dst, in_=st[:, dc * 128:(dc + 1) * 128]),
                    waits=[ew], inc=m_cw[ui % 2], inc_n=16)
            STG[ui] = m_cw[ui % 2].n
        return True

    def pump(n):
        for _ in range(n):
            if not emit_qunit():
                return

    for _ in range(4):
        pump(6)
    QUANT_COLLECTIVES = []

    def emit_quant_collectives():
        assert qstate["ui"] == len(qunits), qstate["ui"]
        sch.op("gpsimd", lambda e: e.collective_compute("AllGather", ALU.bypass, replica_groups=RG,
                                                        ins=[c_wgu.ap().opt()], outs=[wgu_ag.ap().opt()]),
               waits=[(m_cw[0], m_cw[0].n), (m_cw[1], m_cw[1].n)], inc=c_sem)
        QUANT_COLLECTIVES.append(c_sem.n)
        sch.op("gpsimd", lambda e: e.collective_compute("AllGather", ALU.bypass, replica_groups=RG,
                                                        ins=[c_wpr.ap().opt()], outs=[wpr_ag.ap().opt()]),
               waits=[(m_cw[0], m_cw[0].n), (m_cw[1], m_cw[1].n)], inc=c_sem)
        sch.op("gpsimd", lambda e: e.collective_compute("AllReduce", ALU.add, replica_groups=RG,
                                                        ins=[c_bar.ap().opt()], outs=[c_bar2.ap().opt()]),
               inc=c_sem)
        QUANT_COLLECTIVES.append(c_sem.n)
        QUANT_COLLECTIVES[0] = c_sem.n
        if DBG:
            sch.op("sync", lambda e: e.dma_start(out=d_wgu, in_=wgu_ag.ap()),
                   waits=[(c_sem, QUANT_COLLECTIVES[0])], inc=m_tap, inc_n=16)
            sch.op("sync", lambda e: e.dma_start(out=d_wpr, in_=wpr_ag.ap()),
                   waits=[(c_sem, QUANT_COLLECTIVES[1])], inc=m_tap, inc_n=16)
    if DBG:
        sch.op("sync", lambda e: e.dma_start(out=d_xnag, in_=xn_ag.ap()), waits=[(c_sem, C_XN)], inc=m_tap, inc_n=16)
    psB.close()
    psA.close()
    stB.close()
    stA.close()
    stCsc = ExitStack()
    sqs = [stCsc.enter_context(nc.sbuf_tensor(f"sqs{i}", [128, 512], BF16)) for i in range(2)]
    vfs = [stCsc.enter_context(nc.sbuf_tensor(f"vfs{i}", [128, 512], BF16)) for i in range(2)]
    numbf = stCsc.enter_context(nc.sbuf_tensor("numbf", [128, 512], BF16))
    PTb = [stCsc.enter_context(nc.sbuf_tensor(f"PTb{i}", [128, 512], BF16)) for i in range(4)]
    PTs = stCsc.enter_context(nc.sbuf_tensor("PTs", [128, 512], BF16))

    xga_free = {}
    lfree = {}
    numfree = {}

    def phase_C(h, prev_d_done):
        stC = ExitStack()
        ps_q = [stC.enter_context(nc.psum_tensor(f"psq{h}{i}", [128, 512], F32)) for i in range(2)]
        ps_k = [stC.enter_context(nc.psum_tensor(f"psk{h}{i}", [128, 512], F32)) for i in range(2)]
        ps_v = [stC.enter_context(nc.psum_tensor(f"psv{h}{i}", [128, 512], F32)) for i in range(2)]
        ps_n = stC.enter_context(nc.psum_tensor(f"psn{h}", [1, 512], F32))
        ps_tv = stC.enter_context(nc.psum_tensor(f"pstv{h}", [128, 1024], BF16))
        qstate["trb"] = ps_tv[:]

        for (wsb, wsrc) in ((wq_sb, wqT), (wk_sb, wkT), (wv_sb, wvT)):
            for q in range(4):
                srcap = wsrc[q * 512:(q + 1) * 512, h * 128:(h + 1) * 128].rearrange("(a p) n -> p a n", p=128)
                dstap = wsb.ap()[:, q * 512:(q + 1) * 512].rearrange("p (a n) -> p a n", n=128)
                sch.op("gpsimd", lambda e, dstap=dstap, srcap=srcap: e.dma_start(out=dstap, in_=srcap),
                       waits=([] if h == 0 else [(t_c, t_c.n)]), inc=m_w, inc_n=16)
        W_ATTN = m_w.n
        sch.op("vector", lambda e: e.memset(nrm[:, 0:2], -3e38), waits=[(v_c, v_c.n)], inc=v_c)
        ev_prev = (0, 0, 0)
        vtr = {}
        for pr in range(8):
            tA = 2 * pr
            lwq = {}
            for q in range(4):
                for k in range(2):
                    tt = tA + k
                    buf = tt % 3
                    rank, tl = tt // 2, (tt % 2) * 512
                    slot = buf * 4 + q
                    waits = [(t_c, xga_free.get((buf, q), 0))]
                    if h == 0 and pr == 0 and q == 0:
                        waits.append((c_sem, C_XN))
                    mgs = m_xga[slot]
                    srcap = xn_ag.ap()[rank * D + q * 512: rank * D + (q + 1) * 512,
                                       tl:tl + 512].rearrange("(a p) n -> p a n", p=128)
                    dstap = xgab[buf][:, q * 2048:(q + 1) * 2048].rearrange("p (a n) -> p a n", n=512)
                    sch.op("sync", lambda e, dstap=dstap, srcap=srcap: e.dma_start(out=dstap, in_=srcap),
                           waits=waits, inc=mgs, inc_n=16)
                    lwq[(k, q)] = (mgs, mgs.n)
            for dc in range(16):
                for mi, (wsb, ps) in enumerate(((wq_sb, ps_q), (wk_sb, ps_k), (wv_sb, ps_v))):
                    lhsT = wsb[:, dc * 128:(dc + 1) * 128]
                    for k in range(2):
                        buf = (tA + k) % 3
                        waits = [lwq[(k, dc // 4)]]
                        if dc == 0:
                            waits.append((m_w, W_ATTN))
                            waits.append((s_c, ev_prev[mi]))
                            if pr == 0:
                                waits += prev_d_done
                        sch.op("tensor", lambda e, b=ps[k], lhsT=lhsT, buf=buf, dc=dc: e.matmul(
                            b[:], lhsT, xgab[buf][:, dc * 512:(dc + 1) * 512], start=(dc == 0), stop=(dc == 15)),
                            waits=waits, inc=(t_c if (mi == 2 and k == 1) else None))
                for k in range(2):
                    xga_free[((tA + k) % 3, dc // 4)] = t_c.n
            mmw = (t_c, t_c.n)
            for k in range(2):
                tt = tA + k
                col = tt * 512
                sch.op("scalar", lambda e, h=h, b=ps_q[k], col=col: e.activation(
                    out=QTn[h][:, col:col + 512], in_=b[:], func=AF.Identity,
                    bias=bvec[:, 0 + h:1 + h], scale=S4),
                    waits=[mmw, (v_a, BVEC_V)], inc=s_c)
            evq = s_c.n
            for k in range(2):
                tt = tA + k
                col = tt * 512
                sch.op("scalar", lambda e, h=h, b=ps_k[k], col=col: e.activation(
                    out=KT[:, col:col + 512], in_=b[:], func=AF.Identity,
                    bias=bvec[:, 2 + h:3 + h], scale=S4),
                    waits=[mmw], inc=s_c)
            evk = s_c.n
            vfree = []
            for k in range(2):
                tt = tA + k
                sch.op("scalar", lambda e, h=h, b=ps_v[k], k=k: e.activation(
                    out=vfs[k][:], in_=b[:], func=AF.Identity, bias=bvec[:, 4 + h:5 + h], scale=1.0),
                    waits=[mmw, (t_cn, vtr.get(k, 0))], inc=s_c)
                vfree.append(s_c.n)
            evv = s_c.n
            ev_prev = (evq, evk, evv)
            for k in range(2):
                tt = tA + k
                col = tt * 512
                sch.op("scalar", lambda e, h=h, col=col: e.activation(
                    out=sqs[0][:], in_=QTn[h][:, col:col + 512], func=AF.Square),
                    waits=[(t_cn, t_cn.n), (s_c, s_c.n)], inc=s_c)
                sch.op("tensor", lambda e: e.matmul(ps_n[:], onesk[:], sqs[0][:], start=True, stop=True),
                       waits=[(s_c, s_c.n), (v_c, v_c.n), (g_b, ONES_G)], inc=t_cn)
                sch.op("vector", lambda e: e.reduce_max(out=nrm[:, 2:3], in_=ps_n[:], axis=AX),
                       waits=[(t_cn, t_cn.n), (v_c, v_c.n)], inc=v_c)
                sch.op("vector", lambda e: e.tensor_max(out=nrm[:, 0:1], in0=nrm[:, 0:1], in1=nrm[:, 2:3]),
                       waits=[(v_c, v_c.n)], inc=v_c)
                sch.op("scalar", lambda e, col=col: e.activation(
                    out=sqs[1][:], in_=KT[:, col:col + 512], func=AF.Square),
                    waits=[(t_cn, t_cn.n), (s_c, s_c.n)], inc=s_c)
                sch.op("tensor", lambda e: e.matmul(ps_n[:], onesk[:], sqs[1][:], start=True, stop=True),
                       waits=[(s_c, s_c.n), (v_c, v_c.n)], inc=t_cn)
                sch.op("vector", lambda e: e.reduce_max(out=nrm[:, 3:4], in_=ps_n[:], axis=AX),
                       waits=[(t_cn, t_cn.n), (v_c, v_c.n)], inc=v_c)
                sch.op("vector", lambda e: e.tensor_max(out=nrm[:, 1:2], in0=nrm[:, 1:2], in1=nrm[:, 3:4]),
                       waits=[(v_c, v_c.n)], inc=v_c)
                for sub in range(4):
                    sch.op("tensor", lambda e, k=k, sub=sub: e.transpose(
                        ps_tv[:, (k * 4 + sub) * 128:(k * 4 + sub + 1) * 128],
                        vfs[k][:, sub * 128:(sub + 1) * 128], ident[:]),
                        waits=[(s_c, vfree[k]), (v_c, v_c.n), (v_be, v_be.n)], inc=t_cn)
                vtr[k] = t_cn.n
            sch.op("vector", lambda e, pr=pr: e.tensor_copy(out=Vtk[:, pr * 1024:(pr + 1) * 1024], in_=ps_tv[:]),
                   waits=[(t_cn, t_cn.n)], inc=v_c)
            pump(6 if h == 0 else 3)
        sch.op("vector", lambda e: e.tensor_mul(out=nrm[:, 4:5], in0=nrm[:, 0:1], in1=nrm[:, 1:2]),
               waits=[(v_c, v_c.n)], inc=v_c)
        sch.op("scalar", lambda e: e.activation(out=nrm2[:, 0:1], in_=nrm[:, 4:5], func=AF.Sqrt),
               waits=[(v_c, v_c.n), (s_c, s_c.n)], inc=s_c)
        sch.op("vector", lambda e: e.tensor_scalar_mul(out=nrm2[:, 1:2], in0=nrm2[:, 0:1], scalar1=-1.0),
               waits=[(s_c, s_c.n)], inc=v_c)
        sch.op("tensor", lambda e, b=ps_q[0]: e.matmul(b[:, 0:1], onesr_f[:], nrm2[:, 1:2], start=True, stop=True),
               waits=[(v_c, v_c.n), (s_c, s_c.n)], inc=t_cn)
        sch.op("vector", lambda e, h=h: e.tensor_copy(out=negBv[:, h:h + 1], in_=ps_q[0][:, 0:1]),
               waits=[(t_cn, t_cn.n)], inc=v_c)
        stC.close()
        if DBG:
            sch.op("sync", lambda e, h=h: e.dma_start(out=d_qt[h], in_=QTn[h][:]),
                   waits=[(s_c, s_c.n), (v_c, v_c.n)], inc=m_tap, inc_n=16)
            sch.op("sync", lambda e, h=h: e.dma_start(out=d_kt[h], in_=KT[:]),
                   waits=[(s_c, s_c.n)], inc=m_tap, inc_n=16)
            sch.op("sync", lambda e, h=h: e.dma_start(out=d_vt[h], in_=Vtk[:]),
                   waits=[(v_c, v_c.n)], inc=m_tap, inc_n=16)
            sch.op("sync", lambda e, h=h: e.dma_start(out=d_negb[:, h:h + 1], in_=negBv[:, h:h + 1], allow_slow_non_contiguous=True),
                   waits=[(v_c, v_c.n)], inc=m_tap, inc_n=16)
            return (m_tap, m_tap.n)
        return (v_c, v_c.n)

    def phase_D(h, bready):
        stD = ExitStack()
        ps_s = [stD.enter_context(nc.psum_tensor(f"pss{h}{i}", [128, 512], F32)) for i in range(2)]
        ps_nm = [stD.enter_context(nc.psum_tensor(f"psnm{h}{i}", [128, 512], F32)) for i in range(2)]
        ps_l = [stD.enter_context(nc.psum_tensor(f"psl{h}{i}", [1, 512], F32)) for i in range(2)]
        ps_r = stD.enter_context(nc.psum_tensor(f"psr{h}", [128, 512], F32))

        exp_done = {}
        av_done = {}
        add_done = {}
        gi = 0
        chunks = [(qt, kc) for qt in range(16) for kc in range(4 * (qt + 1))]
        ginfo = {}

        def emit_score(gi):
            qt, kc = chunks[gi]
            nkc = 4 * (qt + 1)
            diag = kc >= 4 * qt
            off = 128 * (kc - 4 * qt) if diag else 0
            n = 512 - off
            waits = [(s_d, exp_done.get(gi - 2, 0))]
            if gi == 0:
                waits.append(bready)
            sch.op("tensor", lambda e, qt=qt, kc=kc, off=off, n=n, b=ps_s[gi % 2]: e.matmul(
                b[:, off:off + n], KT[:, kc * 128:(kc + 1) * 128],
                QTn[h][:, qt * 512 + off: qt * 512 + off + n], start=True, stop=True),
                waits=waits, inc=t_ds)
            sw = t_ds.n
            pt = gi % 4
            sch.op("scalar", lambda e, b=ps_s[gi % 2], pt=pt, off=off, n=n: e.activation(
                out=PTb[pt][:, off:off + n], in_=b[:, off:off + n], func=AF.Exp,
                bias=negBv[:, h:h + 1], scale=1.0),
                waits=[(t_ds, sw), (t_da, av_done.get(gi - 4, 0)), (v_dl, add_done.get(gi - 4, 0))], inc=s_d)
            exp_done[gi] = s_d.n
            if diag:
                sch.op("vector", lambda e, pt=pt, off=off: e.tensor_mul(
                    out=PTb[pt][:, off:off + 128], in0=PTb[pt][:, off:off + 128], in1=triu[:]),
                    waits=[(s_d, s_d.n), (m_id, IDENT_W), (v_dl, v_dl.n)], inc=v_dm)
                ptw = (v_dm, v_dm.n)
            else:
                ptw = (s_d, s_d.n)
            ginfo[gi] = (qt, kc, off, n, pt, ptw, sw)

        def emit_consume(gi):
            qt, kc, off, n, pt, ptw, sw = ginfo[gi]
            nkc = 4 * (qt + 1)
            sb = qt % 2
            diag = kc >= 4 * qt
            first = (kc == 0)
            if diag:
                lmw = [ptw]
                if first:
                    lmw.append((v_dn, lfree.get((h, qt - 2), 0)))
                sch.op("tensor", lambda e, pt=pt, off=off, n=n, b=ps_l[sb], first=first, last=(kc == nkc - 1): e.matmul(
                    b[:, off:off + n], onesk[:], PTb[pt][:, off:off + n], start=first, stop=last),
                    waits=lmw)
            else:
                sub = kc % 4
                if sub == 0:
                    sch.op("vector", lambda e, pt=pt: e.tensor_copy(out=PTs[:], in_=PTb[pt][:]),
                           waits=[ptw, (t_dl, t_dl.n), (v_dl, v_dl.n)], inc=v_dl)
                else:
                    sch.op("vector", lambda e, pt=pt: e.tensor_add(out=PTs[:], in0=PTs[:], in1=PTb[pt][:]),
                           waits=[ptw, (v_dl, v_dl.n)], inc=v_dl)
                add_done[gi] = v_dl.n
                if sub == 3:
                    lmw = [(v_dl, v_dl.n)]
                    if kc == 3:
                        lmw.append((v_dn, lfree.get((h, qt - 2), 0)))
                    sch.op("tensor", lambda e, b=ps_l[sb], first=(kc == 3): e.matmul(
                        b[:], onesk[:], PTs[:], start=first, stop=False),
                        waits=lmw, inc=t_dl)
            avw = [ptw]
            if first:
                avw.append((s_dn, numfree.get((h, qt - 2), 0)))
            sch.op("tensor", lambda e, pt=pt, off=off, n=n, kc=kc, b=ps_nm[sb], first=first, last=(kc == nkc - 1): e.matmul(
                b[:, off:off + n], Vtk[:, kc * 128:(kc + 1) * 128], PTb[pt][:, off:off + n],
                start=first, stop=last),
                waits=avw, inc=t_da)
            av_done[gi] = t_da.n
            if kc == nkc - 1:
                emit_epilogue(qt)

        def emit_epilogue(qt):
            sb = qt % 2
            mmw = (t_da, t_da.n)
            qt_last_s = ginfo[gi_last_s[qt]][6]
            sch.op("vector", lambda e, b=ps_l[sb]: e.reciprocal(out=rrow[:], in_=b[:]),
                   waits=[mmw, (t_dl, t_dl.n), (t_dr, t_dr.n), (v_dn, v_dn.n)], inc=v_dn)
            lfree[(h, qt)] = v_dn.n
            sch.op("tensor", lambda e: e.matmul(ps_r[:], onesr_f[:], rrow[:], start=True, stop=True),
                   waits=[(v_dn, v_dn.n)], inc=t_dr)
            sch.op("scalar", lambda e, b=ps_nm[sb]: e.activation(out=numbf[:], in_=b[:], func=AF.Identity),
                   waits=[mmw, (v_dn, v_dn.n)], inc=s_dn)
            numfree[(h, qt)] = s_dn.n
            sch.op("vector", lambda e, qt=qt: e.tensor_tensor(
                out=QTn[h][:, qt * 512:(qt + 1) * 512], in0=numbf[:], in1=ps_r[:], op=ALU.mult),
                waits=[(s_dn, s_dn.n), (t_ds, qt_last_s), (t_dr, t_dr.n)], inc=v_dn)

        gi_last_s = {}
        NG = len(chunks)
        for gi in range(NG):
            emit_score(gi)
            qt, kc = chunks[gi]
            if kc == 4 * (qt + 1) - 1:
                gi_last_s[qt] = gi
            if gi >= 1:
                emit_consume(gi - 1)
        emit_consume(NG - 1)
        stD.close()

    prev = []
    for h in range(HPC):
        br = phase_C(h, prev)
        if h == HPC - 1:
            pump(len(qunits))
            emit_quant_collectives()
        phase_D(h, br)
        prev = [(v_dn, v_dn.n), (s_dn, s_dn.n), (t_dr, t_dr.n)]
        if DBG:
            sch.op("sync", lambda e, h=h: e.dma_start(out=d_num[h], in_=QTn[h][:]),
                   waits=[(v_dn, v_dn.n)], inc=m_tap, inc_n=16)
            prev = prev + [(m_tap, m_tap.n)]

    C_WGU, C_WPR = QUANT_COLLECTIVES
    B_CO = (m_cw[0].n, m_cw[1].n)
    stB.close()

    stE = ExitStack()
    wo_sb = stE.enter_context(nc.sbuf_tensor("wo_sb", [128, HPC * D], BF16))
    att_st = [stE.enter_context(nc.sbuf_tensor(f"attst{i}", [128, D], BF16)) for i in range(2)]
    psE = ExitStack()
    ps_o = [psE.enter_context(nc.psum_tensor(f"pso{i}", [128, 512], F32)) for i in range(8)]

    for hc in range(HPC):
        sch.op("gpsimd", lambda e, hc=hc: e.dma_start(out=wo_sb[:, hc * D:(hc + 1) * D],
                                                      in_=woT[hc * 128:(hc + 1) * 128, :]),
               inc=m_wo, inc_n=16)
    W_WO = m_wo.n
    efree = {}
    efree2 = {}
    edma = {}
    for qi in range(S // 128):
        bs = (qi % 2) * 4
        stg = att_st[qi % 2]
        for j in range(4):
            waits = [(v_a, 2), (m_wo, W_WO), (g_b, ONES_G),
                     (v_e, efree.get(qi - 2, 0)), (s_e, efree2.get(qi - 2, 0))]
            if qi == 0 and j == 0:
                waits += prev
            sch.op("tensor", lambda e, j=j, b=ps_o[bs + j]: e.matmul(
                b[:], onesr_b[:], bo8bf[:, j * 512:(j + 1) * 512], start=True, stop=False),
                waits=waits)
        for hh in range(HPC):
            for j in range(4):
                sch.op("tensor", lambda e, hh=hh, qi=qi, j=j, b=ps_o[bs + j]: e.matmul(
                    b[:], QTn[hh][:, qi * 128:(qi + 1) * 128],
                    wo_sb[:, hh * D + j * 512: hh * D + (j + 1) * 512],
                    start=False, stop=(hh == HPC - 1)),
                    inc=(t_e if (hh == HPC - 1 and j == 3) else None))
        mmw = (t_e, t_e.n)
        for j in range(4):
            if j < 2:
                sch.op("vector", lambda e, stg=stg, j=j, b=ps_o[bs + j]: e.tensor_copy(
                    out=stg[:, j * 512:(j + 1) * 512], in_=b[:]),
                    waits=[mmw, (m_cw[0], B_CO[0]), (m_cw[1], B_CO[1]), (m_ca[qi % 2], edma.get(qi - 2, 0))], inc=v_e)
            else:
                sch.op("scalar", lambda e, stg=stg, j=j, b=ps_o[bs + j]: e.activation(
                    out=stg[:, j * 512:(j + 1) * 512], in_=b[:], func=AF.Identity),
                    waits=[mmw, (m_cw[0], B_CO[0]), (m_cw[1], B_CO[1]), (m_ca[qi % 2], edma.get(qi - 2, 0))], inc=s_e)
        efree[qi] = v_e.n
        efree2[qi] = s_e.n
        mca = m_ca[qi % 2]
        sch.op("sync", lambda e, qi=qi, stg=stg: e.dma_start(
            out=c_att.ap()[qi * 128:(qi + 1) * 128, :], in_=stg[:]),
            waits=[(v_e, v_e.n), (s_e, s_e.n)], inc=mca, inc_n=16)
        edma[qi] = mca.n
    sch.op("gpsimd", lambda e: e.collective_compute("ReduceScatter", ALU.add, replica_groups=RG,
                                                    ins=[c_att.ap().opt()], outs=[att_rs.ap().opt()]),
           waits=[(m_ca[0], m_ca[0].n), (m_ca[1], m_ca[1].n)], inc=c_sem)
    sch.op("gpsimd", lambda e: e.collective_compute("AllReduce", ALU.add, replica_groups=RG,
                                                    ins=[c_bar.ap().opt()], outs=[c_bar2.ap().opt()]),
           inc=c_sem)
    C_RS = c_sem.n
    E_V, E_S, E_CO0, E_CO1 = v_e.n, s_e.n, m_ca[0].n, m_ca[1].n
    if DBG:
        sch.op("sync", lambda e: e.dma_start(out=d_catt, in_=c_att.ap()), waits=[(m_ca[0], m_ca[0].n), (m_ca[1], m_ca[1].n)], inc=m_tap, inc_n=16)
        sch.op("sync", lambda e: e.dma_start(out=d_rs, in_=att_rs.ap()), waits=[(c_sem, C_RS)], inc=m_tap, inc_n=16)
    psE.close()
    stE.close()
    stCsc.close()
    stCDE.close()

    stF = ExitStack()
    hT = stF.enter_context(nc.sbuf_tensor("hT", [128, 64 * 512], BF16))
    wgt = [stF.enter_context(nc.sbuf_tensor(f"wgt{i}", [128, 2048], BF16)) for i in range(4)]
    wut = [stF.enter_context(nc.sbuf_tensor(f"wut{i}", [128, 2048], BF16)) for i in range(4)]
    sgs = [stF.enter_context(nc.sbuf_tensor(f"sgs{i}", [128, 512], F32)) for i in range(2)]
    hti = [stF.enter_context(nc.sbuf_tensor(f"hti{i}", [128, 512], BF16)) for i in range(4)]
    hsq = stF.enter_context(nc.sbuf_tensor("hsq", [128, 512], BF16))
    wpt = [stF.enter_context(nc.sbuf_tensor(f"wpt{i}", [128, 2048], BF16)) for i in range(2)]
    fxb = [stF.enter_context(nc.sbuf_tensor(f"fxb{i}", [128, 1024], F32)) for i in range(2)]
    fab = [stF.enter_context(nc.sbuf_tensor(f"fab{i}", [128, 1024], BF16)) for i in range(2)]
    fob = [stF.enter_context(nc.sbuf_tensor(f"fob{i}", [128, 1024], F32)) for i in range(2)]

    sch.wait("vector", [(m_ca[0], E_CO0), (m_ca[1], E_CO1)])
    sch.wait("scalar", [(m_ca[0], E_CO0), (m_ca[1], E_CO1)])
    sch.wait("tensor", [(v_e, E_V), (s_e, E_S)])

    wslot = {}
    vffree = {}
    trev = {}
    rmsred = 0
    ppD = {}
    outD = {}
    for slab in range(2):
        stG = ExitStack()
        ps_g = [stG.enter_context(nc.psum_tensor(f"psg{slab}{i}", [128, 512], F32)) for i in range(4)]
        ps_u = [stG.enter_context(nc.psum_tensor(f"psu{slab}{i}", [128, 512], F32)) for i in range(4)]
        for g in range(16):
            lwq = {}
            for q in range(4):
                qs = q
                rg_, cg = (g * 512) // 2048, (g * 512) % 2048
                ru, cu = ((g + 16) * 512) // 2048, ((g + 16) * 512) % 2048
                waits = [(t_f, wslot.get(qs, 0))]
                if slab == 0 and g == 0 and q == 0:
                    waits.append((c_sem, C_WGU))
                mg1, mg2 = m_wgu[qs], m_wgu[4 + qs]
                srcg = wgu_ag.ap()[rg_ * D + q * 512: rg_ * D + (q + 1) * 512,
                                   cg:cg + 512].rearrange("(a p) n -> p a n", p=128)
                srcu = wgu_ag.ap()[ru * D + q * 512: ru * D + (q + 1) * 512,
                                   cu:cu + 512].rearrange("(a p) n -> p a n", p=128)
                dstg = wgt[qs].ap().rearrange("p (a n) -> p a n", n=512)
                dstu = wut[qs].ap().rearrange("p (a n) -> p a n", n=512)
                sch.op("sync", lambda e, dstg=dstg, srcg=srcg: e.dma_start(out=dstg, in_=srcg),
                       waits=waits, inc=mg1, inc_n=16)
                sch.op("sync", lambda e, dstu=dstu, srcu=srcu: e.dma_start(out=dstu, in_=srcu),
                       waits=[(t_f, wslot.get(qs, 0))], inc=mg2, inc_n=16)
                lwq[q] = [(mg1, mg1.n), (mg2, mg2.n)]
            for dc in range(16):
                qs = dc // 4
                co = (dc % 4) * 512
                for tc in range(4):
                    lhsT = xnT[:, dc * 1024 + (slab * 4 + tc) * 128: dc * 1024 + (slab * 4 + tc) * 128 + 128]
                    waits = list(lwq[dc // 4])
                    if dc == 0:
                        waits.append((v_f, trev.get((slab, g - 1), 0) if g > 0 else (trev.get((slab - 1, 15), 0))))
                        waits.append((v_ae, v_ae.n))
                    sch.op("tensor", lambda e, b=ps_g[tc], lhsT=lhsT, qs=qs, co=co, dc=dc: e.matmul(
                        b[:], lhsT, wgt[qs][:, co:co + 512], start=(dc == 0), stop=(dc == 15)), waits=waits)
                    sch.op("tensor", lambda e, b=ps_u[tc], lhsT=lhsT, qs=qs, co=co, dc=dc: e.matmul(
                        b[:], lhsT, wut[qs][:, co:co + 512], start=(dc == 0), stop=(dc == 15)),
                        inc=(t_f if tc == 3 else None))
                if dc % 4 == 3:
                    wslot[qs] = t_f.n
            mmw = (t_f, t_f.n)
            for tc in range(4):
                sg = sgs[tc % 2]
                ht = hti[tc % 4]
                sch.op("scalar", lambda e, sg=sg, b=ps_g[tc]: e.activation(out=sg[:], in_=b[:], func=AF.Silu),
                       waits=[mmw, (v_f, v_f.n)], inc=s_f)
                sch.op("vector", lambda e, sg=sg, ht=ht, b=ps_u[tc]: e.tensor_tensor(
                    out=ht[:], in0=sg[:], in1=b[:], op=ALU.mult),
                    waits=[(s_f, s_f.n), (t_ft, t_ft.n)], inc=v_f)
                vffree[(slab, g, tc)] = v_f.n
                htw = (v_f, v_f.n)
                sch.op("scalar", lambda e, ht=ht, g=g, tc=tc: e.activation(
                    out=hsq[:], in_=ht[:], func=AF.Square, accum_out=rmsp[:, tc * 16 + g:tc * 16 + g + 1]),
                    waits=[htw, (v_f, rmsred)], inc=s_f)
                for sub in range(4):
                    pst = ps_u[tc].bitcast(BF16)[:, sub * 128:(sub + 1) * 128]
                    sch.op("tensor", lambda e, ht=ht, sub=sub, pst=pst: e.transpose(
                        pst, ht[:, sub * 128:(sub + 1) * 128], ident[:]),
                        waits=[htw, (s_f, s_f.n), (v_f, v_f.n)], inc=t_ft)
                dst = bass.AP(hT, (g * 4) * 512 + tc * 128, [[64 * 512, 128], [512, 4], [1, 128]])
                srcv = ps_u[tc].bitcast(BF16)[:, 0:512].rearrange("p (a n) -> p a n", n=128)
                sch.op("vector", lambda e, dst=dst, srcv=srcv: e.tensor_copy(out=dst, in_=srcv),
                       waits=[(t_ft, t_ft.n)], inc=v_f)
            trev[(slab, g)] = v_f.n
        for tc in range(4):
            sch.op("vector", lambda e, tc=tc: e.reduce_sum(out=lnst[:, 0:1], in_=rmsp[:, tc * 16:(tc + 1) * 16], axis=AX),
                   waits=[(s_f, s_f.n), (v_f, v_f.n)], inc=v_f)
            sch.op("vector", lambda e: e.tensor_scalar(out=lnst[:, 2:3], in0=lnst[:, 0:1],
                                                       scalar1=1.0 / HID, scalar2=RMS_EPS,
                                                       op0=ALU.mult, op1=ALU.add),
                   waits=[(v_f, v_f.n)], inc=v_f)
            sch.op("scalar", lambda e: e.activation(out=lnst[:, 1:2], in_=lnst[:, 2:3], func=AF.Sqrt),
                   waits=[(v_f, v_f.n), (s_f, s_f.n)], inc=s_f)
            sch.op("vector", lambda e, slab=slab, tc=tc: e.reciprocal(
                out=rstd8[:, slab * 4 + tc:slab * 4 + tc + 1], in_=lnst[:, 1:2]),
                waits=[(s_f, s_f.n)], inc=v_f)
        rmsred = v_f.n
        if DBG:
            sch.op("sync", lambda e, slab=slab: e.dma_start(out=d_ht[slab], in_=hT[:]),
                   waits=[(v_f, v_f.n)], inc=m_tap, inc_n=16)
            sch.op("sync", lambda e: e.dma_start(out=d_rstd, in_=rstd8[:]),
                   waits=[(v_f, v_f.n)], inc=m_tap, inc_n=16)
        stG.close()

        stP = ExitStack()
        ps_p = [stP.enter_context(nc.psum_tensor(f"psp{slab}{i}", [128, 512], F32)) for i in range(8)]
        for tp in range(2):
            tcs = (2 * tp, 2 * tp + 1)
            for hc in range(64):
                slot = hc % 2
                waits = [(t_p, ppD.get(slot, 0))]
                if slab == 0 and tp == 0 and hc < 2:
                    waits.append((c_sem, C_WPR))
                mps = m_wp[slot]
                srcap = wpr_ag.ap().rearrange("(r q) j -> q r j", r=NCORES)[hc * 128:(hc + 1) * 128, :, :]
                dstap = wpt[slot].ap().rearrange("p (r j) -> p r j", j=256)
                sch.op("sync", lambda e, dstap=dstap, srcap=srcap: e.dma_start(out=dstap, in_=srcap),
                       waits=waits, inc=mps, inc_n=16)
                lw = (mps, mps.n)
                for k, tc in enumerate(tcs):
                    lhsT = hT[:, hc * 512 + tc * 128: hc * 512 + (tc + 1) * 128]
                    for j in range(4):
                        waits2 = [lw]
                        if hc == 0 and j == 0:
                            waits2.append((v_f, v_f.n))
                            waits2.append((v_p, v_p.n))
                        sch.op("tensor", lambda e, b=ps_p[k * 4 + j], lhsT=lhsT, s=slot, j=j, hc=hc: e.matmul(
                            b[:], lhsT, wpt[s][:, j * 512:(j + 1) * 512], start=(hc == 0), stop=(hc == 63)),
                            waits=waits2, inc=(t_p if (k == 1 and j == 3) else None))
                ppD[slot] = t_p.n
            mmw = (t_p, t_p.n)
            for k, tc in enumerate(tcs):
                stc = slab * 4 + tc
                row = slab * 512 + tc * 128
                for half in range(2):
                    mfa, mfx_ = m_fx[k], m_fx[2 + k]
                    mo = m_out4[k * 2 + half]
                    mo2 = m_out4[k * 2 + (1 - half)]
                    sch.op("sync", lambda e, k=k, row=row, half=half: e.dma_start(
                        out=fab[k][:], in_=att_rs.ap()[row:row + 128, half * 1024:(half + 1) * 1024]),
                        waits=[(c_sem, C_RS), (v_p, v_p.n), (mo, outD.get((k, half), 0)),
                               (mo2, outD.get((k, 1 - half), 0))],
                        inc=mfa, inc_n=16)
                    sch.op("sync", lambda e, k=k, row=row, half=half: e.dma_start(
                        out=fxb[k][:], in_=x_in[row:row + 128, half * 1024:(half + 1) * 1024]),
                        inc=mfx_, inc_n=16)
                    fxw = (mfa, mfa.n)
                    fxw2 = (mfx_, mfx_.n)
                    for jj in range(2):
                        j = half * 2 + jj
                        sch.op("vector", lambda e, k=k, j=j, jj=jj, stc=stc, b=ps_p[k * 4 + j]: e.scalar_tensor_tensor(
                            out=fob[k][:, jj * 512:(jj + 1) * 512], in0=b[:], scalar=rstd8[:, stc:stc + 1],
                            in1=fab[k][:, jj * 512:(jj + 1) * 512], op0=ALU.mult, op1=ALU.add),
                            waits=[mmw, fxw, (v_f, v_f.n)], inc=v_p)
                        sch.op("vector", lambda e, k=k, jj=jj: e.tensor_add(
                            out=fob[k][:, jj * 512:(jj + 1) * 512], in0=fob[k][:, jj * 512:(jj + 1) * 512],
                            in1=fxb[k][:, jj * 512:(jj + 1) * 512]),
                            waits=[(v_p, v_p.n), fxw2], inc=v_p)
                    sch.op("sync", lambda e, k=k, row=row, half=half: e.dma_start(
                        out=out_t[row:row + 128, half * 1024:(half + 1) * 1024], in_=fob[k][:]),
                        waits=[(v_p, v_p.n)], inc=mo, inc_n=16)
                    outD[(k, half)] = mo.n
        stP.close()
    stF.close()

    sch.wait("sync", [(m, m.n) for m in m_out4])
    if DBG:
        sch.wait("sync", [(m_tap, m_tap.n)])
    sch.wait("gpsimd", [(c_sem, c_sem.n)])

    with nc.Block() as blk:
        blk.sync(lambda e: [f(e) for f in sch.streams["sync"]])
        blk.tensor(lambda e: [f(e) for f in sch.streams["tensor"]])
        blk.vector(lambda e: [f(e) for f in sch.streams["vector"]])
        blk.scalar(lambda e: [f(e) for f in sch.streams["scalar"]])
        blk.gpsimd(lambda e: [f(e) for f in sch.streams["gpsimd"]])
    big.close()
    return nc


def kernel(L0_attn_X, L0_attn_Wq, L0_attn_bq, L0_attn_Wk, L0_attn_bk, L0_attn_Wv,
           L0_attn_bv, L0_attn_Wo, L0_attn_bo, L0_ffn_w_gate_up, L0_ffn_w_proj):
    X = np.ascontiguousarray(np.asarray(L0_attn_X, dtype=np.float32).reshape(S, D))
    WqT = np.ascontiguousarray(np.asarray(L0_attn_Wq, np.float32).T)
    WkT = np.ascontiguousarray(np.asarray(L0_attn_Wk, np.float32).T)
    WvT = np.ascontiguousarray(np.asarray(L0_attn_Wv, np.float32).T)
    WoT = np.ascontiguousarray(np.asarray(L0_attn_Wo, np.float32).T)
    bq = np.asarray(L0_attn_bq, np.float32)
    bk = np.asarray(L0_attn_bk, np.float32)
    bv = np.asarray(L0_attn_bv, np.float32)
    bo = np.asarray(L0_attn_bo, np.float32).reshape(1, D)
    wgu = np.asarray(L0_ffn_w_gate_up, np.float32)
    wpr = np.asarray(L0_ffn_w_proj, np.float32)
    ident = np.eye(128, dtype=np.float32)
    triu = np.triu(np.ones((128, 128), dtype=np.float32))

    in_maps = []
    for c in range(NCORES):
        j0 = c * HPC * DH
        b6_c = np.stack([bq[j0:j0 + DH], bq[j0 + DH:j0 + 2 * DH],
                         bk[j0:j0 + DH], bk[j0 + DH:j0 + 2 * DH],
                         bv[j0:j0 + DH], bv[j0 + DH:j0 + 2 * DH]], axis=0)
        in_maps.append({
            "x": np.ascontiguousarray(X[c * T:(c + 1) * T]),
            "wqT": np.ascontiguousarray(WqT[:, j0:j0 + HPC * DH]),
            "wkT": np.ascontiguousarray(WkT[:, j0:j0 + HPC * DH]),
            "wvT": np.ascontiguousarray(WvT[:, j0:j0 + HPC * DH]),
            "woT": np.ascontiguousarray(WoT[j0:j0 + HPC * DH, :]),
            "b6": np.ascontiguousarray(b6_c),
            "bo": bo,
            "wgu": np.ascontiguousarray(wgu[c * (2 * HID // NCORES):(c + 1) * (2 * HID // NCORES)]),
            "wpr": np.ascontiguousarray(wpr[c * (D // NCORES):(c + 1) * (D // NCORES)]),
            "identin": ident,
            "triuin": triu,
        })

    nc = build()
    res = run_bass_kernel_spmd(nc, in_maps, core_ids=list(range(NCORES)))
    out = np.concatenate([res.results[c]["out"] for c in range(NCORES)], axis=0)
    return out.reshape(1, S, D)


# version 18 (final)
# speedup vs baseline: 1.1427x; 1.1249x over previous
from contextlib import ExitStack

import numpy as np
import concourse.bass as bass
import concourse.mybir as mybir
from concourse.bass_utils import run_bass_kernel_spmd

F32 = mybir.dt.float32
BF16 = mybir.dt.bfloat16
AF = mybir.ActivationFunctionType
ALU = mybir.AluOpType
AX = mybir.AxisListType.X

NCORES = 8
S = 8192
D = 2048
NH = 16
DH = 128
HID = 8192
T = S // NCORES
HPC = NH // NCORES
S4 = float(np.float32(1.0 / np.sqrt(np.sqrt(float(DH)))))
LN_EPS = 1e-5
RMS_EPS = float(np.finfo(np.float32).eps)

ENGS = ("sync", "tensor", "vector", "scalar", "gpsimd")


class Sem:
    __slots__ = ("h", "n", "name")

    def __init__(self, h, name):
        self.h = h
        self.n = 0
        self.name = name


class Sched:
    def __init__(self, nc):
        self.nc = nc
        self.streams = {e: [] for e in ENGS}

    def sem(self, name):
        return Sem(self.nc.alloc_semaphore(name=name), name)

    def op(self, eng, build, waits=(), inc=None, inc_n=1):
        wl = []
        for w in waits:
            if isinstance(w, tuple):
                s, v = w
            else:
                s, v = w, w.n
            if v > 0:
                wl.append((s.h, v))
        il = None
        if inc is not None:
            inc.n += inc_n
            il = (inc.h, inc_n)

        def thunk(e, build=build, wl=wl, il=il):
            for h, v in wl:
                e.wait_ge(h, v)
            r = build(e)
            if il is not None:
                r.then_inc(il[0], il[1])

        self.streams[eng].append(thunk)

    def wait(self, eng, waits):
        wl = []
        for w in waits:
            if isinstance(w, tuple):
                s, v = w
            else:
                s, v = w, w.n
            if v > 0:
                wl.append((s.h, v))

        def thunk(e, wl=wl):
            for h, v in wl:
                e.wait_ge(h, v)

        self.streams[eng].append(thunk)


DBG = False


def build():
    nc = bass.Bass("TRN2", target_bir_lowering=False)

    x_in = nc.dram_tensor("x", [T, D], F32, kind="ExternalInput").ap()
    wqT = nc.dram_tensor("wqT", [D, HPC * DH], F32, kind="ExternalInput").ap()
    wkT = nc.dram_tensor("wkT", [D, HPC * DH], F32, kind="ExternalInput").ap()
    wvT = nc.dram_tensor("wvT", [D, HPC * DH], F32, kind="ExternalInput").ap()
    woT = nc.dram_tensor("woT", [HPC * DH, D], F32, kind="ExternalInput").ap()
    b6 = nc.dram_tensor("b6", [6, DH], F32, kind="ExternalInput").ap()
    bo_in = nc.dram_tensor("bo", [1, D], F32, kind="ExternalInput").ap()
    wgu_in = nc.dram_tensor("wgu", [2 * HID // NCORES, D], F32, kind="ExternalInput").ap()
    wpr_in = nc.dram_tensor("wpr", [D // NCORES, HID], F32, kind="ExternalInput").ap()
    ident_in = nc.dram_tensor("identin", [128, 128], F32, kind="ExternalInput").ap()
    triu_in = nc.dram_tensor("triuin", [128, 128], F32, kind="ExternalInput").ap()
    out_t = nc.dram_tensor("out", [T, D], F32, kind="ExternalOutput").ap()
    if DBG:
        d_xnag = nc.dram_tensor("d_xnag", [NCORES * D, T], BF16, kind="ExternalOutput").ap()
        d_qt = [nc.dram_tensor(f"d_qt{h}", [128, S], BF16, kind="ExternalOutput").ap() for h in range(2)]
        d_kt = [nc.dram_tensor(f"d_kt{h}", [128, S], BF16, kind="ExternalOutput").ap() for h in range(2)]
        d_vt = [nc.dram_tensor(f"d_vt{h}", [128, S], BF16, kind="ExternalOutput").ap() for h in range(2)]
        d_num = [nc.dram_tensor(f"d_num{h}", [128, S], BF16, kind="ExternalOutput").ap() for h in range(2)]
        d_negb = nc.dram_tensor("d_negb", [128, 2], F32, kind="ExternalOutput").ap()
        d_catt = nc.dram_tensor("d_catt", [S, D], BF16, kind="ExternalOutput").ap()
        d_rs = nc.dram_tensor("d_rs", [T, D], BF16, kind="ExternalOutput").ap()
        d_wgu = nc.dram_tensor("d_wgu", [NCORES * D, 2 * HID // NCORES], BF16, kind="ExternalOutput").ap()
        d_wpr = nc.dram_tensor("d_wpr", [NCORES * HID, D // NCORES], BF16, kind="ExternalOutput").ap()
        d_ht = [nc.dram_tensor(f"d_ht{i}", [128, 64 * 512], BF16, kind="ExternalOutput").ap() for i in range(2)]
        d_rstd = nc.dram_tensor("d_rstd", [128, 8], F32, kind="ExternalOutput").ap()
        m_tap = None

    c_xn = nc.dram_tensor("c_xn", [D, T], BF16, kind="Internal")
    xn_ag = nc.dram_tensor("xn_ag", [NCORES * D, T], BF16, kind="Internal", addr_space="Shared")
    c_wgu = nc.dram_tensor("c_wgu", [D, 2 * HID // NCORES], BF16, kind="Internal")
    wgu_ag = nc.dram_tensor("wgu_ag", [NCORES * D, 2 * HID // NCORES], BF16, kind="Internal", addr_space="Shared")
    c_wpr = nc.dram_tensor("c_wpr", [HID, D // NCORES], BF16, kind="Internal")
    wpr_ag = nc.dram_tensor("wpr_ag", [NCORES * HID, D // NCORES], BF16, kind="Internal", addr_space="Shared")
    c_att = nc.dram_tensor("c_att", [S, D], BF16, kind="Internal")
    att_rs = nc.dram_tensor("att_rs", [T, D], BF16, kind="Internal")
    c_bar = nc.dram_tensor("c_bar", [1, 64], F32, kind="Internal")
    c_bar2 = nc.dram_tensor("c_bar2", [1, 64], F32, kind="Internal")

    RG = [list(range(NCORES))]
    sch = Sched(nc)

    m_x = [sch.sem(f"m_x{i}") for i in range(2)]
    m_b = sch.sem("m_b")
    m_id = sch.sem("m_id")
    m_w = sch.sem("m_w")
    m_wo = sch.sem("m_wo")
    m_q = [sch.sem(f"m_q{i}") for i in range(2)]
    m_xga = [sch.sem(f"m_xga{i}") for i in range(12)]
    m_wgu = [sch.sem(f"m_wgu{i}") for i in range(16)]
    m_wp = [sch.sem(f"m_wp{i}") for i in range(2)]
    m_fx = [sch.sem(f"m_fx{i}") for i in range(4)]
    m_cxn = sch.sem("m_cxn")
    m_cb = sch.sem("m_cb")
    m_cw = [sch.sem(f"m_cw{i}") for i in range(2)]
    m_ca = [sch.sem(f"m_ca{i}") for i in range(2)]
    m_out4 = [sch.sem(f"m_out{i}") for i in range(4)]
    c_sem = sch.sem("c_sem")
    m_tap = sch.sem("m_tap")

    v_a = sch.sem("v_a")
    s_a = sch.sem("s_a")
    t_a = sch.sem("t_a")
    v_ae = sch.sem("v_ae")

    v_b = sch.sem("v_b")
    g_b = sch.sem("g_b")
    t_b = sch.sem("t_b")
    v_be = sch.sem("v_be")

    t_c = sch.sem("t_c")
    s_c = sch.sem("s_c")
    t_cn = sch.sem("t_cn")
    v_c = sch.sem("v_c")

    s_d = sch.sem("s_d")
    t_ds = sch.sem("t_ds")
    t_da = sch.sem("t_da")
    v_dm = sch.sem("v_dm")
    s_dn = sch.sem("s_dn")
    v_dl = sch.sem("v_dl")
    t_dl = sch.sem("t_dl")
    v_dn = sch.sem("v_dn")
    t_dr = sch.sem("t_dr")

    t_e = sch.sem("t_e")
    v_e = sch.sem("v_e")
    s_e = sch.sem("s_e")

    t_f = sch.sem("t_f")
    s_f = sch.sem("s_f")
    v_f = sch.sem("v_f")
    t_ft = sch.sem("t_ft")
    t_p = sch.sem("t_p")
    v_p = sch.sem("v_p")

    big = ExitStack()
    ident = big.enter_context(nc.sbuf_tensor("ident", [128, 128], BF16))
    triu = big.enter_context(nc.sbuf_tensor("triu", [128, 128], BF16))
    onesk = big.enter_context(nc.sbuf_tensor("onesk", [128, 1], BF16))
    onesr_f = big.enter_context(nc.sbuf_tensor("onesr_f", [1, 128], F32))
    onesr_b = big.enter_context(nc.sbuf_tensor("onesr_b", [1, 128], BF16))
    bvec = big.enter_context(nc.sbuf_tensor("bvec", [128, 8], F32))
    bo8bf = big.enter_context(nc.sbuf_tensor("bo8bf", [1, D], BF16))
    negBv = big.enter_context(nc.sbuf_tensor("negBv", [128, 2], F32))
    nrm = big.enter_context(nc.sbuf_tensor("nrm", [1, 8], F32))
    nrm2 = big.enter_context(nc.sbuf_tensor("nrm2", [1, 8], F32))
    rrow = big.enter_context(nc.sbuf_tensor("rrow", [1, 512], F32))
    rstd8 = big.enter_context(nc.sbuf_tensor("rstd8", [128, 8], F32))
    rmsp = big.enter_context(nc.sbuf_tensor("rmsp", [128, 64], F32))
    lnst = big.enter_context(nc.sbuf_tensor("lnst", [128, 8], F32))
    xnT = big.enter_context(nc.sbuf_tensor("xnT", [128, 16 * 1024], BF16))

    stCDE = ExitStack()
    wq_sb = stCDE.enter_context(nc.sbuf_tensor("wq_sb", [128, 2048], BF16))
    wk_sb = stCDE.enter_context(nc.sbuf_tensor("wk_sb", [128, 2048], BF16))
    wv_sb = stCDE.enter_context(nc.sbuf_tensor("wv_sb", [128, 2048], BF16))
    QTn = [stCDE.enter_context(nc.sbuf_tensor(f"QTn{h}", [128, S], BF16)) for h in range(HPC)]
    KT = stCDE.enter_context(nc.sbuf_tensor("KT", [128, S], BF16))
    Vtk = stCDE.enter_context(nc.sbuf_tensor("Vtk", [128, S], BF16))
    xgab = [stCDE.enter_context(nc.sbuf_tensor(f"xgab{i}", [128, 8192], BF16)) for i in range(3)]

    sch.op("gpsimd", lambda e: e.dma_start(out=ident[:], in_=ident_in), inc=m_id, inc_n=16)
    sch.op("gpsimd", lambda e: e.dma_start(out=triu[:], in_=triu_in), inc=m_id, inc_n=16)
    IDENT_W = 32
    sch.op("gpsimd", lambda e: e.memset(onesk[:], 1.0), inc=g_b)
    sch.op("gpsimd", lambda e: e.memset(onesr_f[:], 1.0), inc=g_b)
    sch.op("gpsimd", lambda e: e.memset(onesr_b[:], 1.0), inc=g_b)
    sch.op("gpsimd", lambda e: e.memset(nrm2[:], 0.0), inc=g_b)
    sch.op("gpsimd", lambda e: e.memset(rrow[:, 0:64], 0.0), inc=g_b)
    ONES_G = g_b.n
    sch.op("gpsimd", lambda e: e.dma_start(out=c_bar.ap(), in_=rrow[:, 0:64]), waits=[(g_b, ONES_G)], inc=m_cb, inc_n=16)


    stA = ExitStack()
    xld = [stA.enter_context(nc.sbuf_tensor(f"xld{i}", [128, D], F32)) for i in range(2)]
    xnbuf = stA.enter_context(nc.sbuf_tensor("xnbuf", [128, D], BF16))
    psA = ExitStack()
    ps_tra = psA.enter_context(nc.psum_tensor("ps_tra", [128, 1024], BF16))
    ps_bias = psA.enter_context(nc.psum_tensor("ps_bias", [128, 8], F32))

    b6row = xld[0]
    borow = xld[1]
    sch.op("sync", lambda e: e.dma_start(out=b6row[0:1, 0:768].bitcast(F32) if False else b6row[0:1, 0:768], in_=b6.rearrange("a b -> (a b)")[None, :]), inc=m_b, inc_n=16)
    sch.op("sync", lambda e: e.dma_start(out=borow[0:1, :], in_=bo_in), inc=m_b, inc_n=16)
    sch.op("vector", lambda e: e.tensor_scalar_mul(out=bo8bf[:], in0=borow[0:1, :], scalar1=0.125),
           waits=[(m_b, 32)], inc=v_a)
    for i in range(6):
        sch.op("tensor",
               lambda e, i=i: e.matmul(ps_bias[:, i:i + 1], b6row[0:1, i * 128:(i + 1) * 128],
                                       onesr_f[:, 0:1], start=True, stop=True),
               waits=[(m_b, 32), (g_b, ONES_G)], inc=t_a)
    sch.op("vector", lambda e: e.tensor_scalar(out=bvec[:, 0:4], in0=ps_bias[:, 0:4],
                                               scalar1=S4, scalar2=None, op0=ALU.mult),
           waits=[(t_a, 6)], inc=v_a)
    sch.op("vector", lambda e: e.tensor_copy(out=bvec[:, 4:6], in_=ps_bias[:, 4:6]),
           waits=[(t_a, 6)], inc=v_a)
    BVEC_V = v_a.n

    NT = T // 128
    xn_done = [0, 0]
    for ti in range(NT):
        buf = xld[ti % 2]
        mxs = m_x[ti % 2]
        bgate = []
        if ti == 0:
            bgate = [(t_a, 6)]
        elif ti == 1:
            bgate = [(v_a, 1)]
        sch.op("sync", lambda e, ti=ti, buf=buf: e.dma_start(out=buf[:], in_=x_in[ti * 128:(ti + 1) * 128, :]),
               waits=[(v_a, xn_done[ti % 2])] + bgate, inc=mxs, inc_n=16)
        xw = (mxs, mxs.n)
        sch.op("vector", lambda e, buf=buf: e.reduce_sum(out=lnst[:, 0:1], in_=buf[:], axis=AX),
               waits=[xw, (s_a, s_a.n)], inc=v_a)
        sch.op("scalar", lambda e, buf=buf: e.activation(out=xnbuf[:], in_=buf[:], func=AF.Square,
                                                         accum_out=lnst[:, 1:2]),
               waits=[xw, (v_a, v_a.n), (t_a, t_a.n)], inc=s_a)
        sch.op("vector", lambda e: e.tensor_scalar_mul(out=lnst[:, 2:3], in0=lnst[:, 0:1], scalar1=1.0 / D),
               waits=[(v_a, v_a.n)], inc=v_a)
        sch.op("vector", lambda e: e.tensor_scalar(out=lnst[:, 3:4], in0=lnst[:, 2:3],
                                                   scalar1=lnst[:, 2:3], scalar2=-LN_EPS,
                                                   op0=ALU.mult, op1=ALU.add),
               waits=[(v_a, v_a.n)], inc=v_a)
        sch.op("vector", lambda e: e.scalar_tensor_tensor(out=lnst[:, 4:5], in0=lnst[:, 1:2],
                                                          scalar=1.0 / D, in1=lnst[:, 3:4],
                                                          op0=ALU.mult, op1=ALU.subtract),
               waits=[(v_a, v_a.n), (s_a, s_a.n)], inc=v_a)
        sch.op("scalar", lambda e: e.activation(out=lnst[:, 5:6], in_=lnst[:, 4:5], func=AF.Sqrt),
               waits=[(v_a, v_a.n)], inc=s_a)
        sch.op("vector", lambda e: e.reciprocal(out=lnst[:, 6:7], in_=lnst[:, 5:6]),
               waits=[(s_a, s_a.n)], inc=v_a)
        sch.op("vector", lambda e: e.scalar_tensor_tensor(out=lnst[:, 7:8], in0=lnst[:, 2:3],
                                                          scalar=-1.0, in1=lnst[:, 6:7],
                                                          op0=ALU.mult, op1=ALU.mult),
               waits=[(v_a, v_a.n)], inc=v_a)
        sch.op("vector", lambda e, buf=buf: e.tensor_scalar(out=xnbuf[:], in0=buf[:],
                                                            scalar1=lnst[:, 6:7], scalar2=lnst[:, 7:8],
                                                            op0=ALU.mult, op1=ALU.add),
               waits=[(v_a, v_a.n), (t_a, t_a.n), (s_a, s_a.n)], inc=v_a)
        xn_done[ti % 2] = v_a.n
        xnw = (v_a, v_a.n)
        for bq in range(2):
            for s8 in range(8):
                dc = bq * 8 + s8
                sch.op("tensor", lambda e, dc=dc, s8=s8: e.transpose(
                    ps_tra[:, s8 * 128:(s8 + 1) * 128], xnbuf[:, dc * 128:(dc + 1) * 128], ident[:]),
                    waits=[xnw, (m_id, IDENT_W), (v_ae, v_ae.n)], inc=t_a)
            dst = bass.AP(xnT, ti * 128 + bq * 8 * 1024, [[16 * 1024, 128], [1024, 8], [1, 128]])
            srcv = ps_tra.ap().rearrange("p (a n) -> p a n", n=128)
            sch.op("vector", lambda e, dst=dst, srcv=srcv: e.tensor_copy(out=dst, in_=srcv),
                   waits=[(t_a, t_a.n)], inc=v_ae)
    for dc in range(16):
        sch.op("sync", lambda e, dc=dc: e.dma_start(out=c_xn.ap()[dc * 128:(dc + 1) * 128, :],
                                                    in_=xnT[:, dc * 1024:(dc + 1) * 1024]),
               waits=[(v_ae, v_ae.n)], inc=m_cxn, inc_n=16)
    sch.op("gpsimd", lambda e: e.collective_compute("AllGather", ALU.bypass, replica_groups=RG,
                                                    ins=[c_xn.ap().opt()], outs=[xn_ag.ap().opt()]),
           waits=[(m_cxn, m_cxn.n)], inc=c_sem)
    sch.op("gpsimd", lambda e: e.collective_compute("AllReduce", ALU.add, replica_groups=RG,
                                                    ins=[c_bar.ap().opt()], outs=[c_bar2.ap().opt()]),
           waits=[(m_cb, 16)], inc=c_sem)
    C_XN = c_sem.n

    stB = ExitStack()
    wbf = [stB.enter_context(nc.sbuf_tensor(f"wbf{i}", [128, 512], BF16)) for i in range(2)]
    tqb = stB.enter_context(nc.sbuf_tensor("tqb", [128, 512], BF16))
    cbq = stB.enter_context(nc.sbuf_tensor("cbq", [128, 512], BF16))
    dbq = stB.enter_context(nc.sbuf_tensor("dbq", [128, 512], BF16))
    qbq = stB.enter_context(nc.sbuf_tensor("qbq", [128, 512], BF16))
    whb = stB.enter_context(nc.sbuf_tensor("whb", [128, 512], BF16))
    web = stB.enter_context(nc.sbuf_tensor("web", [128, 512], BF16))
    wfb = [stB.enter_context(nc.sbuf_tensor(f"wfb{i}", [128, 512], BF16)) for i in range(6)]
    gsum = stB.enter_context(nc.sbuf_tensor("gsum", [128, 8], F32))
    gscl = stB.enter_context(nc.sbuf_tensor("gscl", [128, 8], BF16))
    ginv = stB.enter_context(nc.sbuf_tensor("ginv", [128, 8], F32))
    wTs = [stB.enter_context(nc.sbuf_tensor(f"wTs{i}", [128, 512], BF16)) for i in range(2)]
    psB = ExitStack()
    ps_trb = psB.enter_context(nc.psum_tensor("ps_trb", [128, 1024], BF16))

    qunits = []
    for ti in range(16):
        for sc in range(4):
            qunits.append(("gu", ti, sc, wgu_in[ti * 128:(ti + 1) * 128, sc * 512:(sc + 1) * 512]))
    for ri in range(2):
        for sc in range(16):
            qunits.append(("pr", ri, sc, wpr_in[ri * 128:(ri + 1) * 128, sc * 512:(sc + 1) * 512]))

    WEFF = {}
    TRB = {}
    STG = {}
    qstate = {"ui": 0, "trb": ps_trb[:]}

    def emit_qunit():
        ui = qstate["ui"]
        if ui >= len(qunits):
            return False
        qstate["ui"] = ui + 1
        kind, i0, i1, srcap = qunits[ui]
        trb = qstate["trb"]
        wb = wbf[ui % 2]
        wf = wfb[ui % 6]
        war = [(v_b, WEFF.get(ui - 2, 0))]
        mqs = m_q[ui % 2]
        sch.op("gpsimd", lambda e, wb=wb, srcap=srcap: e.dma_start(out=wb[:], in_=srcap),
               waits=war, inc=mqs, inc_n=16)
        lw = (mqs, mqs.n)
        v3 = wb.ap().rearrange("p (g k) -> p g k", k=64)
        sch.op("vector", lambda e, v3=v3: e.reduce_sum(out=gsum[:], in_=v3, axis=AX, apply_absolute_value=True),
               waits=[lw, (v_b, v_b.n)], inc=v_b)
        sch.op("vector", lambda e: e.tensor_scalar(out=gscl[:], in0=gsum[:], scalar1=1.0 / 64,
                                                   scalar2=1e-8, op0=ALU.mult, op1=ALU.max),
               waits=[(v_b, v_b.n)], inc=v_b)
        sch.op("vector", lambda e: e.reciprocal(out=ginv[:], in_=gscl[:]), waits=[(v_b, v_b.n)], inc=v_b)
        ginv_b = bass.AP(ginv, 0, [[8, 128], [1, 8], [0, 64]])
        gscl_b = bass.AP(gscl, 0, [[8, 128], [1, 8], [0, 64]])
        tq3 = tqb.ap().rearrange("p (g k) -> p g k", k=64)
        sch.op("vector", lambda e, v3=v3, tq3=tq3, ginv_b=ginv_b: e.tensor_tensor(
            out=tq3, in0=v3, in1=ginv_b, op=ALU.mult),
            waits=[(v_b, v_b.n)], inc=v_b)
        sch.op("vector", lambda e: e.tensor_single_scalar(out=cbq[:], in_=tqb[:], scalar=0.5, op=ALU.is_gt),
               waits=[(v_b, v_b.n)], inc=v_b)
        sch.op("vector", lambda e: e.tensor_single_scalar(out=dbq[:], in_=tqb[:], scalar=-0.5, op=ALU.is_lt),
               waits=[(v_b, v_b.n)], inc=v_b)
        sch.op("vector", lambda e: e.tensor_sub(out=qbq[:], in0=cbq[:], in1=dbq[:]),
               waits=[(v_b, v_b.n)], inc=v_b)
        qb3 = qbq.ap().rearrange("p (g k) -> p g k", k=64)
        wh3 = whb.ap().rearrange("p (g k) -> p g k", k=64)
        sch.op("vector", lambda e, qb3=qb3, wh3=wh3, gscl_b=gscl_b: e.tensor_tensor(
            out=wh3, in0=qb3, in1=gscl_b, op=ALU.mult),
            waits=[(v_b, v_b.n)], inc=v_b)
        sch.op("vector", lambda e, wb=wb: e.tensor_sub(out=web[:], in0=whb[:], in1=wb[:]),
               waits=[(v_b, v_b.n)], inc=v_b)
        sch.op("vector", lambda e, wb=wb, wf=wf: e.tensor_add(out=wf[:], in0=wb[:], in1=web[:]),
               waits=[(v_b, v_b.n), (t_b, TRB.get(ui - 6, 0))], inc=v_b)
        WEFF[ui] = v_b.n
        ww = (v_b, v_b.n)

        def finish(ui=ui, ww=ww, wf=wf, kind=kind, i0=i0, i1=i1):
            trb = qstate["trb"]
            st = wTs[ui % 2]
            for dc in range(4):
                sch.op("tensor", lambda e, wf=wf, dc=dc, trb=trb: e.transpose(
                    trb[:, dc * 128:(dc + 1) * 128], wf[:, dc * 128:(dc + 1) * 128], ident[:]),
                    waits=[ww, (m_id, IDENT_W), (v_be, v_be.n), (v_c, v_c.n)], inc=t_b)
            sch.op("vector", lambda e, st=st, trb=trb: e.tensor_copy(out=st[:], in_=trb[:, 0:512]),
                   waits=[(t_b, t_b.n), (m_cw[ui % 2], STG.get(ui - 2, 0))], inc=v_be)
            TRB[ui] = t_b.n
            ew = (v_be, v_be.n)
            for dc in range(4):
                if kind == "gu":
                    dst = c_wgu.ap()[(i1 * 4 + dc) * 128:(i1 * 4 + dc + 1) * 128, i0 * 128:(i0 + 1) * 128]
                else:
                    dst = c_wpr.ap()[(i1 * 4 + dc) * 128:(i1 * 4 + dc + 1) * 128, i0 * 128:(i0 + 1) * 128]
                sch.op("gpsimd", lambda e, st=st, dc=dc, dst=dst: e.dma_start(
                    out=dst, in_=st[:, dc * 128:(dc + 1) * 128]),
                    waits=[ew], inc=m_cw[ui % 2], inc_n=16)
            STG[ui] = m_cw[ui % 2].n
        qfin.append(finish)
        return True

    qfin = []

    def pump(n):
        fins, qfin[:] = qfin[:], []
        for f in fins:
            f()
        for _ in range(n):
            if not emit_qunit():
                return

    for _ in range(4):
        pump(6)
    QUANT_COLLECTIVES = []

    def emit_quant_collectives():
        assert qstate["ui"] == len(qunits), qstate["ui"]
        fins, qfin[:] = qfin[:], []
        for f in fins:
            f()
        sch.op("gpsimd", lambda e: e.collective_compute("AllGather", ALU.bypass, replica_groups=RG,
                                                        ins=[c_wgu.ap().opt()], outs=[wgu_ag.ap().opt()]),
               waits=[(m_cw[0], m_cw[0].n), (m_cw[1], m_cw[1].n)], inc=c_sem)
        QUANT_COLLECTIVES.append(c_sem.n)
        sch.op("gpsimd", lambda e: e.collective_compute("AllGather", ALU.bypass, replica_groups=RG,
                                                        ins=[c_wpr.ap().opt()], outs=[wpr_ag.ap().opt()]),
               waits=[(m_cw[0], m_cw[0].n), (m_cw[1], m_cw[1].n)], inc=c_sem)
        sch.op("gpsimd", lambda e: e.collective_compute("AllReduce", ALU.add, replica_groups=RG,
                                                        ins=[c_bar.ap().opt()], outs=[c_bar2.ap().opt()]),
               inc=c_sem)
        QUANT_COLLECTIVES.append(c_sem.n)
        QUANT_COLLECTIVES[0] = c_sem.n
        if DBG:
            sch.op("sync", lambda e: e.dma_start(out=d_wgu, in_=wgu_ag.ap()),
                   waits=[(c_sem, QUANT_COLLECTIVES[0])], inc=m_tap, inc_n=16)
            sch.op("sync", lambda e: e.dma_start(out=d_wpr, in_=wpr_ag.ap()),
                   waits=[(c_sem, QUANT_COLLECTIVES[1])], inc=m_tap, inc_n=16)
    if DBG:
        sch.op("sync", lambda e: e.dma_start(out=d_xnag, in_=xn_ag.ap()), waits=[(c_sem, C_XN)], inc=m_tap, inc_n=16)
    psB.close()
    psA.close()
    stB.close()
    stA.close()
    stCsc = ExitStack()
    sqs = [stCsc.enter_context(nc.sbuf_tensor(f"sqs{i}", [128, 512], BF16)) for i in range(2)]
    vfs = [stCsc.enter_context(nc.sbuf_tensor(f"vfs{i}", [128, 512], BF16)) for i in range(2)]
    numbf = stCsc.enter_context(nc.sbuf_tensor("numbf", [128, 512], BF16))
    PTb = [stCsc.enter_context(nc.sbuf_tensor(f"PTb{i}", [128, 512], BF16)) for i in range(4)]
    PTs = stCsc.enter_context(nc.sbuf_tensor("PTs", [128, 512], BF16))

    xga_free = {}
    lfree = {}
    numfree = {}

    def phase_C(h, prev_d_done):
        stC = ExitStack()
        ps_q = [stC.enter_context(nc.psum_tensor(f"psq{h}{i}", [128, 512], F32)) for i in range(2)]
        ps_k = [stC.enter_context(nc.psum_tensor(f"psk{h}{i}", [128, 512], F32)) for i in range(2)]
        ps_v = [stC.enter_context(nc.psum_tensor(f"psv{h}{i}", [128, 512], F32)) for i in range(2)]
        ps_n = stC.enter_context(nc.psum_tensor(f"psn{h}", [1, 512], F32))
        ps_tv = stC.enter_context(nc.psum_tensor(f"pstv{h}", [128, 1024], BF16))
        qstate["trb"] = ps_tv[:]

        for (wsb, wsrc) in ((wq_sb, wqT), (wk_sb, wkT), (wv_sb, wvT)):
            for q in range(4):
                srcap = wsrc[q * 512:(q + 1) * 512, h * 128:(h + 1) * 128].rearrange("(a p) n -> p a n", p=128)
                dstap = wsb.ap()[:, q * 512:(q + 1) * 512].rearrange("p (a n) -> p a n", n=128)
                sch.op("gpsimd", lambda e, dstap=dstap, srcap=srcap: e.dma_start(out=dstap, in_=srcap),
                       waits=([] if h == 0 else [(t_c, t_c.n)]), inc=m_w, inc_n=16)
        W_ATTN = m_w.n
        sch.op("vector", lambda e: e.memset(nrm[:, 0:2], -3e38), waits=[(v_c, v_c.n)], inc=v_c)
        ev_prev = (0, 0, 0)
        vtr = {}
        for pr in range(8):
            tA = 2 * pr
            lwq = {}
            for q in range(4):
                for k in range(2):
                    tt = tA + k
                    buf = tt % 3
                    rank, tl = tt // 2, (tt % 2) * 512
                    slot = buf * 4 + q
                    waits = [(t_c, xga_free.get((buf, q), 0))]
                    if h == 0 and pr == 0 and q == 0:
                        waits.append((c_sem, C_XN))
                    mgs = m_xga[slot]
                    srcap = xn_ag.ap()[rank * D + q * 512: rank * D + (q + 1) * 512,
                                       tl:tl + 512].rearrange("(a p) n -> p a n", p=128)
                    dstap = xgab[buf][:, q * 2048:(q + 1) * 2048].rearrange("p (a n) -> p a n", n=512)
                    sch.op("sync", lambda e, dstap=dstap, srcap=srcap: e.dma_start(out=dstap, in_=srcap),
                           waits=waits, inc=mgs, inc_n=16)
                    lwq[(k, q)] = (mgs, mgs.n)
            for dc in range(16):
                for mi, (wsb, ps) in enumerate(((wq_sb, ps_q), (wk_sb, ps_k), (wv_sb, ps_v))):
                    lhsT = wsb[:, dc * 128:(dc + 1) * 128]
                    for k in range(2):
                        buf = (tA + k) % 3
                        waits = [lwq[(k, dc // 4)]]
                        if dc == 0:
                            waits.append((m_w, W_ATTN))
                            waits.append((s_c, ev_prev[mi]))
                            if pr == 0:
                                waits += prev_d_done
                        sch.op("tensor", lambda e, b=ps[k], lhsT=lhsT, buf=buf, dc=dc: e.matmul(
                            b[:], lhsT, xgab[buf][:, dc * 512:(dc + 1) * 512], start=(dc == 0), stop=(dc == 15)),
                            waits=waits, inc=(t_c if (mi == 2 and k == 1) else None))
                for k in range(2):
                    xga_free[((tA + k) % 3, dc // 4)] = t_c.n
            mmw = (t_c, t_c.n)
            for k in range(2):
                tt = tA + k
                col = tt * 512
                sch.op("scalar", lambda e, h=h, b=ps_q[k], col=col: e.activation(
                    out=QTn[h][:, col:col + 512], in_=b[:], func=AF.Identity,
                    bias=bvec[:, 0 + h:1 + h], scale=S4),
                    waits=[mmw, (v_a, BVEC_V)], inc=s_c)
            evq = s_c.n
            for k in range(2):
                tt = tA + k
                col = tt * 512
                sch.op("scalar", lambda e, h=h, b=ps_k[k], col=col: e.activation(
                    out=KT[:, col:col + 512], in_=b[:], func=AF.Identity,
                    bias=bvec[:, 2 + h:3 + h], scale=S4),
                    waits=[mmw], inc=s_c)
            evk = s_c.n
            vfree = []
            for k in range(2):
                tt = tA + k
                sch.op("scalar", lambda e, h=h, b=ps_v[k], k=k: e.activation(
                    out=vfs[k][:], in_=b[:], func=AF.Identity, bias=bvec[:, 4 + h:5 + h], scale=1.0),
                    waits=[mmw, (t_cn, vtr.get(k, 0))], inc=s_c)
                vfree.append(s_c.n)
            evv = s_c.n
            ev_prev = (evq, evk, evv)
            for k in range(2):
                tt = tA + k
                col = tt * 512
                sch.op("scalar", lambda e, h=h, col=col: e.activation(
                    out=sqs[0][:], in_=QTn[h][:, col:col + 512], func=AF.Square),
                    waits=[(t_cn, t_cn.n), (s_c, s_c.n)], inc=s_c)
                sch.op("tensor", lambda e: e.matmul(ps_n[:], onesk[:], sqs[0][:], start=True, stop=True),
                       waits=[(s_c, s_c.n), (v_c, v_c.n), (g_b, ONES_G)], inc=t_cn)
                sch.op("vector", lambda e: e.reduce_max(out=nrm[:, 2:3], in_=ps_n[:], axis=AX),
                       waits=[(t_cn, t_cn.n), (v_c, v_c.n)], inc=v_c)
                sch.op("vector", lambda e: e.tensor_max(out=nrm[:, 0:1], in0=nrm[:, 0:1], in1=nrm[:, 2:3]),
                       waits=[(v_c, v_c.n)], inc=v_c)
                sch.op("scalar", lambda e, col=col: e.activation(
                    out=sqs[1][:], in_=KT[:, col:col + 512], func=AF.Square),
                    waits=[(t_cn, t_cn.n), (s_c, s_c.n)], inc=s_c)
                sch.op("tensor", lambda e: e.matmul(ps_n[:], onesk[:], sqs[1][:], start=True, stop=True),
                       waits=[(s_c, s_c.n), (v_c, v_c.n)], inc=t_cn)
                sch.op("vector", lambda e: e.reduce_max(out=nrm[:, 3:4], in_=ps_n[:], axis=AX),
                       waits=[(t_cn, t_cn.n), (v_c, v_c.n)], inc=v_c)
                sch.op("vector", lambda e: e.tensor_max(out=nrm[:, 1:2], in0=nrm[:, 1:2], in1=nrm[:, 3:4]),
                       waits=[(v_c, v_c.n)], inc=v_c)
                for sub in range(4):
                    sch.op("tensor", lambda e, k=k, sub=sub: e.transpose(
                        ps_tv[:, (k * 4 + sub) * 128:(k * 4 + sub + 1) * 128],
                        vfs[k][:, sub * 128:(sub + 1) * 128], ident[:]),
                        waits=[(s_c, vfree[k]), (v_c, v_c.n), (v_be, v_be.n)], inc=t_cn)
                vtr[k] = t_cn.n
            sch.op("vector", lambda e, pr=pr: e.tensor_copy(out=Vtk[:, pr * 1024:(pr + 1) * 1024], in_=ps_tv[:]),
                   waits=[(t_cn, t_cn.n)], inc=v_c)
            pump(6 if h == 0 else 3)
        sch.op("vector", lambda e: e.tensor_mul(out=nrm[:, 4:5], in0=nrm[:, 0:1], in1=nrm[:, 1:2]),
               waits=[(v_c, v_c.n)], inc=v_c)
        sch.op("scalar", lambda e: e.activation(out=nrm2[:, 0:1], in_=nrm[:, 4:5], func=AF.Sqrt),
               waits=[(v_c, v_c.n), (s_c, s_c.n)], inc=s_c)
        sch.op("vector", lambda e: e.tensor_scalar_mul(out=nrm2[:, 1:2], in0=nrm2[:, 0:1], scalar1=-1.0),
               waits=[(s_c, s_c.n)], inc=v_c)
        sch.op("tensor", lambda e, b=ps_q[0]: e.matmul(b[:, 0:1], onesr_f[:], nrm2[:, 1:2], start=True, stop=True),
               waits=[(v_c, v_c.n), (s_c, s_c.n)], inc=t_cn)
        sch.op("vector", lambda e, h=h: e.tensor_copy(out=negBv[:, h:h + 1], in_=ps_q[0][:, 0:1]),
               waits=[(t_cn, t_cn.n)], inc=v_c)
        stC.close()
        if DBG:
            sch.op("sync", lambda e, h=h: e.dma_start(out=d_qt[h], in_=QTn[h][:]),
                   waits=[(s_c, s_c.n), (v_c, v_c.n)], inc=m_tap, inc_n=16)
            sch.op("sync", lambda e, h=h: e.dma_start(out=d_kt[h], in_=KT[:]),
                   waits=[(s_c, s_c.n)], inc=m_tap, inc_n=16)
            sch.op("sync", lambda e, h=h: e.dma_start(out=d_vt[h], in_=Vtk[:]),
                   waits=[(v_c, v_c.n)], inc=m_tap, inc_n=16)
            sch.op("sync", lambda e, h=h: e.dma_start(out=d_negb[:, h:h + 1], in_=negBv[:, h:h + 1], allow_slow_non_contiguous=True),
                   waits=[(v_c, v_c.n)], inc=m_tap, inc_n=16)
            return (m_tap, m_tap.n)
        return (v_c, v_c.n)

    def phase_D(h, bready):
        stD = ExitStack()
        ps_s = [stD.enter_context(nc.psum_tensor(f"pss{h}{i}", [128, 512], F32)) for i in range(2)]
        ps_nm = [stD.enter_context(nc.psum_tensor(f"psnm{h}{i}", [128, 512], F32)) for i in range(2)]
        ps_l = [stD.enter_context(nc.psum_tensor(f"psl{h}{i}", [1, 512], F32)) for i in range(2)]
        ps_r = stD.enter_context(nc.psum_tensor(f"psr{h}", [128, 512], F32))

        exp_done = {}
        av_done = {}
        add_done = {}
        gi = 0
        chunks = [(qt, kc) for qt in range(16) for kc in range(4 * (qt + 1))]
        ginfo = {}

        def emit_score(gi):
            qt, kc = chunks[gi]
            nkc = 4 * (qt + 1)
            diag = kc >= 4 * qt
            off = 128 * (kc - 4 * qt) if diag else 0
            n = 512 - off
            waits = [(s_d, exp_done.get(gi - 2, 0))]
            if gi == 0:
                waits.append(bready)
            sch.op("tensor", lambda e, qt=qt, kc=kc, off=off, n=n, b=ps_s[gi % 2]: e.matmul(
                b[:, off:off + n], KT[:, kc * 128:(kc + 1) * 128],
                QTn[h][:, qt * 512 + off: qt * 512 + off + n], start=True, stop=True),
                waits=waits, inc=t_ds)
            sw = t_ds.n
            pt = gi % 4
            sch.op("scalar", lambda e, b=ps_s[gi % 2], pt=pt, off=off, n=n: e.activation(
                out=PTb[pt][:, off:off + n], in_=b[:, off:off + n], func=AF.Exp,
                bias=negBv[:, h:h + 1], scale=1.0),
                waits=[(t_ds, sw), (t_da, av_done.get(gi - 4, 0)), (v_dl, add_done.get(gi - 4, 0))], inc=s_d)
            exp_done[gi] = s_d.n
            if diag:
                sch.op("vector", lambda e, pt=pt, off=off: e.tensor_mul(
                    out=PTb[pt][:, off:off + 128], in0=PTb[pt][:, off:off + 128], in1=triu[:]),
                    waits=[(s_d, s_d.n), (m_id, IDENT_W), (v_dl, v_dl.n)], inc=v_dm)
                ptw = (v_dm, v_dm.n)
            else:
                ptw = (s_d, s_d.n)
            ginfo[gi] = (qt, kc, off, n, pt, ptw, sw)

        def emit_consume(gi):
            qt, kc, off, n, pt, ptw, sw = ginfo[gi]
            nkc = 4 * (qt + 1)
            sb = qt % 2
            diag = kc >= 4 * qt
            first = (kc == 0)
            if diag:
                lmw = [ptw]
                if first:
                    lmw.append((v_dn, lfree.get((h, qt - 2), 0)))
                sch.op("tensor", lambda e, pt=pt, off=off, n=n, b=ps_l[sb], first=first, last=(kc == nkc - 1): e.matmul(
                    b[:, off:off + n], onesk[:], PTb[pt][:, off:off + n], start=first, stop=last),
                    waits=lmw)
            else:
                sub = kc % 4
                if sub == 0:
                    sch.op("vector", lambda e, pt=pt: e.tensor_copy(out=PTs[:], in_=PTb[pt][:]),
                           waits=[ptw, (t_dl, t_dl.n), (v_dl, v_dl.n)], inc=v_dl)
                else:
                    sch.op("vector", lambda e, pt=pt: e.tensor_add(out=PTs[:], in0=PTs[:], in1=PTb[pt][:]),
                           waits=[ptw, (v_dl, v_dl.n)], inc=v_dl)
                add_done[gi] = v_dl.n
                if sub == 3:
                    lmw = [(v_dl, v_dl.n)]
                    if kc == 3:
                        lmw.append((v_dn, lfree.get((h, qt - 2), 0)))
                    sch.op("tensor", lambda e, b=ps_l[sb], first=(kc == 3): e.matmul(
                        b[:], onesk[:], PTs[:], start=first, stop=False),
                        waits=lmw, inc=t_dl)
            avw = [ptw]
            if first:
                avw.append((s_dn, numfree.get((h, qt - 2), 0)))
            sch.op("tensor", lambda e, pt=pt, off=off, n=n, kc=kc, b=ps_nm[sb], first=first, last=(kc == nkc - 1): e.matmul(
                b[:, off:off + n], Vtk[:, kc * 128:(kc + 1) * 128], PTb[pt][:, off:off + n],
                start=first, stop=last),
                waits=avw, inc=t_da)
            av_done[gi] = t_da.n
            if kc == nkc - 1:
                emit_epilogue(qt)

        def emit_epilogue(qt):
            sb = qt % 2
            mmw = (t_da, t_da.n)
            qt_last_s = ginfo[gi_last_s[qt]][6]
            sch.op("vector", lambda e, b=ps_l[sb]: e.reciprocal(out=rrow[:], in_=b[:]),
                   waits=[mmw, (t_dl, t_dl.n), (t_dr, t_dr.n), (v_dn, v_dn.n)], inc=v_dn)
            lfree[(h, qt)] = v_dn.n
            sch.op("tensor", lambda e: e.matmul(ps_r[:], onesr_f[:], rrow[:], start=True, stop=True),
                   waits=[(v_dn, v_dn.n)], inc=t_dr)
            sch.op("scalar", lambda e, b=ps_nm[sb]: e.activation(out=numbf[:], in_=b[:], func=AF.Identity),
                   waits=[mmw, (v_dn, v_dn.n)], inc=s_dn)
            numfree[(h, qt)] = s_dn.n
            sch.op("vector", lambda e, qt=qt: e.tensor_tensor(
                out=QTn[h][:, qt * 512:(qt + 1) * 512], in0=numbf[:], in1=ps_r[:], op=ALU.mult),
                waits=[(s_dn, s_dn.n), (t_ds, qt_last_s), (t_dr, t_dr.n)], inc=v_dn)

        gi_last_s = {}
        NG = len(chunks)
        for gi in range(NG):
            emit_score(gi)
            qt, kc = chunks[gi]
            if kc == 4 * (qt + 1) - 1:
                gi_last_s[qt] = gi
            if gi >= 1:
                emit_consume(gi - 1)
        emit_consume(NG - 1)
        stD.close()

    prev = []
    for h in range(HPC):
        br = phase_C(h, prev)
        if h == HPC - 1:
            pump(len(qunits))
            emit_quant_collectives()
        phase_D(h, br)
        prev = [(v_dn, v_dn.n), (s_dn, s_dn.n), (t_dr, t_dr.n)]
        if DBG:
            sch.op("sync", lambda e, h=h: e.dma_start(out=d_num[h], in_=QTn[h][:]),
                   waits=[(v_dn, v_dn.n)], inc=m_tap, inc_n=16)
            prev = prev + [(m_tap, m_tap.n)]

    C_WGU, C_WPR = QUANT_COLLECTIVES
    B_CO = (m_cw[0].n, m_cw[1].n)
    stB.close()

    stE = ExitStack()
    wo_sb = stE.enter_context(nc.sbuf_tensor("wo_sb", [128, HPC * D], BF16))
    att_st = [stE.enter_context(nc.sbuf_tensor(f"attst{i}", [128, D], BF16)) for i in range(2)]
    psE = ExitStack()
    ps_o = [psE.enter_context(nc.psum_tensor(f"pso{i}", [128, 512], F32)) for i in range(8)]

    for hc in range(HPC):
        sch.op("gpsimd", lambda e, hc=hc: e.dma_start(out=wo_sb[:, hc * D:(hc + 1) * D],
                                                      in_=woT[hc * 128:(hc + 1) * 128, :]),
               inc=m_wo, inc_n=16)
    W_WO = m_wo.n
    efree = {}
    efree2 = {}
    edma = {}
    for qi in range(S // 128):
        bs = (qi % 2) * 4
        stg = att_st[qi % 2]
        for j in range(4):
            waits = [(v_a, 2), (m_wo, W_WO), (g_b, ONES_G),
                     (v_e, efree.get(qi - 2, 0)), (s_e, efree2.get(qi - 2, 0))]
            if qi == 0 and j == 0:
                waits += prev
            sch.op("tensor", lambda e, j=j, b=ps_o[bs + j]: e.matmul(
                b[:], onesr_b[:], bo8bf[:, j * 512:(j + 1) * 512], start=True, stop=False),
                waits=waits)
        for hh in range(HPC):
            for j in range(4):
                sch.op("tensor", lambda e, hh=hh, qi=qi, j=j, b=ps_o[bs + j]: e.matmul(
                    b[:], QTn[hh][:, qi * 128:(qi + 1) * 128],
                    wo_sb[:, hh * D + j * 512: hh * D + (j + 1) * 512],
                    start=False, stop=(hh == HPC - 1)),
                    inc=(t_e if (hh == HPC - 1 and j == 3) else None))
        mmw = (t_e, t_e.n)
        for j in range(4):
            if j < 2:
                sch.op("vector", lambda e, stg=stg, j=j, b=ps_o[bs + j]: e.tensor_copy(
                    out=stg[:, j * 512:(j + 1) * 512], in_=b[:]),
                    waits=[mmw, (m_cw[0], B_CO[0]), (m_cw[1], B_CO[1]), (m_ca[qi % 2], edma.get(qi - 2, 0))], inc=v_e)
            else:
                sch.op("scalar", lambda e, stg=stg, j=j, b=ps_o[bs + j]: e.activation(
                    out=stg[:, j * 512:(j + 1) * 512], in_=b[:], func=AF.Identity),
                    waits=[mmw, (m_cw[0], B_CO[0]), (m_cw[1], B_CO[1]), (m_ca[qi % 2], edma.get(qi - 2, 0))], inc=s_e)
        efree[qi] = v_e.n
        efree2[qi] = s_e.n
        mca = m_ca[qi % 2]
        sch.op("sync", lambda e, qi=qi, stg=stg: e.dma_start(
            out=c_att.ap()[qi * 128:(qi + 1) * 128, :], in_=stg[:]),
            waits=[(v_e, v_e.n), (s_e, s_e.n)], inc=mca, inc_n=16)
        edma[qi] = mca.n
    sch.op("gpsimd", lambda e: e.collective_compute("ReduceScatter", ALU.add, replica_groups=RG,
                                                    ins=[c_att.ap().opt()], outs=[att_rs.ap().opt()]),
           waits=[(m_ca[0], m_ca[0].n), (m_ca[1], m_ca[1].n)], inc=c_sem)
    sch.op("gpsimd", lambda e: e.collective_compute("AllReduce", ALU.add, replica_groups=RG,
                                                    ins=[c_bar.ap().opt()], outs=[c_bar2.ap().opt()]),
           inc=c_sem)
    C_RS = c_sem.n
    E_V, E_S, E_CO0, E_CO1 = v_e.n, s_e.n, m_ca[0].n, m_ca[1].n
    if DBG:
        sch.op("sync", lambda e: e.dma_start(out=d_catt, in_=c_att.ap()), waits=[(m_ca[0], m_ca[0].n), (m_ca[1], m_ca[1].n)], inc=m_tap, inc_n=16)
        sch.op("sync", lambda e: e.dma_start(out=d_rs, in_=att_rs.ap()), waits=[(c_sem, C_RS)], inc=m_tap, inc_n=16)
    psE.close()
    stE.close()
    stCsc.close()
    stCDE.close()

    stF = ExitStack()
    hT = stF.enter_context(nc.sbuf_tensor("hT", [128, 64 * 512], BF16))
    wgt = [stF.enter_context(nc.sbuf_tensor(f"wgt{i}", [128, 2048], BF16)) for i in range(4)]
    wut = [stF.enter_context(nc.sbuf_tensor(f"wut{i}", [128, 2048], BF16)) for i in range(4)]
    sgs = [stF.enter_context(nc.sbuf_tensor(f"sgs{i}", [128, 512], F32)) for i in range(2)]
    hti = [stF.enter_context(nc.sbuf_tensor(f"hti{i}", [128, 512], BF16)) for i in range(4)]
    hsq = stF.enter_context(nc.sbuf_tensor("hsq", [128, 512], BF16))
    wpt = [stF.enter_context(nc.sbuf_tensor(f"wpt{i}", [128, 2048], BF16)) for i in range(2)]
    fxb = [stF.enter_context(nc.sbuf_tensor(f"fxb{i}", [128, 1024], F32)) for i in range(2)]
    fab = [stF.enter_context(nc.sbuf_tensor(f"fab{i}", [128, 1024], BF16)) for i in range(2)]
    fob = [stF.enter_context(nc.sbuf_tensor(f"fob{i}", [128, 1024], F32)) for i in range(2)]

    sch.wait("vector", [(m_ca[0], E_CO0), (m_ca[1], E_CO1)])
    sch.wait("scalar", [(m_ca[0], E_CO0), (m_ca[1], E_CO1)])
    sch.wait("tensor", [(v_e, E_V), (s_e, E_S)])

    wslot = {}
    vffree = {}
    trev = {}
    rmsred = 0
    ppD = {}
    outD = {}
    for slab in range(2):
        stG = ExitStack()
        ps_g = [stG.enter_context(nc.psum_tensor(f"psg{slab}{i}", [128, 512], F32)) for i in range(4)]
        ps_u = [stG.enter_context(nc.psum_tensor(f"psu{slab}{i}", [128, 512], F32)) for i in range(4)]
        for g in range(16):
            lwq = {}
            for q in range(4):
                qs = q
                rg_, cg = (g * 512) // 2048, (g * 512) % 2048
                ru, cu = ((g + 16) * 512) // 2048, ((g + 16) * 512) % 2048
                waits = [(t_f, wslot.get(qs, 0))]
                if slab == 0 and g == 0 and q == 0:
                    waits.append((c_sem, C_WGU))
                mg1, mg2 = m_wgu[qs], m_wgu[4 + qs]
                srcg = wgu_ag.ap()[rg_ * D + q * 512: rg_ * D + (q + 1) * 512,
                                   cg:cg + 512].rearrange("(a p) n -> p a n", p=128)
                srcu = wgu_ag.ap()[ru * D + q * 512: ru * D + (q + 1) * 512,
                                   cu:cu + 512].rearrange("(a p) n -> p a n", p=128)
                dstg = wgt[qs].ap().rearrange("p (a n) -> p a n", n=512)
                dstu = wut[qs].ap().rearrange("p (a n) -> p a n", n=512)
                sch.op("sync", lambda e, dstg=dstg, srcg=srcg: e.dma_start(out=dstg, in_=srcg),
                       waits=waits, inc=mg1, inc_n=16)
                sch.op("sync", lambda e, dstu=dstu, srcu=srcu: e.dma_start(out=dstu, in_=srcu),
                       waits=[(t_f, wslot.get(qs, 0))], inc=mg2, inc_n=16)
                lwq[q] = [(mg1, mg1.n), (mg2, mg2.n)]
            for dc in range(16):
                qs = dc // 4
                co = (dc % 4) * 512
                for tc in range(4):
                    lhsT = xnT[:, dc * 1024 + (slab * 4 + tc) * 128: dc * 1024 + (slab * 4 + tc) * 128 + 128]
                    waits = list(lwq[dc // 4])
                    if dc == 0:
                        waits.append((v_f, trev.get((slab, g - 1), 0) if g > 0 else (trev.get((slab - 1, 15), 0))))
                        waits.append((v_ae, v_ae.n))
                    sch.op("tensor", lambda e, b=ps_g[tc], lhsT=lhsT, qs=qs, co=co, dc=dc: e.matmul(
                        b[:], lhsT, wgt[qs][:, co:co + 512], start=(dc == 0), stop=(dc == 15)), waits=waits)
                    sch.op("tensor", lambda e, b=ps_u[tc], lhsT=lhsT, qs=qs, co=co, dc=dc: e.matmul(
                        b[:], lhsT, wut[qs][:, co:co + 512], start=(dc == 0), stop=(dc == 15)),
                        inc=(t_f if tc == 3 else None))
                if dc % 4 == 3:
                    wslot[qs] = t_f.n
            mmw = (t_f, t_f.n)
            for tc in range(4):
                sg = sgs[tc % 2]
                ht = hti[tc % 4]
                sch.op("scalar", lambda e, sg=sg, b=ps_g[tc]: e.activation(out=sg[:], in_=b[:], func=AF.Silu),
                       waits=[mmw, (v_f, v_f.n)], inc=s_f)
                sch.op("vector", lambda e, sg=sg, ht=ht, b=ps_u[tc]: e.tensor_tensor(
                    out=ht[:], in0=sg[:], in1=b[:], op=ALU.mult),
                    waits=[(s_f, s_f.n), (t_ft, t_ft.n)], inc=v_f)
                vffree[(slab, g, tc)] = v_f.n
                htw = (v_f, v_f.n)
                sch.op("scalar", lambda e, ht=ht, g=g, tc=tc: e.activation(
                    out=hsq[:], in_=ht[:], func=AF.Square, accum_out=rmsp[:, tc * 16 + g:tc * 16 + g + 1]),
                    waits=[htw, (v_f, rmsred)], inc=s_f)
                for sub in range(4):
                    pst = ps_u[tc].bitcast(BF16)[:, sub * 128:(sub + 1) * 128]
                    sch.op("tensor", lambda e, ht=ht, sub=sub, pst=pst: e.transpose(
                        pst, ht[:, sub * 128:(sub + 1) * 128], ident[:]),
                        waits=[htw, (s_f, s_f.n), (v_f, v_f.n)], inc=t_ft)
                dst = bass.AP(hT, (g * 4) * 512 + tc * 128, [[64 * 512, 128], [512, 4], [1, 128]])
                srcv = ps_u[tc].bitcast(BF16)[:, 0:512].rearrange("p (a n) -> p a n", n=128)
                sch.op("vector", lambda e, dst=dst, srcv=srcv: e.tensor_copy(out=dst, in_=srcv),
                       waits=[(t_ft, t_ft.n)], inc=v_f)
            trev[(slab, g)] = v_f.n
        for tc in range(4):
            sch.op("vector", lambda e, tc=tc: e.reduce_sum(out=lnst[:, 0:1], in_=rmsp[:, tc * 16:(tc + 1) * 16], axis=AX),
                   waits=[(s_f, s_f.n), (v_f, v_f.n)], inc=v_f)
            sch.op("vector", lambda e: e.tensor_scalar(out=lnst[:, 2:3], in0=lnst[:, 0:1],
                                                       scalar1=1.0 / HID, scalar2=RMS_EPS,
                                                       op0=ALU.mult, op1=ALU.add),
                   waits=[(v_f, v_f.n)], inc=v_f)
            sch.op("scalar", lambda e: e.activation(out=lnst[:, 1:2], in_=lnst[:, 2:3], func=AF.Sqrt),
                   waits=[(v_f, v_f.n), (s_f, s_f.n)], inc=s_f)
            sch.op("vector", lambda e, slab=slab, tc=tc: e.reciprocal(
                out=rstd8[:, slab * 4 + tc:slab * 4 + tc + 1], in_=lnst[:, 1:2]),
                waits=[(s_f, s_f.n)], inc=v_f)
        rmsred = v_f.n
        if DBG:
            sch.op("sync", lambda e, slab=slab: e.dma_start(out=d_ht[slab], in_=hT[:]),
                   waits=[(v_f, v_f.n)], inc=m_tap, inc_n=16)
            sch.op("sync", lambda e: e.dma_start(out=d_rstd, in_=rstd8[:]),
                   waits=[(v_f, v_f.n)], inc=m_tap, inc_n=16)
        stG.close()

        stP = ExitStack()
        ps_p = [stP.enter_context(nc.psum_tensor(f"psp{slab}{i}", [128, 512], F32)) for i in range(8)]
        for tp in range(2):
            tcs = (2 * tp, 2 * tp + 1)
            for hc in range(64):
                slot = hc % 2
                waits = [(t_p, ppD.get(slot, 0))]
                if slab == 0 and tp == 0 and hc < 2:
                    waits.append((c_sem, C_WPR))
                mps = m_wp[slot]
                srcap = wpr_ag.ap().rearrange("(r q) j -> q r j", r=NCORES)[hc * 128:(hc + 1) * 128, :, :]
                dstap = wpt[slot].ap().rearrange("p (r j) -> p r j", j=256)
                sch.op("sync", lambda e, dstap=dstap, srcap=srcap: e.dma_start(out=dstap, in_=srcap),
                       waits=waits, inc=mps, inc_n=16)
                lw = (mps, mps.n)
                for k, tc in enumerate(tcs):
                    lhsT = hT[:, hc * 512 + tc * 128: hc * 512 + (tc + 1) * 128]
                    for j in range(4):
                        waits2 = [lw]
                        if hc == 0 and j == 0:
                            waits2.append((v_f, v_f.n))
                            waits2.append((v_p, v_p.n))
                        sch.op("tensor", lambda e, b=ps_p[k * 4 + j], lhsT=lhsT, s=slot, j=j, hc=hc: e.matmul(
                            b[:], lhsT, wpt[s][:, j * 512:(j + 1) * 512], start=(hc == 0), stop=(hc == 63)),
                            waits=waits2, inc=(t_p if (k == 1 and j == 3) else None))
                ppD[slot] = t_p.n
            mmw = (t_p, t_p.n)
            for k, tc in enumerate(tcs):
                stc = slab * 4 + tc
                row = slab * 512 + tc * 128
                for half in range(2):
                    mfa, mfx_ = m_fx[k], m_fx[2 + k]
                    mo = m_out4[k * 2 + half]
                    mo2 = m_out4[k * 2 + (1 - half)]
                    sch.op("sync", lambda e, k=k, row=row, half=half: e.dma_start(
                        out=fab[k][:], in_=att_rs.ap()[row:row + 128, half * 1024:(half + 1) * 1024]),
                        waits=[(c_sem, C_RS), (v_p, v_p.n), (mo, outD.get((k, half), 0)),
                               (mo2, outD.get((k, 1 - half), 0))],
                        inc=mfa, inc_n=16)
                    sch.op("sync", lambda e, k=k, row=row, half=half: e.dma_start(
                        out=fxb[k][:], in_=x_in[row:row + 128, half * 1024:(half + 1) * 1024]),
                        inc=mfx_, inc_n=16)
                    fxw = (mfa, mfa.n)
                    fxw2 = (mfx_, mfx_.n)
                    for jj in range(2):
                        j = half * 2 + jj
                        sch.op("vector", lambda e, k=k, j=j, jj=jj, stc=stc, b=ps_p[k * 4 + j]: e.scalar_tensor_tensor(
                            out=fob[k][:, jj * 512:(jj + 1) * 512], in0=b[:], scalar=rstd8[:, stc:stc + 1],
                            in1=fab[k][:, jj * 512:(jj + 1) * 512], op0=ALU.mult, op1=ALU.add),
                            waits=[mmw, fxw, (v_f, v_f.n)], inc=v_p)
                        sch.op("vector", lambda e, k=k, jj=jj: e.tensor_add(
                            out=fob[k][:, jj * 512:(jj + 1) * 512], in0=fob[k][:, jj * 512:(jj + 1) * 512],
                            in1=fxb[k][:, jj * 512:(jj + 1) * 512]),
                            waits=[(v_p, v_p.n), fxw2], inc=v_p)
                    sch.op("sync", lambda e, k=k, row=row, half=half: e.dma_start(
                        out=out_t[row:row + 128, half * 1024:(half + 1) * 1024], in_=fob[k][:]),
                        waits=[(v_p, v_p.n)], inc=mo, inc_n=16)
                    outD[(k, half)] = mo.n
        stP.close()
    stF.close()

    sch.wait("sync", [(m, m.n) for m in m_out4])
    if DBG:
        sch.wait("sync", [(m_tap, m_tap.n)])
    sch.wait("gpsimd", [(c_sem, c_sem.n)])

    with nc.Block() as blk:
        blk.sync(lambda e: [f(e) for f in sch.streams["sync"]])
        blk.tensor(lambda e: [f(e) for f in sch.streams["tensor"]])
        blk.vector(lambda e: [f(e) for f in sch.streams["vector"]])
        blk.scalar(lambda e: [f(e) for f in sch.streams["scalar"]])
        blk.gpsimd(lambda e: [f(e) for f in sch.streams["gpsimd"]])
    big.close()
    return nc


def kernel(L0_attn_X, L0_attn_Wq, L0_attn_bq, L0_attn_Wk, L0_attn_bk, L0_attn_Wv,
           L0_attn_bv, L0_attn_Wo, L0_attn_bo, L0_ffn_w_gate_up, L0_ffn_w_proj):
    X = np.ascontiguousarray(np.asarray(L0_attn_X, dtype=np.float32).reshape(S, D))
    WqT = np.ascontiguousarray(np.asarray(L0_attn_Wq, np.float32).T)
    WkT = np.ascontiguousarray(np.asarray(L0_attn_Wk, np.float32).T)
    WvT = np.ascontiguousarray(np.asarray(L0_attn_Wv, np.float32).T)
    WoT = np.ascontiguousarray(np.asarray(L0_attn_Wo, np.float32).T)
    bq = np.asarray(L0_attn_bq, np.float32)
    bk = np.asarray(L0_attn_bk, np.float32)
    bv = np.asarray(L0_attn_bv, np.float32)
    bo = np.asarray(L0_attn_bo, np.float32).reshape(1, D)
    wgu = np.asarray(L0_ffn_w_gate_up, np.float32)
    wpr = np.asarray(L0_ffn_w_proj, np.float32)
    ident = np.eye(128, dtype=np.float32)
    triu = np.triu(np.ones((128, 128), dtype=np.float32))

    in_maps = []
    for c in range(NCORES):
        j0 = c * HPC * DH
        b6_c = np.stack([bq[j0:j0 + DH], bq[j0 + DH:j0 + 2 * DH],
                         bk[j0:j0 + DH], bk[j0 + DH:j0 + 2 * DH],
                         bv[j0:j0 + DH], bv[j0 + DH:j0 + 2 * DH]], axis=0)
        in_maps.append({
            "x": np.ascontiguousarray(X[c * T:(c + 1) * T]),
            "wqT": np.ascontiguousarray(WqT[:, j0:j0 + HPC * DH]),
            "wkT": np.ascontiguousarray(WkT[:, j0:j0 + HPC * DH]),
            "wvT": np.ascontiguousarray(WvT[:, j0:j0 + HPC * DH]),
            "woT": np.ascontiguousarray(WoT[j0:j0 + HPC * DH, :]),
            "b6": np.ascontiguousarray(b6_c),
            "bo": bo,
            "wgu": np.ascontiguousarray(wgu[c * (2 * HID // NCORES):(c + 1) * (2 * HID // NCORES)]),
            "wpr": np.ascontiguousarray(wpr[c * (D // NCORES):(c + 1) * (D // NCORES)]),
            "identin": ident,
            "triuin": triu,
        })

    nc = build()
    res = run_bass_kernel_spmd(nc, in_maps, core_ids=list(range(NCORES)))
    out = np.concatenate([res.results[c]["out"] for c in range(NCORES)], axis=0)
    return out.reshape(1, S, D)


# version 19
# speedup vs baseline: 1.2178x; 1.0657x over previous
from contextlib import ExitStack

import numpy as np
import concourse.bass as bass
import concourse.mybir as mybir
from concourse.bass_utils import run_bass_kernel_spmd

F32 = mybir.dt.float32
BF16 = mybir.dt.bfloat16
AF = mybir.ActivationFunctionType
ALU = mybir.AluOpType
AX = mybir.AxisListType.X

NCORES = 8
S = 8192
D = 2048
NH = 16
DH = 128
HID = 8192
T = S // NCORES
HPC = NH // NCORES
S4 = float(np.float32(1.0 / np.sqrt(np.sqrt(float(DH)))))
LN_EPS = 1e-5
RMS_EPS = float(np.finfo(np.float32).eps)

ENGS = ("sync", "tensor", "vector", "scalar", "gpsimd")


class Sem:
    __slots__ = ("h", "n", "name")

    def __init__(self, h, name):
        self.h = h
        self.n = 0
        self.name = name


class Sched:
    def __init__(self, nc):
        self.nc = nc
        self.streams = {e: [] for e in ENGS}

    def sem(self, name):
        return Sem(self.nc.alloc_semaphore(name=name), name)

    def op(self, eng, build, waits=(), inc=None, inc_n=1):
        wl = []
        for w in waits:
            if isinstance(w, tuple):
                s, v = w
            else:
                s, v = w, w.n
            if v > 0:
                wl.append((s.h, v))
        il = None
        if inc is not None:
            inc.n += inc_n
            il = (inc.h, inc_n)

        def thunk(e, build=build, wl=wl, il=il):
            for h, v in wl:
                e.wait_ge(h, v)
            r = build(e)
            if il is not None:
                r.then_inc(il[0], il[1])

        self.streams[eng].append(thunk)

    def wait(self, eng, waits):
        wl = []
        for w in waits:
            if isinstance(w, tuple):
                s, v = w
            else:
                s, v = w, w.n
            if v > 0:
                wl.append((s.h, v))

        def thunk(e, wl=wl):
            for h, v in wl:
                e.wait_ge(h, v)

        self.streams[eng].append(thunk)


DBG = False


def build():
    nc = bass.Bass("TRN2", target_bir_lowering=False)

    x_in = nc.dram_tensor("x", [T, D], F32, kind="ExternalInput").ap()
    wqT = nc.dram_tensor("wqT", [D, HPC * DH], F32, kind="ExternalInput").ap()
    wkT = nc.dram_tensor("wkT", [D, HPC * DH], F32, kind="ExternalInput").ap()
    wvT = nc.dram_tensor("wvT", [D, HPC * DH], F32, kind="ExternalInput").ap()
    woT = nc.dram_tensor("woT", [HPC * DH, D], F32, kind="ExternalInput").ap()
    b6 = nc.dram_tensor("b6", [6, DH], F32, kind="ExternalInput").ap()
    bo_in = nc.dram_tensor("bo", [1, D], F32, kind="ExternalInput").ap()
    wgu_in = nc.dram_tensor("wgu", [2 * HID // NCORES, D], F32, kind="ExternalInput").ap()
    wpr_in = nc.dram_tensor("wpr", [D // NCORES, HID], F32, kind="ExternalInput").ap()
    ident_in = nc.dram_tensor("identin", [128, 128], F32, kind="ExternalInput").ap()
    triu_in = nc.dram_tensor("triuin", [128, 128], F32, kind="ExternalInput").ap()
    out_t = nc.dram_tensor("out", [T, D], F32, kind="ExternalOutput").ap()
    if DBG:
        d_xnag = nc.dram_tensor("d_xnag", [NCORES * D, T], BF16, kind="ExternalOutput").ap()
        d_qt = [nc.dram_tensor(f"d_qt{h}", [128, S], BF16, kind="ExternalOutput").ap() for h in range(2)]
        d_kt = [nc.dram_tensor(f"d_kt{h}", [128, S], BF16, kind="ExternalOutput").ap() for h in range(2)]
        d_vt = [nc.dram_tensor(f"d_vt{h}", [128, S], BF16, kind="ExternalOutput").ap() for h in range(2)]
        d_num = [nc.dram_tensor(f"d_num{h}", [128, S], BF16, kind="ExternalOutput").ap() for h in range(2)]
        d_negb = nc.dram_tensor("d_negb", [128, 2], F32, kind="ExternalOutput").ap()
        d_catt = nc.dram_tensor("d_catt", [S, D], BF16, kind="ExternalOutput").ap()
        d_rs = nc.dram_tensor("d_rs", [T, D], BF16, kind="ExternalOutput").ap()
        d_wgu = nc.dram_tensor("d_wgu", [NCORES * D, 2 * HID // NCORES], BF16, kind="ExternalOutput").ap()
        d_wpr = nc.dram_tensor("d_wpr", [NCORES * HID, D // NCORES], BF16, kind="ExternalOutput").ap()
        d_ht = [nc.dram_tensor(f"d_ht{i}", [128, 64 * 512], BF16, kind="ExternalOutput").ap() for i in range(2)]
        d_rstd = nc.dram_tensor("d_rstd", [128, 8], F32, kind="ExternalOutput").ap()
        m_tap = None

    c_xn = nc.dram_tensor("c_xn", [D, T], BF16, kind="Internal")
    xn_ag = nc.dram_tensor("xn_ag", [NCORES * D, T], BF16, kind="Internal", addr_space="Shared")
    c_wgu = nc.dram_tensor("c_wgu", [D, 2 * HID // NCORES], BF16, kind="Internal")
    wgu_ag = nc.dram_tensor("wgu_ag", [NCORES * D, 2 * HID // NCORES], BF16, kind="Internal", addr_space="Shared")
    c_wpr = nc.dram_tensor("c_wpr", [HID, D // NCORES], BF16, kind="Internal")
    wpr_ag = nc.dram_tensor("wpr_ag", [NCORES * HID, D // NCORES], BF16, kind="Internal", addr_space="Shared")
    c_att = nc.dram_tensor("c_att", [S, D], BF16, kind="Internal")
    att_rs = nc.dram_tensor("att_rs", [T, D], BF16, kind="Internal")
    c_bar = nc.dram_tensor("c_bar", [1, 64], F32, kind="Internal")
    c_bar2 = nc.dram_tensor("c_bar2", [1, 64], F32, kind="Internal")

    RG = [list(range(NCORES))]
    sch = Sched(nc)

    m_x = [sch.sem(f"m_x{i}") for i in range(2)]
    m_b = sch.sem("m_b")
    m_id = sch.sem("m_id")
    m_w = sch.sem("m_w")
    m_wo = sch.sem("m_wo")
    m_q = [sch.sem(f"m_q{i}") for i in range(2)]
    m_xga = [sch.sem(f"m_xga{i}") for i in range(12)]
    m_wgu = [sch.sem(f"m_wgu{i}") for i in range(16)]
    m_wp = [sch.sem(f"m_wp{i}") for i in range(2)]
    m_fx = [sch.sem(f"m_fx{i}") for i in range(4)]
    m_cxn = sch.sem("m_cxn")
    m_cb = sch.sem("m_cb")
    m_cw = [sch.sem(f"m_cw{i}") for i in range(2)]
    m_ca = [sch.sem(f"m_ca{i}") for i in range(2)]
    m_out4 = [sch.sem(f"m_out{i}") for i in range(4)]
    c_sem = sch.sem("c_sem")
    m_tap = sch.sem("m_tap")

    v_a = sch.sem("v_a")
    s_a = sch.sem("s_a")
    t_a = sch.sem("t_a")
    v_ae = sch.sem("v_ae")

    v_b = sch.sem("v_b")
    g_b = sch.sem("g_b")
    t_b = sch.sem("t_b")
    v_be = sch.sem("v_be")

    t_c = sch.sem("t_c")
    s_c = sch.sem("s_c")
    t_cn = sch.sem("t_cn")
    v_c = sch.sem("v_c")

    s_d = sch.sem("s_d")
    t_ds = sch.sem("t_ds")
    t_da = sch.sem("t_da")
    v_dm = sch.sem("v_dm")
    s_dn = sch.sem("s_dn")
    v_dl = sch.sem("v_dl")
    t_dl = sch.sem("t_dl")
    v_dn = sch.sem("v_dn")
    t_dr = sch.sem("t_dr")

    t_e = sch.sem("t_e")
    v_e = sch.sem("v_e")
    s_e = sch.sem("s_e")

    t_f = sch.sem("t_f")
    s_f = sch.sem("s_f")
    v_f = sch.sem("v_f")
    t_ft = sch.sem("t_ft")
    t_p = sch.sem("t_p")
    v_p = sch.sem("v_p")

    big = ExitStack()
    ident = big.enter_context(nc.sbuf_tensor("ident", [128, 128], BF16))
    triu = big.enter_context(nc.sbuf_tensor("triu", [128, 128], BF16))
    onesk = big.enter_context(nc.sbuf_tensor("onesk", [128, 1], BF16))
    onesr_f = big.enter_context(nc.sbuf_tensor("onesr_f", [1, 128], F32))
    onesr_b = big.enter_context(nc.sbuf_tensor("onesr_b", [1, 128], BF16))
    bvec = big.enter_context(nc.sbuf_tensor("bvec", [128, 8], F32))
    bo8bf = big.enter_context(nc.sbuf_tensor("bo8bf", [1, D], BF16))
    negBv = big.enter_context(nc.sbuf_tensor("negBv", [128, 2], F32))
    nrm = big.enter_context(nc.sbuf_tensor("nrm", [1, 8], F32))
    nrm2 = big.enter_context(nc.sbuf_tensor("nrm2", [1, 8], F32))
    rrow = big.enter_context(nc.sbuf_tensor("rrow", [1, 512], F32))
    rstd8 = big.enter_context(nc.sbuf_tensor("rstd8", [128, 8], F32))
    rmsp = big.enter_context(nc.sbuf_tensor("rmsp", [128, 64], F32))
    lnst = big.enter_context(nc.sbuf_tensor("lnst", [128, 8], F32))
    xnT = big.enter_context(nc.sbuf_tensor("xnT", [128, 16 * 1024], BF16))

    stCDE = ExitStack()
    wq_sb = stCDE.enter_context(nc.sbuf_tensor("wq_sb", [128, 2048], BF16))
    wk_sb = stCDE.enter_context(nc.sbuf_tensor("wk_sb", [128, 2048], BF16))
    wv_sb = stCDE.enter_context(nc.sbuf_tensor("wv_sb", [128, 2048], BF16))
    QTn = [stCDE.enter_context(nc.sbuf_tensor(f"QTn{h}", [128, S], BF16)) for h in range(HPC)]
    KT = stCDE.enter_context(nc.sbuf_tensor("KT", [128, S], BF16))
    Vtk = stCDE.enter_context(nc.sbuf_tensor("Vtk", [128, S], BF16))
    xgab = [stCDE.enter_context(nc.sbuf_tensor(f"xgab{i}", [128, 8192], BF16)) for i in range(3)]

    sch.op("gpsimd", lambda e: e.dma_start(out=ident[:], in_=ident_in), inc=m_id, inc_n=16)
    sch.op("gpsimd", lambda e: e.dma_start(out=triu[:], in_=triu_in), inc=m_id, inc_n=16)
    IDENT_W = 32
    sch.op("gpsimd", lambda e: e.memset(onesk[:], 1.0), inc=g_b)
    sch.op("gpsimd", lambda e: e.memset(onesr_f[:], 1.0), inc=g_b)
    sch.op("gpsimd", lambda e: e.memset(onesr_b[:], 1.0), inc=g_b)
    sch.op("gpsimd", lambda e: e.memset(nrm2[:], 0.0), inc=g_b)
    sch.op("gpsimd", lambda e: e.memset(rrow[:, 0:64], 0.0), inc=g_b)
    ONES_G = g_b.n
    sch.op("gpsimd", lambda e: e.dma_start(out=c_bar.ap(), in_=rrow[:, 0:64]), waits=[(g_b, ONES_G)], inc=m_cb, inc_n=16)


    stA = ExitStack()
    xld = [stA.enter_context(nc.sbuf_tensor(f"xld{i}", [128, D], F32)) for i in range(2)]
    xnbuf = stA.enter_context(nc.sbuf_tensor("xnbuf", [128, D], BF16))
    psA = ExitStack()
    ps_tra = psA.enter_context(nc.psum_tensor("ps_tra", [128, 1024], BF16))
    ps_bias = psA.enter_context(nc.psum_tensor("ps_bias", [128, 8], F32))

    b6row = xld[0]
    borow = xld[1]
    sch.op("sync", lambda e: e.dma_start(out=b6row[0:1, 0:768].bitcast(F32) if False else b6row[0:1, 0:768], in_=b6.rearrange("a b -> (a b)")[None, :]), inc=m_b, inc_n=16)
    sch.op("sync", lambda e: e.dma_start(out=borow[0:1, :], in_=bo_in), inc=m_b, inc_n=16)
    sch.op("vector", lambda e: e.tensor_scalar_mul(out=bo8bf[:], in0=borow[0:1, :], scalar1=0.125),
           waits=[(m_b, 32)], inc=v_a)
    for i in range(6):
        sch.op("tensor",
               lambda e, i=i: e.matmul(ps_bias[:, i:i + 1], b6row[0:1, i * 128:(i + 1) * 128],
                                       onesr_f[:, 0:1], start=True, stop=True),
               waits=[(m_b, 32), (g_b, ONES_G)], inc=t_a)
    sch.op("vector", lambda e: e.tensor_scalar(out=bvec[:, 0:4], in0=ps_bias[:, 0:4],
                                               scalar1=S4, scalar2=None, op0=ALU.mult),
           waits=[(t_a, 6)], inc=v_a)
    sch.op("vector", lambda e: e.tensor_copy(out=bvec[:, 4:6], in_=ps_bias[:, 4:6]),
           waits=[(t_a, 6)], inc=v_a)
    BVEC_V = v_a.n

    NT = T // 128
    xn_done = [0, 0]
    for ti in range(NT):
        buf = xld[ti % 2]
        mxs = m_x[ti % 2]
        bgate = []
        if ti == 0:
            bgate = [(t_a, 6)]
        elif ti == 1:
            bgate = [(v_a, 1)]
        sch.op("sync", lambda e, ti=ti, buf=buf: e.dma_start(out=buf[:], in_=x_in[ti * 128:(ti + 1) * 128, :]),
               waits=[(v_a, xn_done[ti % 2])] + bgate, inc=mxs, inc_n=16)
        xw = (mxs, mxs.n)
        sch.op("vector", lambda e, buf=buf: e.reduce_sum(out=lnst[:, 0:1], in_=buf[:], axis=AX),
               waits=[xw, (s_a, s_a.n)], inc=v_a)
        sch.op("scalar", lambda e, buf=buf: e.activation(out=xnbuf[:], in_=buf[:], func=AF.Square,
                                                         accum_out=lnst[:, 1:2]),
               waits=[xw, (v_a, v_a.n), (t_a, t_a.n)], inc=s_a)
        sch.op("vector", lambda e: e.tensor_scalar_mul(out=lnst[:, 2:3], in0=lnst[:, 0:1], scalar1=1.0 / D),
               waits=[(v_a, v_a.n)], inc=v_a)
        sch.op("vector", lambda e: e.tensor_scalar(out=lnst[:, 3:4], in0=lnst[:, 2:3],
                                                   scalar1=lnst[:, 2:3], scalar2=-LN_EPS,
                                                   op0=ALU.mult, op1=ALU.add),
               waits=[(v_a, v_a.n)], inc=v_a)
        sch.op("vector", lambda e: e.scalar_tensor_tensor(out=lnst[:, 4:5], in0=lnst[:, 1:2],
                                                          scalar=1.0 / D, in1=lnst[:, 3:4],
                                                          op0=ALU.mult, op1=ALU.subtract),
               waits=[(v_a, v_a.n), (s_a, s_a.n)], inc=v_a)
        sch.op("scalar", lambda e: e.activation(out=lnst[:, 5:6], in_=lnst[:, 4:5], func=AF.Sqrt),
               waits=[(v_a, v_a.n)], inc=s_a)
        sch.op("vector", lambda e: e.reciprocal(out=lnst[:, 6:7], in_=lnst[:, 5:6]),
               waits=[(s_a, s_a.n)], inc=v_a)
        sch.op("vector", lambda e: e.scalar_tensor_tensor(out=lnst[:, 7:8], in0=lnst[:, 2:3],
                                                          scalar=-1.0, in1=lnst[:, 6:7],
                                                          op0=ALU.mult, op1=ALU.mult),
               waits=[(v_a, v_a.n)], inc=v_a)
        sch.op("vector", lambda e, buf=buf: e.tensor_scalar(out=xnbuf[:], in0=buf[:],
                                                            scalar1=lnst[:, 6:7], scalar2=lnst[:, 7:8],
                                                            op0=ALU.mult, op1=ALU.add),
               waits=[(v_a, v_a.n), (t_a, t_a.n), (s_a, s_a.n)], inc=v_a)
        xn_done[ti % 2] = v_a.n
        xnw = (v_a, v_a.n)
        for bq in range(2):
            for s8 in range(8):
                dc = bq * 8 + s8
                sch.op("tensor", lambda e, dc=dc, s8=s8: e.transpose(
                    ps_tra[:, s8 * 128:(s8 + 1) * 128], xnbuf[:, dc * 128:(dc + 1) * 128], ident[:]),
                    waits=[xnw, (m_id, IDENT_W), (v_ae, v_ae.n)], inc=t_a)
            dst = bass.AP(xnT, ti * 128 + bq * 8 * 1024, [[16 * 1024, 128], [1024, 8], [1, 128]])
            srcv = ps_tra.ap().rearrange("p (a n) -> p a n", n=128)
            sch.op("vector", lambda e, dst=dst, srcv=srcv: e.tensor_copy(out=dst, in_=srcv),
                   waits=[(t_a, t_a.n)], inc=v_ae)
    for dc in range(16):
        sch.op("sync", lambda e, dc=dc: e.dma_start(out=c_xn.ap()[dc * 128:(dc + 1) * 128, :],
                                                    in_=xnT[:, dc * 1024:(dc + 1) * 1024]),
               waits=[(v_ae, v_ae.n)], inc=m_cxn, inc_n=16)
    sch.op("gpsimd", lambda e: e.collective_compute("AllGather", ALU.bypass, replica_groups=RG,
                                                    ins=[c_xn.ap().opt()], outs=[xn_ag.ap().opt()]),
           waits=[(m_cxn, m_cxn.n)], inc=c_sem)
    sch.op("gpsimd", lambda e: e.collective_compute("AllReduce", ALU.add, replica_groups=RG,
                                                    ins=[c_bar.ap().opt()], outs=[c_bar2.ap().opt()]),
           waits=[(m_cb, 16)], inc=c_sem)
    C_XN = c_sem.n

    stB = ExitStack()
    wbf = [stB.enter_context(nc.sbuf_tensor(f"wbf{i}", [128, 512], BF16)) for i in range(2)]
    tqb = stB.enter_context(nc.sbuf_tensor("tqb", [128, 512], BF16))
    cbq = stB.enter_context(nc.sbuf_tensor("cbq", [128, 512], BF16))
    dbq = stB.enter_context(nc.sbuf_tensor("dbq", [128, 512], BF16))
    qbq = stB.enter_context(nc.sbuf_tensor("qbq", [128, 512], BF16))
    whb = stB.enter_context(nc.sbuf_tensor("whb", [128, 512], BF16))
    web = stB.enter_context(nc.sbuf_tensor("web", [128, 512], BF16))
    wfb = [stB.enter_context(nc.sbuf_tensor(f"wfb{i}", [128, 512], BF16)) for i in range(6)]
    gsum = stB.enter_context(nc.sbuf_tensor("gsum", [128, 8], F32))
    gscl = stB.enter_context(nc.sbuf_tensor("gscl", [128, 8], BF16))
    ginv = stB.enter_context(nc.sbuf_tensor("ginv", [128, 8], F32))
    wTs = [stB.enter_context(nc.sbuf_tensor(f"wTs{i}", [128, 512], BF16)) for i in range(2)]
    psB = ExitStack()
    ps_trb = psB.enter_context(nc.psum_tensor("ps_trb", [128, 1024], BF16))

    qunits = []
    for ti in range(16):
        for sc in range(4):
            qunits.append(("gu", ti, sc, wgu_in[ti * 128:(ti + 1) * 128, sc * 512:(sc + 1) * 512]))
    for ri in range(2):
        for sc in range(16):
            qunits.append(("pr", ri, sc, wpr_in[ri * 128:(ri + 1) * 128, sc * 512:(sc + 1) * 512]))

    WEFF = {}
    TRB = {}
    STG = {}
    qstate = {"ui": 0, "trb": ps_trb[:]}

    def emit_qunit():
        ui = qstate["ui"]
        if ui >= len(qunits):
            return False
        qstate["ui"] = ui + 1
        kind, i0, i1, srcap = qunits[ui]
        trb = qstate["trb"]
        wb = wbf[ui % 2]
        wf = wfb[ui % 6]
        war = [(v_b, WEFF.get(ui - 2, 0))]
        mqs = m_q[ui % 2]
        sch.op("gpsimd", lambda e, wb=wb, srcap=srcap: e.dma_start(out=wb[:], in_=srcap),
               waits=war, inc=mqs, inc_n=16)
        lw = (mqs, mqs.n)
        v3 = wb.ap().rearrange("p (g k) -> p g k", k=64)
        sch.op("vector", lambda e, v3=v3: e.reduce_sum(out=gsum[:], in_=v3, axis=AX, apply_absolute_value=True),
               waits=[lw, (v_b, v_b.n)], inc=v_b)
        sch.op("vector", lambda e: e.tensor_scalar(out=gscl[:], in0=gsum[:], scalar1=1.0 / 64,
                                                   scalar2=1e-8, op0=ALU.mult, op1=ALU.max),
               waits=[(v_b, v_b.n)], inc=v_b)
        sch.op("vector", lambda e: e.reciprocal(out=ginv[:], in_=gscl[:]), waits=[(v_b, v_b.n)], inc=v_b)
        ginv_b = bass.AP(ginv, 0, [[8, 128], [1, 8], [0, 64]])
        gscl_b = bass.AP(gscl, 0, [[8, 128], [1, 8], [0, 64]])
        tq3 = tqb.ap().rearrange("p (g k) -> p g k", k=64)
        sch.op("vector", lambda e, v3=v3, tq3=tq3, ginv_b=ginv_b: e.tensor_tensor(
            out=tq3, in0=v3, in1=ginv_b, op=ALU.mult),
            waits=[(v_b, v_b.n)], inc=v_b)
        sch.op("vector", lambda e: e.tensor_single_scalar(out=cbq[:], in_=tqb[:], scalar=0.5, op=ALU.is_gt),
               waits=[(v_b, v_b.n)], inc=v_b)
        sch.op("vector", lambda e: e.tensor_single_scalar(out=dbq[:], in_=tqb[:], scalar=-0.5, op=ALU.is_lt),
               waits=[(v_b, v_b.n)], inc=v_b)
        sch.op("vector", lambda e: e.tensor_sub(out=qbq[:], in0=cbq[:], in1=dbq[:]),
               waits=[(v_b, v_b.n)], inc=v_b)
        qb3 = qbq.ap().rearrange("p (g k) -> p g k", k=64)
        wh3 = whb.ap().rearrange("p (g k) -> p g k", k=64)
        sch.op("vector", lambda e, qb3=qb3, wh3=wh3, gscl_b=gscl_b: e.tensor_tensor(
            out=wh3, in0=qb3, in1=gscl_b, op=ALU.mult),
            waits=[(v_b, v_b.n)], inc=v_b)
        sch.op("vector", lambda e, wb=wb: e.tensor_sub(out=web[:], in0=whb[:], in1=wb[:]),
               waits=[(v_b, v_b.n)], inc=v_b)
        sch.op("vector", lambda e, wb=wb, wf=wf: e.tensor_add(out=wf[:], in0=wb[:], in1=web[:]),
               waits=[(v_b, v_b.n), (t_b, TRB.get(ui - 6, 0))], inc=v_b)
        WEFF[ui] = v_b.n
        ww = (v_b, v_b.n)

        def finish(ui=ui, ww=ww, wf=wf, kind=kind, i0=i0, i1=i1):
            trb = qstate["trb"]
            st = wTs[ui % 2]
            for dc in range(4):
                sch.op("tensor", lambda e, wf=wf, dc=dc, trb=trb: e.transpose(
                    trb[:, dc * 128:(dc + 1) * 128], wf[:, dc * 128:(dc + 1) * 128], ident[:]),
                    waits=[ww, (m_id, IDENT_W), (v_be, v_be.n), (v_c, v_c.n)], inc=t_b)
            sch.op("vector", lambda e, st=st, trb=trb: e.tensor_copy(out=st[:], in_=trb[:, 0:512]),
                   waits=[(t_b, t_b.n), (m_cw[ui % 2], STG.get(ui - 2, 0))], inc=v_be)
            TRB[ui] = t_b.n
            ew = (v_be, v_be.n)
            for dc in range(4):
                if kind == "gu":
                    dst = c_wgu.ap()[(i1 * 4 + dc) * 128:(i1 * 4 + dc + 1) * 128, i0 * 128:(i0 + 1) * 128]
                else:
                    dst = c_wpr.ap()[(i1 * 4 + dc) * 128:(i1 * 4 + dc + 1) * 128, i0 * 128:(i0 + 1) * 128]
                sch.op("gpsimd", lambda e, st=st, dc=dc, dst=dst: e.dma_start(
                    out=dst, in_=st[:, dc * 128:(dc + 1) * 128]),
                    waits=[ew], inc=m_cw[ui % 2], inc_n=16)
            STG[ui] = m_cw[ui % 2].n
        qfin.append(finish)
        return True

    qfin = []

    def pump(n):
        fins, qfin[:] = qfin[:], []
        for f in fins:
            f()
        for _ in range(n):
            if not emit_qunit():
                return

    for _ in range(4):
        pump(6)
    QUANT_COLLECTIVES = []

    def emit_quant_collectives():
        assert qstate["ui"] == len(qunits), qstate["ui"]
        fins, qfin[:] = qfin[:], []
        for f in fins:
            f()
        sch.op("gpsimd", lambda e: e.collective_compute("AllGather", ALU.bypass, replica_groups=RG,
                                                        ins=[c_wgu.ap().opt()], outs=[wgu_ag.ap().opt()]),
               waits=[(m_cw[0], m_cw[0].n), (m_cw[1], m_cw[1].n)], inc=c_sem)
        QUANT_COLLECTIVES.append(c_sem.n)
        sch.op("gpsimd", lambda e: e.collective_compute("AllGather", ALU.bypass, replica_groups=RG,
                                                        ins=[c_wpr.ap().opt()], outs=[wpr_ag.ap().opt()]),
               waits=[(m_cw[0], m_cw[0].n), (m_cw[1], m_cw[1].n)], inc=c_sem)
        sch.op("gpsimd", lambda e: e.collective_compute("AllReduce", ALU.add, replica_groups=RG,
                                                        ins=[c_bar.ap().opt()], outs=[c_bar2.ap().opt()]),
               inc=c_sem)
        QUANT_COLLECTIVES.append(c_sem.n)
        QUANT_COLLECTIVES[0] = c_sem.n
        if DBG:
            sch.op("sync", lambda e: e.dma_start(out=d_wgu, in_=wgu_ag.ap()),
                   waits=[(c_sem, QUANT_COLLECTIVES[0])], inc=m_tap, inc_n=16)
            sch.op("sync", lambda e: e.dma_start(out=d_wpr, in_=wpr_ag.ap()),
                   waits=[(c_sem, QUANT_COLLECTIVES[1])], inc=m_tap, inc_n=16)
    if DBG:
        sch.op("sync", lambda e: e.dma_start(out=d_xnag, in_=xn_ag.ap()), waits=[(c_sem, C_XN)], inc=m_tap, inc_n=16)
    psB.close()
    psA.close()
    stB.close()
    stA.close()
    stCsc = ExitStack()
    sqs = [stCsc.enter_context(nc.sbuf_tensor(f"sqs{i}", [128, 512], BF16)) for i in range(2)]
    vfs = [stCsc.enter_context(nc.sbuf_tensor(f"vfs{i}", [128, 512], BF16)) for i in range(2)]
    numbf = stCsc.enter_context(nc.sbuf_tensor("numbf", [128, 512], BF16))
    PTb = [stCsc.enter_context(nc.sbuf_tensor(f"PTb{i}", [128, 512], BF16)) for i in range(4)]
    PTs = stCsc.enter_context(nc.sbuf_tensor("PTs", [128, 512], BF16))

    xga_free = {}
    lfree = {}
    numfree = {}

    def phase_C(h, prev_d_done):
        stC = ExitStack()
        ps_q = [stC.enter_context(nc.psum_tensor(f"psq{h}{i}", [128, 512], F32)) for i in range(2)]
        ps_k = [stC.enter_context(nc.psum_tensor(f"psk{h}{i}", [128, 512], F32)) for i in range(2)]
        ps_v = [stC.enter_context(nc.psum_tensor(f"psv{h}{i}", [128, 512], F32)) for i in range(2)]
        ps_n = stC.enter_context(nc.psum_tensor(f"psn{h}", [1, 512], F32))
        ps_tv = stC.enter_context(nc.psum_tensor(f"pstv{h}", [128, 1024], BF16))
        qstate["trb"] = ps_tv[:]

        for (wsb, wsrc) in ((wq_sb, wqT), (wk_sb, wkT), (wv_sb, wvT)):
            for q in range(4):
                srcap = wsrc[q * 512:(q + 1) * 512, h * 128:(h + 1) * 128].rearrange("(a p) n -> p a n", p=128)
                dstap = wsb.ap()[:, q * 512:(q + 1) * 512].rearrange("p (a n) -> p a n", n=128)
                sch.op("gpsimd", lambda e, dstap=dstap, srcap=srcap: e.dma_start(out=dstap, in_=srcap),
                       waits=([] if h == 0 else [(t_c, t_c.n)]), inc=m_w, inc_n=16)
        W_ATTN = m_w.n
        sch.op("vector", lambda e: e.memset(nrm[:, 0:2], -3e38), waits=[(v_c, v_c.n)], inc=v_c)
        ev_prev = (0, 0, 0)
        vtr = {}
        for pr in range(8):
            tA = 2 * pr
            lwq = {}
            for q in range(4):
                for k in range(2):
                    tt = tA + k
                    buf = tt % 3
                    rank, tl = tt // 2, (tt % 2) * 512
                    slot = buf * 4 + q
                    waits = [(t_c, xga_free.get((buf, q), 0))]
                    if h == 0 and pr == 0 and q == 0:
                        waits.append((c_sem, C_XN))
                    mgs = m_xga[slot]
                    srcap = xn_ag.ap()[rank * D + q * 512: rank * D + (q + 1) * 512,
                                       tl:tl + 512].rearrange("(a p) n -> p a n", p=128)
                    dstap = xgab[buf][:, q * 2048:(q + 1) * 2048].rearrange("p (a n) -> p a n", n=512)
                    sch.op("sync", lambda e, dstap=dstap, srcap=srcap: e.dma_start(out=dstap, in_=srcap),
                           waits=waits, inc=mgs, inc_n=16)
                    lwq[(k, q)] = (mgs, mgs.n)
            for dc in range(16):
                for mi, (wsb, ps) in enumerate(((wq_sb, ps_q), (wk_sb, ps_k), (wv_sb, ps_v))):
                    lhsT = wsb[:, dc * 128:(dc + 1) * 128]
                    for k in range(2):
                        buf = (tA + k) % 3
                        waits = [lwq[(k, dc // 4)]]
                        if dc == 0:
                            waits.append((m_w, W_ATTN))
                            waits.append((s_c, ev_prev[mi]))
                            if pr == 0:
                                waits += prev_d_done
                        sch.op("tensor", lambda e, b=ps[k], lhsT=lhsT, buf=buf, dc=dc: e.matmul(
                            b[:], lhsT, xgab[buf][:, dc * 512:(dc + 1) * 512], start=(dc == 0), stop=(dc == 15)),
                            waits=waits, inc=(t_c if (mi == 2 and k == 1) else None))
                for k in range(2):
                    xga_free[((tA + k) % 3, dc // 4)] = t_c.n
            mmw = (t_c, t_c.n)
            for k in range(2):
                tt = tA + k
                col = tt * 512
                sch.op("scalar", lambda e, h=h, b=ps_q[k], col=col: e.activation(
                    out=QTn[h][:, col:col + 512], in_=b[:], func=AF.Identity,
                    bias=bvec[:, 0 + h:1 + h], scale=S4),
                    waits=[mmw, (v_a, BVEC_V)], inc=s_c)
            evq = s_c.n
            for k in range(2):
                tt = tA + k
                col = tt * 512
                sch.op("scalar", lambda e, h=h, b=ps_k[k], col=col: e.activation(
                    out=KT[:, col:col + 512], in_=b[:], func=AF.Identity,
                    bias=bvec[:, 2 + h:3 + h], scale=S4),
                    waits=[mmw], inc=s_c)
            evk = s_c.n
            vfree = []
            for k in range(2):
                tt = tA + k
                sch.op("scalar", lambda e, h=h, b=ps_v[k], k=k: e.activation(
                    out=vfs[k][:], in_=b[:], func=AF.Identity, bias=bvec[:, 4 + h:5 + h], scale=1.0),
                    waits=[mmw, (t_cn, vtr.get(k, 0))], inc=s_c)
                vfree.append(s_c.n)
            evv = s_c.n
            ev_prev = (evq, evk, evv)
            for k in range(2):
                tt = tA + k
                col = tt * 512
                sch.op("scalar", lambda e, h=h, col=col: e.activation(
                    out=sqs[0][:], in_=QTn[h][:, col:col + 512], func=AF.Square),
                    waits=[(t_cn, t_cn.n), (s_c, s_c.n)], inc=s_c)
                sch.op("tensor", lambda e: e.matmul(ps_n[:], onesk[:], sqs[0][:], start=True, stop=True),
                       waits=[(s_c, s_c.n), (v_c, v_c.n), (g_b, ONES_G)], inc=t_cn)
                sch.op("vector", lambda e: e.reduce_max(out=nrm[:, 2:3], in_=ps_n[:], axis=AX),
                       waits=[(t_cn, t_cn.n), (v_c, v_c.n)], inc=v_c)
                sch.op("vector", lambda e: e.tensor_max(out=nrm[:, 0:1], in0=nrm[:, 0:1], in1=nrm[:, 2:3]),
                       waits=[(v_c, v_c.n)], inc=v_c)
                sch.op("scalar", lambda e, col=col: e.activation(
                    out=sqs[1][:], in_=KT[:, col:col + 512], func=AF.Square),
                    waits=[(t_cn, t_cn.n), (s_c, s_c.n)], inc=s_c)
                sch.op("tensor", lambda e: e.matmul(ps_n[:], onesk[:], sqs[1][:], start=True, stop=True),
                       waits=[(s_c, s_c.n), (v_c, v_c.n)], inc=t_cn)
                sch.op("vector", lambda e: e.reduce_max(out=nrm[:, 3:4], in_=ps_n[:], axis=AX),
                       waits=[(t_cn, t_cn.n), (v_c, v_c.n)], inc=v_c)
                sch.op("vector", lambda e: e.tensor_max(out=nrm[:, 1:2], in0=nrm[:, 1:2], in1=nrm[:, 3:4]),
                       waits=[(v_c, v_c.n)], inc=v_c)
                for sub in range(4):
                    sch.op("tensor", lambda e, k=k, sub=sub: e.transpose(
                        ps_tv[:, (k * 4 + sub) * 128:(k * 4 + sub + 1) * 128],
                        vfs[k][:, sub * 128:(sub + 1) * 128], ident[:]),
                        waits=[(s_c, vfree[k]), (v_c, v_c.n), (v_be, v_be.n)], inc=t_cn)
                vtr[k] = t_cn.n
            sch.op("vector", lambda e, pr=pr: e.tensor_copy(out=Vtk[:, pr * 1024:(pr + 1) * 1024], in_=ps_tv[:]),
                   waits=[(t_cn, t_cn.n)], inc=v_c)
            pump(6 if h == 0 else 3)
        sch.op("vector", lambda e: e.tensor_mul(out=nrm[:, 4:5], in0=nrm[:, 0:1], in1=nrm[:, 1:2]),
               waits=[(v_c, v_c.n)], inc=v_c)
        sch.op("scalar", lambda e: e.activation(out=nrm2[:, 0:1], in_=nrm[:, 4:5], func=AF.Sqrt),
               waits=[(v_c, v_c.n), (s_c, s_c.n)], inc=s_c)
        sch.op("vector", lambda e: e.tensor_scalar_mul(out=nrm2[:, 1:2], in0=nrm2[:, 0:1], scalar1=-1.0),
               waits=[(s_c, s_c.n)], inc=v_c)
        sch.op("tensor", lambda e, b=ps_q[0]: e.matmul(b[:, 0:1], onesr_f[:], nrm2[:, 1:2], start=True, stop=True),
               waits=[(v_c, v_c.n), (s_c, s_c.n)], inc=t_cn)
        sch.op("vector", lambda e, h=h: e.tensor_copy(out=negBv[:, h:h + 1], in_=ps_q[0][:, 0:1]),
               waits=[(t_cn, t_cn.n)], inc=v_c)
        stC.close()
        if DBG:
            sch.op("sync", lambda e, h=h: e.dma_start(out=d_qt[h], in_=QTn[h][:]),
                   waits=[(s_c, s_c.n), (v_c, v_c.n)], inc=m_tap, inc_n=16)
            sch.op("sync", lambda e, h=h: e.dma_start(out=d_kt[h], in_=KT[:]),
                   waits=[(s_c, s_c.n)], inc=m_tap, inc_n=16)
            sch.op("sync", lambda e, h=h: e.dma_start(out=d_vt[h], in_=Vtk[:]),
                   waits=[(v_c, v_c.n)], inc=m_tap, inc_n=16)
            sch.op("sync", lambda e, h=h: e.dma_start(out=d_negb[:, h:h + 1], in_=negBv[:, h:h + 1], allow_slow_non_contiguous=True),
                   waits=[(v_c, v_c.n)], inc=m_tap, inc_n=16)
            return (m_tap, m_tap.n)
        return (v_c, v_c.n)

    def phase_D(h, bready):
        stD = ExitStack()
        ps_s = [stD.enter_context(nc.psum_tensor(f"pss{h}{i}", [128, 512], F32)) for i in range(3)]
        ps_nm = [stD.enter_context(nc.psum_tensor(f"psnm{h}{i}", [128, 512], F32)) for i in range(2)]
        ps_l = [stD.enter_context(nc.psum_tensor(f"psl{h}{i}", [1, 512], F32)) for i in range(2)]
        ps_r = stD.enter_context(nc.psum_tensor(f"psr{h}", [128, 512], F32))

        exp_done = {}
        av_done = {}
        add_done = {}
        gi = 0
        chunks = [(qt, kc) for qt in range(16) for kc in range(4 * (qt + 1))]
        ginfo = {}

        def emit_score(gi):
            qt, kc = chunks[gi]
            nkc = 4 * (qt + 1)
            diag = kc >= 4 * qt
            off = 128 * (kc - 4 * qt) if diag else 0
            n = 512 - off
            waits = [(s_d, exp_done.get(gi - 3, 0))]
            if gi == 0:
                waits.append(bready)
            sch.op("tensor", lambda e, qt=qt, kc=kc, off=off, n=n, b=ps_s[gi % 3]: e.matmul(
                b[:, off:off + n], KT[:, kc * 128:(kc + 1) * 128],
                QTn[h][:, qt * 512 + off: qt * 512 + off + n], start=True, stop=True),
                waits=waits, inc=t_ds)
            sw = t_ds.n
            pt = gi % 4
            sch.op("scalar", lambda e, b=ps_s[gi % 3], pt=pt, off=off, n=n: e.activation(
                out=PTb[pt][:, off:off + n], in_=b[:, off:off + n], func=AF.Exp,
                bias=negBv[:, h:h + 1], scale=1.0),
                waits=[(t_ds, sw), (t_da, av_done.get(gi - 4, 0)), (v_dl, add_done.get(gi - 4, 0))], inc=s_d)
            exp_done[gi] = s_d.n
            if diag:
                sch.op("vector", lambda e, pt=pt, off=off: e.tensor_mul(
                    out=PTb[pt][:, off:off + 128], in0=PTb[pt][:, off:off + 128], in1=triu[:]),
                    waits=[(s_d, s_d.n), (m_id, IDENT_W), (v_dl, v_dl.n)], inc=v_dm)
                ptw = (v_dm, v_dm.n)
            else:
                ptw = (s_d, s_d.n)
            ginfo[gi] = (qt, kc, off, n, pt, ptw, sw)

        def emit_consume(gi):
            qt, kc, off, n, pt, ptw, sw = ginfo[gi]
            nkc = 4 * (qt + 1)
            sb = qt % 2
            diag = kc >= 4 * qt
            first = (kc == 0)
            if diag:
                lmw = [ptw]
                if first:
                    lmw.append((v_dn, lfree.get((h, qt - 2), 0)))
                sch.op("tensor", lambda e, pt=pt, off=off, n=n, b=ps_l[sb], first=first, last=(kc == nkc - 1): e.matmul(
                    b[:, off:off + n], onesk[:], PTb[pt][:, off:off + n], start=first, stop=last),
                    waits=lmw)
            else:
                sub = kc % 4
                if sub == 0:
                    sch.op("vector", lambda e, pt=pt: e.tensor_copy(out=PTs[:], in_=PTb[pt][:]),
                           waits=[ptw, (t_dl, t_dl.n), (v_dl, v_dl.n)], inc=v_dl)
                else:
                    sch.op("vector", lambda e, pt=pt: e.tensor_add(out=PTs[:], in0=PTs[:], in1=PTb[pt][:]),
                           waits=[ptw, (v_dl, v_dl.n)], inc=v_dl)
                add_done[gi] = v_dl.n
                if sub == 3:
                    lmw = [(v_dl, v_dl.n)]
                    if kc == 3:
                        lmw.append((v_dn, lfree.get((h, qt - 2), 0)))
                    sch.op("tensor", lambda e, b=ps_l[sb], first=(kc == 3): e.matmul(
                        b[:], onesk[:], PTs[:], start=first, stop=False),
                        waits=lmw, inc=t_dl)
            avw = [ptw]
            if first:
                avw.append((s_dn, numfree.get((h, qt - 2), 0)))
            sch.op("tensor", lambda e, pt=pt, off=off, n=n, kc=kc, b=ps_nm[sb], first=first, last=(kc == nkc - 1): e.matmul(
                b[:, off:off + n], Vtk[:, kc * 128:(kc + 1) * 128], PTb[pt][:, off:off + n],
                start=first, stop=last),
                waits=avw, inc=t_da)
            av_done[gi] = t_da.n
            if kc == nkc - 1:
                emit_epilogue(qt)

        def emit_epilogue(qt):
            sb = qt % 2
            mmw = (t_da, t_da.n)
            qt_last_s = ginfo[gi_last_s[qt]][6]
            sch.op("vector", lambda e, b=ps_l[sb]: e.reciprocal(out=rrow[:], in_=b[:]),
                   waits=[mmw, (t_dl, t_dl.n), (t_dr, t_dr.n), (v_dn, v_dn.n)], inc=v_dn)
            lfree[(h, qt)] = v_dn.n
            sch.op("tensor", lambda e: e.matmul(ps_r[:], onesr_f[:], rrow[:], start=True, stop=True),
                   waits=[(v_dn, v_dn.n)], inc=t_dr)
            sch.op("scalar", lambda e, b=ps_nm[sb]: e.activation(out=numbf[:], in_=b[:], func=AF.Identity),
                   waits=[mmw, (v_dn, v_dn.n)], inc=s_dn)
            numfree[(h, qt)] = s_dn.n
            sch.op("vector", lambda e, qt=qt: e.tensor_tensor(
                out=QTn[h][:, qt * 512:(qt + 1) * 512], in0=numbf[:], in1=ps_r[:], op=ALU.mult),
                waits=[(s_dn, s_dn.n), (t_ds, qt_last_s), (t_dr, t_dr.n)], inc=v_dn)

        gi_last_s = {}
        NG = len(chunks)
        for gi in range(NG):
            emit_score(gi)
            qt, kc = chunks[gi]
            if kc == 4 * (qt + 1) - 1:
                gi_last_s[qt] = gi
            if gi >= 2:
                emit_consume(gi - 2)
        emit_consume(NG - 2)
        emit_consume(NG - 1)
        stD.close()

    prev = []
    for h in range(HPC):
        br = phase_C(h, prev)
        if h == HPC - 1:
            pump(len(qunits))
            emit_quant_collectives()
        phase_D(h, br)
        prev = [(v_dn, v_dn.n), (s_dn, s_dn.n), (t_dr, t_dr.n)]
        if DBG:
            sch.op("sync", lambda e, h=h: e.dma_start(out=d_num[h], in_=QTn[h][:]),
                   waits=[(v_dn, v_dn.n)], inc=m_tap, inc_n=16)
            prev = prev + [(m_tap, m_tap.n)]

    C_WGU, C_WPR = QUANT_COLLECTIVES
    B_CO = (m_cw[0].n, m_cw[1].n)
    stB.close()

    stE = ExitStack()
    wo_sb = stE.enter_context(nc.sbuf_tensor("wo_sb", [128, HPC * D], BF16))
    att_st = [stE.enter_context(nc.sbuf_tensor(f"attst{i}", [128, D], BF16)) for i in range(2)]
    psE = ExitStack()
    ps_o = [psE.enter_context(nc.psum_tensor(f"pso{i}", [128, 512], F32)) for i in range(8)]

    for hc in range(HPC):
        sch.op("gpsimd", lambda e, hc=hc: e.dma_start(out=wo_sb[:, hc * D:(hc + 1) * D],
                                                      in_=woT[hc * 128:(hc + 1) * 128, :]),
               inc=m_wo, inc_n=16)
    W_WO = m_wo.n
    efree = {}
    efree2 = {}
    edma = {}
    for qi in range(S // 128):
        bs = (qi % 2) * 4
        stg = att_st[qi % 2]
        for j in range(4):
            waits = [(v_a, 2), (m_wo, W_WO), (g_b, ONES_G),
                     (v_e, efree.get(qi - 2, 0)), (s_e, efree2.get(qi - 2, 0))]
            if qi == 0 and j == 0:
                waits += prev
            sch.op("tensor", lambda e, j=j, b=ps_o[bs + j]: e.matmul(
                b[:], onesr_b[:], bo8bf[:, j * 512:(j + 1) * 512], start=True, stop=False),
                waits=waits)
        for hh in range(HPC):
            for j in range(4):
                sch.op("tensor", lambda e, hh=hh, qi=qi, j=j, b=ps_o[bs + j]: e.matmul(
                    b[:], QTn[hh][:, qi * 128:(qi + 1) * 128],
                    wo_sb[:, hh * D + j * 512: hh * D + (j + 1) * 512],
                    start=False, stop=(hh == HPC - 1)),
                    inc=(t_e if (hh == HPC - 1 and j == 3) else None))
        mmw = (t_e, t_e.n)
        for j in range(4):
            if j < 2:
                sch.op("vector", lambda e, stg=stg, j=j, b=ps_o[bs + j]: e.tensor_copy(
                    out=stg[:, j * 512:(j + 1) * 512], in_=b[:]),
                    waits=[mmw, (m_cw[0], B_CO[0]), (m_cw[1], B_CO[1]), (m_ca[qi % 2], edma.get(qi - 2, 0))], inc=v_e)
            else:
                sch.op("scalar", lambda e, stg=stg, j=j, b=ps_o[bs + j]: e.activation(
                    out=stg[:, j * 512:(j + 1) * 512], in_=b[:], func=AF.Identity),
                    waits=[mmw, (m_cw[0], B_CO[0]), (m_cw[1], B_CO[1]), (m_ca[qi % 2], edma.get(qi - 2, 0))], inc=s_e)
        efree[qi] = v_e.n
        efree2[qi] = s_e.n
        mca = m_ca[qi % 2]
        sch.op("sync", lambda e, qi=qi, stg=stg: e.dma_start(
            out=c_att.ap()[qi * 128:(qi + 1) * 128, :], in_=stg[:]),
            waits=[(v_e, v_e.n), (s_e, s_e.n)], inc=mca, inc_n=16)
        edma[qi] = mca.n
    sch.op("gpsimd", lambda e: e.collective_compute("ReduceScatter", ALU.add, replica_groups=RG,
                                                    ins=[c_att.ap().opt()], outs=[att_rs.ap().opt()]),
           waits=[(m_ca[0], m_ca[0].n), (m_ca[1], m_ca[1].n)], inc=c_sem)
    sch.op("gpsimd", lambda e: e.collective_compute("AllReduce", ALU.add, replica_groups=RG,
                                                    ins=[c_bar.ap().opt()], outs=[c_bar2.ap().opt()]),
           inc=c_sem)
    C_RS = c_sem.n
    E_V, E_S, E_CO0, E_CO1 = v_e.n, s_e.n, m_ca[0].n, m_ca[1].n
    if DBG:
        sch.op("sync", lambda e: e.dma_start(out=d_catt, in_=c_att.ap()), waits=[(m_ca[0], m_ca[0].n), (m_ca[1], m_ca[1].n)], inc=m_tap, inc_n=16)
        sch.op("sync", lambda e: e.dma_start(out=d_rs, in_=att_rs.ap()), waits=[(c_sem, C_RS)], inc=m_tap, inc_n=16)
    psE.close()
    stE.close()
    stCsc.close()
    stCDE.close()

    stF = ExitStack()
    hT = stF.enter_context(nc.sbuf_tensor("hT", [128, 64 * 512], BF16))
    wgt = [stF.enter_context(nc.sbuf_tensor(f"wgt{i}", [128, 2048], BF16)) for i in range(4)]
    wut = [stF.enter_context(nc.sbuf_tensor(f"wut{i}", [128, 2048], BF16)) for i in range(4)]
    sgs = [stF.enter_context(nc.sbuf_tensor(f"sgs{i}", [128, 512], F32)) for i in range(2)]
    hti = [stF.enter_context(nc.sbuf_tensor(f"hti{i}", [128, 512], BF16)) for i in range(4)]
    hsq = stF.enter_context(nc.sbuf_tensor("hsq", [128, 512], BF16))
    wpt = [stF.enter_context(nc.sbuf_tensor(f"wpt{i}", [128, 2048], BF16)) for i in range(2)]
    fxb = [stF.enter_context(nc.sbuf_tensor(f"fxb{i}", [128, 1024], F32)) for i in range(2)]
    fab = [stF.enter_context(nc.sbuf_tensor(f"fab{i}", [128, 1024], BF16)) for i in range(2)]
    fob = [stF.enter_context(nc.sbuf_tensor(f"fob{i}", [128, 1024], F32)) for i in range(2)]

    sch.wait("vector", [(m_ca[0], E_CO0), (m_ca[1], E_CO1)])
    sch.wait("scalar", [(m_ca[0], E_CO0), (m_ca[1], E_CO1)])
    sch.wait("tensor", [(v_e, E_V), (s_e, E_S)])

    wslot = {}
    vffree = {}
    trev = {}
    rmsred = 0
    ppD = {}
    outD = {}
    for slab in range(2):
        stG = ExitStack()
        ps_g = [stG.enter_context(nc.psum_tensor(f"psg{slab}{i}", [128, 512], F32)) for i in range(4)]
        ps_u = [stG.enter_context(nc.psum_tensor(f"psu{slab}{i}", [128, 512], F32)) for i in range(4)]
        for g in range(16):
            lwq = {}
            for q in range(4):
                qs = q
                rg_, cg = (g * 512) // 2048, (g * 512) % 2048
                ru, cu = ((g + 16) * 512) // 2048, ((g + 16) * 512) % 2048
                waits = [(t_f, wslot.get(qs, 0))]
                if slab == 0 and g == 0 and q == 0:
                    waits.append((c_sem, C_WGU))
                mg1, mg2 = m_wgu[qs], m_wgu[4 + qs]
                srcg = wgu_ag.ap()[rg_ * D + q * 512: rg_ * D + (q + 1) * 512,
                                   cg:cg + 512].rearrange("(a p) n -> p a n", p=128)
                srcu = wgu_ag.ap()[ru * D + q * 512: ru * D + (q + 1) * 512,
                                   cu:cu + 512].rearrange("(a p) n -> p a n", p=128)
                dstg = wgt[qs].ap().rearrange("p (a n) -> p a n", n=512)
                dstu = wut[qs].ap().rearrange("p (a n) -> p a n", n=512)
                sch.op("sync", lambda e, dstg=dstg, srcg=srcg: e.dma_start(out=dstg, in_=srcg),
                       waits=waits, inc=mg1, inc_n=16)
                sch.op("sync", lambda e, dstu=dstu, srcu=srcu: e.dma_start(out=dstu, in_=srcu),
                       waits=[(t_f, wslot.get(qs, 0))], inc=mg2, inc_n=16)
                lwq[q] = [(mg1, mg1.n), (mg2, mg2.n)]
            for dc in range(16):
                qs = dc // 4
                co = (dc % 4) * 512
                for tc in range(4):
                    lhsT = xnT[:, dc * 1024 + (slab * 4 + tc) * 128: dc * 1024 + (slab * 4 + tc) * 128 + 128]
                    waits = list(lwq[dc // 4])
                    if dc == 0:
                        waits.append((v_f, trev.get((slab, g - 1), 0) if g > 0 else (trev.get((slab - 1, 15), 0))))
                        waits.append((v_ae, v_ae.n))
                    sch.op("tensor", lambda e, b=ps_g[tc], lhsT=lhsT, qs=qs, co=co, dc=dc: e.matmul(
                        b[:], lhsT, wgt[qs][:, co:co + 512], start=(dc == 0), stop=(dc == 15)), waits=waits)
                    sch.op("tensor", lambda e, b=ps_u[tc], lhsT=lhsT, qs=qs, co=co, dc=dc: e.matmul(
                        b[:], lhsT, wut[qs][:, co:co + 512], start=(dc == 0), stop=(dc == 15)),
                        inc=(t_f if tc == 3 else None))
                if dc % 4 == 3:
                    wslot[qs] = t_f.n
            mmw = (t_f, t_f.n)
            for tc in range(4):
                sg = sgs[tc % 2]
                ht = hti[tc % 4]
                sch.op("scalar", lambda e, sg=sg, b=ps_g[tc]: e.activation(out=sg[:], in_=b[:], func=AF.Silu),
                       waits=[mmw, (v_f, v_f.n)], inc=s_f)
                sch.op("vector", lambda e, sg=sg, ht=ht, b=ps_u[tc]: e.tensor_tensor(
                    out=ht[:], in0=sg[:], in1=b[:], op=ALU.mult),
                    waits=[(s_f, s_f.n), (t_ft, t_ft.n)], inc=v_f)
                vffree[(slab, g, tc)] = v_f.n
                htw = (v_f, v_f.n)
                sch.op("scalar", lambda e, ht=ht, g=g, tc=tc: e.activation(
                    out=hsq[:], in_=ht[:], func=AF.Square, accum_out=rmsp[:, tc * 16 + g:tc * 16 + g + 1]),
                    waits=[htw, (v_f, rmsred)], inc=s_f)
                for sub in range(4):
                    pst = ps_u[tc].bitcast(BF16)[:, sub * 128:(sub + 1) * 128]
                    sch.op("tensor", lambda e, ht=ht, sub=sub, pst=pst: e.transpose(
                        pst, ht[:, sub * 128:(sub + 1) * 128], ident[:]),
                        waits=[htw, (s_f, s_f.n), (v_f, v_f.n)], inc=t_ft)
                dst = bass.AP(hT, (g * 4) * 512 + tc * 128, [[64 * 512, 128], [512, 4], [1, 128]])
                srcv = ps_u[tc].bitcast(BF16)[:, 0:512].rearrange("p (a n) -> p a n", n=128)
                sch.op("vector", lambda e, dst=dst, srcv=srcv: e.tensor_copy(out=dst, in_=srcv),
                       waits=[(t_ft, t_ft.n)], inc=v_f)
            trev[(slab, g)] = v_f.n
        for tc in range(4):
            sch.op("vector", lambda e, tc=tc: e.reduce_sum(out=lnst[:, 0:1], in_=rmsp[:, tc * 16:(tc + 1) * 16], axis=AX),
                   waits=[(s_f, s_f.n), (v_f, v_f.n)], inc=v_f)
            sch.op("vector", lambda e: e.tensor_scalar(out=lnst[:, 2:3], in0=lnst[:, 0:1],
                                                       scalar1=1.0 / HID, scalar2=RMS_EPS,
                                                       op0=ALU.mult, op1=ALU.add),
                   waits=[(v_f, v_f.n)], inc=v_f)
            sch.op("scalar", lambda e: e.activation(out=lnst[:, 1:2], in_=lnst[:, 2:3], func=AF.Sqrt),
                   waits=[(v_f, v_f.n), (s_f, s_f.n)], inc=s_f)
            sch.op("vector", lambda e, slab=slab, tc=tc: e.reciprocal(
                out=rstd8[:, slab * 4 + tc:slab * 4 + tc + 1], in_=lnst[:, 1:2]),
                waits=[(s_f, s_f.n)], inc=v_f)
        rmsred = v_f.n
        if DBG:
            sch.op("sync", lambda e, slab=slab: e.dma_start(out=d_ht[slab], in_=hT[:]),
                   waits=[(v_f, v_f.n)], inc=m_tap, inc_n=16)
            sch.op("sync", lambda e: e.dma_start(out=d_rstd, in_=rstd8[:]),
                   waits=[(v_f, v_f.n)], inc=m_tap, inc_n=16)
        stG.close()

        stP = ExitStack()
        ps_p = [stP.enter_context(nc.psum_tensor(f"psp{slab}{i}", [128, 512], F32)) for i in range(8)]
        for tp in range(2):
            tcs = (2 * tp, 2 * tp + 1)
            for hc in range(64):
                slot = hc % 2
                waits = [(t_p, ppD.get(slot, 0))]
                if slab == 0 and tp == 0 and hc < 2:
                    waits.append((c_sem, C_WPR))
                mps = m_wp[slot]
                srcap = wpr_ag.ap().rearrange("(r q) j -> q r j", r=NCORES)[hc * 128:(hc + 1) * 128, :, :]
                dstap = wpt[slot].ap().rearrange("p (r j) -> p r j", j=256)
                sch.op("sync", lambda e, dstap=dstap, srcap=srcap: e.dma_start(out=dstap, in_=srcap),
                       waits=waits, inc=mps, inc_n=16)
                lw = (mps, mps.n)
                for k, tc in enumerate(tcs):
                    lhsT = hT[:, hc * 512 + tc * 128: hc * 512 + (tc + 1) * 128]
                    for j in range(4):
                        waits2 = [lw]
                        if hc == 0 and j == 0:
                            waits2.append((v_f, v_f.n))
                            waits2.append((v_p, v_p.n))
                        sch.op("tensor", lambda e, b=ps_p[k * 4 + j], lhsT=lhsT, s=slot, j=j, hc=hc: e.matmul(
                            b[:], lhsT, wpt[s][:, j * 512:(j + 1) * 512], start=(hc == 0), stop=(hc == 63)),
                            waits=waits2, inc=(t_p if (k == 1 and j == 3) else None))
                ppD[slot] = t_p.n
            mmw = (t_p, t_p.n)
            for k, tc in enumerate(tcs):
                stc = slab * 4 + tc
                row = slab * 512 + tc * 128
                for half in range(2):
                    mfa, mfx_ = m_fx[k], m_fx[2 + k]
                    mo = m_out4[k * 2 + half]
                    mo2 = m_out4[k * 2 + (1 - half)]
                    sch.op("sync", lambda e, k=k, row=row, half=half: e.dma_start(
                        out=fab[k][:], in_=att_rs.ap()[row:row + 128, half * 1024:(half + 1) * 1024]),
                        waits=[(c_sem, C_RS), (v_p, v_p.n), (mo, outD.get((k, half), 0)),
                               (mo2, outD.get((k, 1 - half), 0))],
                        inc=mfa, inc_n=16)
                    sch.op("sync", lambda e, k=k, row=row, half=half: e.dma_start(
                        out=fxb[k][:], in_=x_in[row:row + 128, half * 1024:(half + 1) * 1024]),
                        inc=mfx_, inc_n=16)
                    fxw = (mfa, mfa.n)
                    fxw2 = (mfx_, mfx_.n)
                    for jj in range(2):
                        j = half * 2 + jj
                        sch.op("vector", lambda e, k=k, j=j, jj=jj, stc=stc, b=ps_p[k * 4 + j]: e.scalar_tensor_tensor(
                            out=fob[k][:, jj * 512:(jj + 1) * 512], in0=b[:], scalar=rstd8[:, stc:stc + 1],
                            in1=fab[k][:, jj * 512:(jj + 1) * 512], op0=ALU.mult, op1=ALU.add),
                            waits=[mmw, fxw, (v_f, v_f.n)], inc=v_p)
                        sch.op("vector", lambda e, k=k, jj=jj: e.tensor_add(
                            out=fob[k][:, jj * 512:(jj + 1) * 512], in0=fob[k][:, jj * 512:(jj + 1) * 512],
                            in1=fxb[k][:, jj * 512:(jj + 1) * 512]),
                            waits=[(v_p, v_p.n), fxw2], inc=v_p)
                    sch.op("sync", lambda e, k=k, row=row, half=half: e.dma_start(
                        out=out_t[row:row + 128, half * 1024:(half + 1) * 1024], in_=fob[k][:]),
                        waits=[(v_p, v_p.n)], inc=mo, inc_n=16)
                    outD[(k, half)] = mo.n
        stP.close()
    stF.close()

    sch.wait("sync", [(m, m.n) for m in m_out4])
    if DBG:
        sch.wait("sync", [(m_tap, m_tap.n)])
    sch.wait("gpsimd", [(c_sem, c_sem.n)])

    with nc.Block() as blk:
        blk.sync(lambda e: [f(e) for f in sch.streams["sync"]])
        blk.tensor(lambda e: [f(e) for f in sch.streams["tensor"]])
        blk.vector(lambda e: [f(e) for f in sch.streams["vector"]])
        blk.scalar(lambda e: [f(e) for f in sch.streams["scalar"]])
        blk.gpsimd(lambda e: [f(e) for f in sch.streams["gpsimd"]])
    big.close()
    return nc


def kernel(L0_attn_X, L0_attn_Wq, L0_attn_bq, L0_attn_Wk, L0_attn_bk, L0_attn_Wv,
           L0_attn_bv, L0_attn_Wo, L0_attn_bo, L0_ffn_w_gate_up, L0_ffn_w_proj):
    X = np.ascontiguousarray(np.asarray(L0_attn_X, dtype=np.float32).reshape(S, D))
    WqT = np.ascontiguousarray(np.asarray(L0_attn_Wq, np.float32).T)
    WkT = np.ascontiguousarray(np.asarray(L0_attn_Wk, np.float32).T)
    WvT = np.ascontiguousarray(np.asarray(L0_attn_Wv, np.float32).T)
    WoT = np.ascontiguousarray(np.asarray(L0_attn_Wo, np.float32).T)
    bq = np.asarray(L0_attn_bq, np.float32)
    bk = np.asarray(L0_attn_bk, np.float32)
    bv = np.asarray(L0_attn_bv, np.float32)
    bo = np.asarray(L0_attn_bo, np.float32).reshape(1, D)
    wgu = np.asarray(L0_ffn_w_gate_up, np.float32)
    wpr = np.asarray(L0_ffn_w_proj, np.float32)
    ident = np.eye(128, dtype=np.float32)
    triu = np.triu(np.ones((128, 128), dtype=np.float32))

    in_maps = []
    for c in range(NCORES):
        j0 = c * HPC * DH
        b6_c = np.stack([bq[j0:j0 + DH], bq[j0 + DH:j0 + 2 * DH],
                         bk[j0:j0 + DH], bk[j0 + DH:j0 + 2 * DH],
                         bv[j0:j0 + DH], bv[j0 + DH:j0 + 2 * DH]], axis=0)
        in_maps.append({
            "x": np.ascontiguousarray(X[c * T:(c + 1) * T]),
            "wqT": np.ascontiguousarray(WqT[:, j0:j0 + HPC * DH]),
            "wkT": np.ascontiguousarray(WkT[:, j0:j0 + HPC * DH]),
            "wvT": np.ascontiguousarray(WvT[:, j0:j0 + HPC * DH]),
            "woT": np.ascontiguousarray(WoT[j0:j0 + HPC * DH, :]),
            "b6": np.ascontiguousarray(b6_c),
            "bo": bo,
            "wgu": np.ascontiguousarray(wgu[c * (2 * HID // NCORES):(c + 1) * (2 * HID // NCORES)]),
            "wpr": np.ascontiguousarray(wpr[c * (D // NCORES):(c + 1) * (D // NCORES)]),
            "identin": ident,
            "triuin": triu,
        })

    nc = build()
    res = run_bass_kernel_spmd(nc, in_maps, core_ids=list(range(NCORES)))
    out = np.concatenate([res.results[c]["out"] for c in range(NCORES)], axis=0)
    return out.reshape(1, S, D)
